# version 16 (final)
# baseline (speedup 1.0000x reference)
import math
import numpy as np
import ml_dtypes
import concourse.bass as bass
import concourse.mybir as mybir
from concourse.bass_utils import run_bass_kernel_spmd

F32 = mybir.dt.float32
BF16 = mybir.dt.bfloat16
AF = mybir.ActivationFunctionType
OP = mybir.AluOpType

B, S, D = 4, 2048, 2048
H, DN, DR = 16, 128, 64
C = 512
UQ = 3072
FI = 5632
TQ = 1024
KB_LO, KB_HI = 8, 16
SCALE = 1.0 / math.sqrt(DN + DR)
EPS_LN = 1e-5
EPS_RMS = 1e-6

ENGINES = ("sync", "scalar", "vector", "tensor", "gpsimd")
_DTB = {F32: 4, BF16: 2}


class Ctr:
    def __init__(self, sem):
        self.sem = sem
        self.n = 0


class Build:
    def __init__(self, nc):
        self.nc = nc
        self.ops = {e: [] for e in ENGINES}
        self.last_wait = {e: {} for e in ENGINES}
        self.dirty = {e: False for e in ENGINES}
        self._ctrs = {}
        self._names = {}
        self.base = (nc.sbuf_base + 63) & ~63

    def ctr(self, name):
        c = self._ctrs.get(name)
        if c is None:
            c = Ctr(self.nc.alloc_semaphore(name))
            self._ctrs[name] = c
        return c

    def wait(self, e, tok):
        if tok is None:
            return
        sem, v = tok
        key = id(sem)
        if self.last_wait[e].get(key, -1) >= v:
            return
        self.last_wait[e][key] = v
        self.ops[e].append(lambda eng: eng.wait_ge(sem, v))

    def waits(self, e, toks):
        for t in toks:
            self.wait(e, t)

    def do(self, e, fn, mark=None, amt=1):
        if mark is not None:
            mark.n += amt
            tok = (mark.sem, mark.n)

            def wrapped(eng, fn=fn, sem=mark.sem, k=amt):
                fn(eng).then_inc(sem, k)
            self.ops[e].append(wrapped)
            if e in ("vector", "scalar", "gpsimd"):
                self.ops[e].append(lambda eng: eng.drain())
                self.dirty[e] = False
            else:
                self.dirty[e] = True
            return tok
        self.ops[e].append(lambda eng: fn(eng))
        self.dirty[e] = True
        return None

    def drain(self, e):
        if not self.dirty[e]:
            return
        self.ops[e].append(lambda eng: eng.drain())
        self.dirty[e] = False

    def uname(self, name):
        k = self._names.get(name, 0)
        self._names[name] = k + 1
        return name if k == 0 else f"{name}_{k}"

    def barrier(self):
        toks = []
        for name, c in self._ctrs.items():
            if c.n:
                toks.append((c.sem, c.n))
        marks = [
            self.do("vector", lambda e: e.engine_nop(), mark=self.ctr("ve")),
            self.do("scalar", lambda e: e.copy(out=self._bar_t[0:1, 0:1],
                                               in_=self._bar_t[0:1, 0:1]),
                    mark=self.ctr("ac")),
            self.do("gpsimd", lambda e: e.engine_nop(), mark=self.ctr("gp")),
        ]
        alltoks = toks + marks
        for e in ENGINES:
            self.waits(e, alltoks)

    def realize(self, block):
        for e in ENGINES:
            ops = self.ops[e]

            def runner(eng, ops=ops):
                for f in ops:
                    f(eng)
            getattr(block, e)(runner)


class Region:
    def __init__(self, b, start, end):
        self.b = b
        self.start = start
        self.end = end
        self.cur = 0

    def alloc(self, name, shape, dt=F32):
        nbytes = shape[-1] * _DTB[dt]
        off = (self.cur + 63) & ~63
        assert off + nbytes <= self.end - self.start, \
            f"region overflow: {name} needs {nbytes}B at {off}, " \
            f"cap {self.end - self.start}"
        self.cur = off + nbytes
        return self.b.nc.alloc_sbuf_tensor_at(
            self.b.uname(name), shape, dt,
            offset=self.b.base + self.start + off)

    def reset(self):
        self.cur = 0


class DmaChan:
    def __init__(self, b, name, engine="sync"):
        self.b = b
        self.e = engine
        self.c = b.ctr("dma_" + name)

    def xfer(self, out_ap, in_ap, after=()):
        b = self.b
        if self.c.n:
            b.wait(self.e, (self.c.sem, self.c.n))
        for t in after:
            b.wait(self.e, t)
        return b.do(self.e, lambda eng: eng.dma_start(out=out_ap, in_=in_ap),
                    mark=self.c, amt=16)

    load = xfer
    store = xfer


def build_program(stage=9, taps=()):
    nc = bass.Bass(trn_type="TRN2", target_bir_lowering=False)
    b = Build(nc)
    taps = set(taps)
    tap_dr = {}

    def tapt(name, shape):
        t = nc.dram_tensor("tap_" + name, shape, F32, kind="ExternalOutput").ap()
        tap_dr[name] = t
        return t

    def din(name, shape, dt=F32):
        return nc.dram_tensor(name, shape, dt, kind="ExternalInput").ap()

    x_b = din("x_b", [S, D])
    xq = din("xq", [TQ, D])
    w_dq = din("w_dq", [D, C])
    w_uq = din("w_uq", [C, UQ])
    w_dkv = din("w_dkv", [D, C])
    w_ukrope = din("w_ukrope", [D, DR])
    w_uv = din("w_uv", [C, D])
    w_uknope = din("w_uknope", [C, D])
    w_o = din("w_o", [D, D])
    gate_w = din("gate_w", [FI, 2 * D], BF16)
    up_w = din("up_w", [FI, 2 * D], BF16)
    down_w = din("down_w", [D, 2 * FI], BF16)
    g_q_t = din("g_q_t", [128, 4])
    g_kv_t = din("g_kv_t", [128, 4])
    gate_s_t = din("gate_s_t", [128, 44])
    up_s_t = din("up_s_t", [128, 44])
    ds_b = din("ds_b", [128, D])
    qpos_b = din("qpos_b", [128, TQ])
    cos_k = din("cos_k", [S, DR])
    sinS_k = din("sinS_k", [S, DR])
    cos_q = din("cos_q", [TQ, DR])
    sinS_q = din("sinS_q", [TQ, DR])

    out_q = nc.dram_tensor("out_q", [TQ, D], F32, kind="ExternalOutput").ap()
    x1_scr = nc.dram_tensor("x1_scr", [TQ, D], F32, kind="Internal").ap()
    acc_scr = nc.dram_tensor("acc_scr", [TQ, D], F32, kind="Internal").ap()
    gwT_scr = nc.dram_tensor("gwT_scr", [44 * 128, D], BF16, kind="Internal").ap()
    uwT_scr = nc.dram_tensor("uwT_scr", [44 * 128, D], BF16, kind="Internal").ap()
    dwT_scr = nc.dram_tensor("dwT_scr", [2 * 8 * 22 * 128, 256], BF16,
                             kind="Internal").ap()

    KBY = 1024
    ST = Region(b, 0, 9 * KBY)
    RA = Region(b, 9 * KBY, 61 * KBY)
    RB = Region(b, 61 * KBY, 93 * KBY)
    SEND = (229344 - 64) - b.base
    SCR = Region(b, 93 * KBY, SEND)
    BIG = Region(b, 9 * KBY, SEND)

    iota_p = ST.alloc("iota_p", [128, 16])
    iota_row = ST.alloc("iota_row", [128, 128])
    eps_ln_t = ST.alloc("eps_ln_t", [128, 16])
    eps_rms_t = ST.alloc("eps_rms_t", [128, 16])
    ident_f = ST.alloc("ident_f", [128, 128])
    ident_b = ST.alloc("ident_b", [128, 128], BF16)
    ones_b = ST.alloc("ones_b", [128, 128], BF16)
    gq_t = ST.alloc("gq_t", [128, 16])
    gkv_t = ST.alloc("gkv_t", [128, 16])
    thr = ST.alloc("thr", [128, 16])
    stat6 = ST.alloc("stat6", [128, 32])
    mv2 = ST.alloc("mv2", [128, 16])
    negm = ST.alloc("negm", [128, 16])
    rsq1 = ST.alloc("rsq1", [128, 16])
    rt1 = ST.alloc("rt1", [128, 512])
    sqf = ST.alloc("sqf", [128, 512])
    sqb = ST.alloc("sqb", [128, 512], BF16)
    sqb2 = ST.alloc("sqb2", [128, 512], BF16)
    rop_a = ST.alloc("rop_a", [128, 64])
    rop_c = ST.alloc("rop_c", [128, 64])
    b._bar_t = rsq1

    ckv_hi = RA.alloc("ckv_hi", [128, 4 * S], BF16)
    ckv_lo = RA.alloc("ckv_lo", [128, 4 * S], BF16)
    cq_hi = RA.alloc("cq_hi", [128, 4 * TQ], BF16)
    cq_lo = RA.alloc("cq_lo", [128, 4 * TQ], BF16)
    krp = RA.alloc("krp", [128, S], BF16)

    PSB = [nc.alloc_psum_tensor(f"psb{i}", [128, 512], F32) for i in range(6)]
    PS_TR2 = nc.alloc_psum_tensor("ps_tr2", [128, 512], F32)
    nc.bump_psum(banks=-1)
    PSBF = nc.alloc_psum_tensor("psbf", [128, 1024], BF16)
    PS_TR = nc.alloc_psum_tensor("ps_tr", [128, 512], F32)

    chx = DmaChan(b, "x", "sync")
    chw = DmaChan(b, "w", "sync")
    chm = DmaChan(b, "meta", "sync")
    cho = DmaChan(b, "out", "sync")
    chs = DmaChan(b, "scr", "scalar")
    chc = DmaChan(b, "cast", "gpsimd")
    cht = DmaChan(b, "tap", "gpsimd")

    ve = b.ctr("ve")
    ac = b.ctr("ac")
    te = b.ctr("te")
    gp = b.ctr("gp")

    def pap(handle, offset, dims):
        return bass.AP(handle, offset, dims)

    b.do("gpsimd", lambda e: e.iota(iota_p[:, 0:1], pattern=[[0, 1]], base=0,
                                    channel_multiplier=1,
                                    allow_small_or_imprecise_dtypes=True))
    b.do("gpsimd", lambda e: e.iota(iota_row[:], pattern=[[1, 128]], base=0,
                                    channel_multiplier=0,
                                    allow_small_or_imprecise_dtypes=True))
    b.do("gpsimd", lambda e: e.memset(ones_b[:], 1.0))
    b.do("gpsimd", lambda e: e.memset(eps_ln_t[:, 0:1], EPS_LN))
    b.do("gpsimd", lambda e: e.memset(eps_rms_t[:, 0:1], EPS_RMS))
    b.drain("gpsimd")
    t_const = b.do("gpsimd", lambda e: e.engine_nop(), mark=gp)

    b.wait("vector", t_const)
    b.do("vector", lambda e: e.tensor_scalar(out=ident_f[:], in0=iota_row[:],
                                             scalar1=iota_p[:, 0:1],
                                             scalar2=None, op0=OP.is_equal))
    b.drain("vector")
    b.do("vector", lambda e: e.tensor_copy(out=ident_b[:], in_=ident_f[:]))
    for kb in range(16):
        b.do("vector", lambda e, kb=kb: e.tensor_scalar(
            out=thr[:, kb:kb + 1], in0=iota_p[:, 0:1],
            scalar1=float(128 * kb), scalar2=None, op0=OP.add))
    b.drain("vector")
    t_ident = b.do("vector", lambda e: e.engine_nop(), mark=ve)

    chm.load(gq_t[:, 0:4], g_q_t[:, :])
    chm.load(gkv_t[:, 0:4], g_kv_t[:, :])

    def ln_stats_apply(src_ap, dst_ap, width, res_ap=None):
        nparts = width // 512
        for i in range(nparts):
            b.do("vector", lambda e, i=i: e.bn_stats(
                out=stat6[:, i * 6:(i + 1) * 6],
                in_=src_ap[:, i * 512:(i + 1) * 512]))
        b.drain("vector")
        b.do("vector", lambda e: e.bn_aggr(out=mv2[:, 0:2],
                                           in_=stat6[:, 0:6 * nparts]))
        b.drain("vector")
        b.do("vector", lambda e: e.tensor_scalar(
            out=negm[:, 0:1], in0=mv2[:, 0:1], scalar1=-1.0, scalar2=None,
            op0=OP.mult))
        t_v = b.do("vector", lambda e: e.engine_nop(), mark=ve)
        b.wait("scalar", t_v)
        b.do("scalar", lambda e: e.activation(out=rsq1[:, 0:1],
                                              in_=mv2[:, 1:2], func=AF.Ln,
                                              bias=eps_ln_t[:, 0:1],
                                              scale=1.0))
        b.drain("scalar")
        t_ac = b.do("scalar", lambda e: e.activation(
            out=rsq1[:, 0:1], in_=rsq1[:, 0:1], func=AF.Exp, scale=-0.5),
            mark=ac)
        b.wait("vector", t_ac)
        b.do("vector", lambda e: e.tensor_tensor(
            out=negm[:, 1:2], in0=negm[:, 0:1], in1=rsq1[:, 0:1],
            op=OP.mult))
        b.drain("vector")
        t_nm = b.do("vector", lambda e: e.engine_nop(), mark=ve)
        b.wait("scalar", t_nm)
        b.do("scalar", lambda e: e.activation(
            out=dst_ap, in_=src_ap, func=AF.Identity,
            scale=rsq1[:, 0:1], bias=negm[:, 1:2]))
        b.drain("scalar")
        t_ap = b.do("scalar", lambda e: e.copy(out=rsq1[0:1, 0:1],
                                               in_=rsq1[0:1, 0:1]), mark=ac)
        if res_ap is not None:
            b.wait("vector", t_ap)
            b.do("vector", lambda e: e.tensor_tensor(
                out=dst_ap, in0=dst_ap, in1=res_ap, op=OP.add))
            b.drain("vector")
            return b.do("vector", lambda e: e.engine_nop(), mark=ve)
        return t_ap

    def mm_group(bank_ap, mms, waits=(), cons=False, pre=None):
        b.waits("tensor", waits)
        n = len(mms)
        tok = None
        for i, (l, r) in enumerate(mms):
            if pre and i in pre:
                b.waits("tensor", tuple(t for t in pre[i] if t is not None))
            s, t = (i == 0), (i == n - 1)
            if t:
                if cons:
                    b.drain("tensor")
                tok = b.do("tensor", lambda e, l=l, r=r, s=s:
                           e.matmul(bank_ap, l, r, start=s, stop=True),
                           mark=te)
            else:
                b.do("tensor", lambda e, l=l, r=r, s=s:
                     e.matmul(bank_ap, l, r, start=s, stop=False))
        return tok

    def split3(hi_ap, lo_ap, src_ap, waits=()):
        b.waits("vector", waits)
        b.do("vector", lambda e: e.tensor_copy(out=hi_ap, in_=src_ap))
        b.drain("vector")
        return b.do("vector", lambda e: e.tensor_tensor(
            out=lo_ap, in0=src_ap, in1=hi_ap, op=OP.subtract), mark=ve)

    def evac_split(bank_ap, hi_ap, lo_ap, t_mm):
        b.wait("scalar", t_mm)
        b.do("scalar", lambda e: e.activation(out=hi_ap, in_=bank_ap,
                                              func=AF.Identity, scale=1.0))
        b.drain("scalar")
        t_hi = b.do("scalar", lambda e: e.copy(out=rsq1[0:1, 0:1],
                                               in_=rsq1[0:1, 0:1]), mark=ac)
        b.wait("vector", t_mm)
        b.wait("vector", t_hi)
        return b.do("vector", lambda e: e.tensor_tensor(
            out=lo_ap, in0=bank_ap, in1=hi_ap, op=OP.subtract), mark=ve)

    def tr_split_tile(src_f32, dst_hi, dst_lo, col_off, col_stride, waits=(),
                      nbanks=2):
        t_ev2 = [None, None]
        t_ev = t_te = None
        for rnd in range(4):
            par = rnd % nbanks
            bank = PS_TR if par == 0 else PS_TR2
            b.waits("tensor", waits)
            if t_ev2[par] is not None:
                b.wait("tensor", t_ev2[par])
            for j in range(3):
                dct = rnd * 4 + j
                b.do("tensor", lambda e, j=j, dct=dct, bank=bank: e.transpose(
                    bank[:, j * 128:(j + 1) * 128],
                    src_f32[:, dct * 128:(dct + 1) * 128], ident_f[:]))
            b.drain("tensor")
            t_te = b.do("tensor", lambda e, dct=rnd * 4 + 3, bank=bank:
                        e.transpose(
                bank[:, 3 * 128:4 * 128],
                src_f32[:, dct * 128:(dct + 1) * 128], ident_f[:]), mark=te)
            prow = list(dst_hi[:].ap[0])
            off = (rnd * 4) * col_stride + col_off
            hi_view = pap(dst_hi, off, [prow, [col_stride, 4], [1, 128]])
            lo_view = pap(dst_lo, off, [prow, [col_stride, 4], [1, 128]])
            ps_view = bank[:].rearrange("p (a f) -> p a f", a=4)
            b.wait("scalar", t_te)
            b.do("scalar", lambda e, hv=hi_view, pv=ps_view: e.activation(
                out=hv, in_=pv, func=AF.Identity, scale=1.0))
            b.drain("scalar")
            t_hi = b.do("scalar", lambda e: e.copy(out=rsq1[0:1, 0:1],
                                                   in_=rsq1[0:1, 0:1]),
                        mark=ac)
            b.wait("vector", t_te)
            b.wait("vector", t_hi)
            t_ev = b.do("vector", lambda e, lv=lo_view, pv=ps_view,
                        hv=hi_view: e.tensor_tensor(
                out=lv, in0=pv, in1=hv, op=OP.subtract), mark=ve)
            t_ev2[par] = t_ev
        return t_ev, t_te

    lnf_hi = SCR.alloc("lnf_hi", [128, 16 * 512], BF16)
    lnf_lo = SCR.alloc("lnf_lo", [128, 16 * 512], BF16)
    ln_tok = SCR.alloc("ln_tok", [128, D])
    wstage = SCR.alloc("wstage", [128, 2048])
    wdkv_hi = SCR.alloc("wdkv_hi", [128, 16 * 512], BF16)
    ln_tok_b = RB.alloc("ln_tok_b", [128, D])
    wdkv_lo = SCR.alloc("wdkv_lo", [128, 16 * 512], BF16)
    ckv_f = SCR.alloc("ckv_f", [128, 4 * S])
    krpre = RB.alloc("krpre", [64, S])
    cosk_t = RB.alloc("cosk_t", [128, 16 * DR])
    sink_t = RB.alloc("sink_t", [128, 16 * DR])
    wkr_hi = RB.alloc("wkr_hi", [128, 16 * DR], BF16)
    wkr_lo = RB.alloc("wkr_lo", [128, 16 * DR], BF16)

    t_cosk = None
    for i in range(16):
        chm.load(cosk_t[:, i * DR:(i + 1) * DR], cos_k[i * 128:(i + 1) * 128, :])
        t_cosk = chm.load(sink_t[:, i * DR:(i + 1) * DR],
                          sinS_k[i * 128:(i + 1) * 128, :])

    last_split = t_ident
    for wsrc, whi, wlo, cols in ((w_dkv, wdkv_hi, wdkv_lo, C),
                                 (w_ukrope, wkr_hi, wkr_lo, DR)):
        spl_hist = [last_split, last_split]
        for ct in range(16):
            stg = wstage[:, 0:cols] if ct % 2 == 0 else wstage[:, cols:2 * cols]
            tl = chw.load(stg, wsrc[ct * 128:(ct + 1) * 128, :],
                          after=(spl_hist[ct % 2],))
            last_split = split3(whi[:, ct * cols:(ct + 1) * cols],
                                wlo[:, ct * cols:(ct + 1) * cols], stg, (tl,))
            spl_hist[ct % 2] = last_split

    def proj_chunks(src, nchunks, whi, wlo, dstf, toklen, do_rope, t_w):
        t_prev_cons = None
        t_evs = None
        t_tr_hist = [None, None]
        for ch in range(nchunks):
            t_lnf = None
            for t4 in range(4):
                ti = ch * 4 + t4
                buf = ln_tok if ti % 2 == 0 else ln_tok_b
                tl = chx.load(buf[:], src[ch * 512 + t4 * 128:
                                          ch * 512 + (t4 + 1) * 128, :],
                              after=(t_tr_hist[ti % 2],)
                              if t_tr_hist[ti % 2] else ())
                b.wait("vector", tl)
                t_ln = ln_stats_apply(buf[:], buf[:], D)
                waits = [t_w, t_ln]
                if t4 == 0 and t_prev_cons is not None:
                    waits.append(t_prev_cons)
                t_lnf, t_tr_hist[ti % 2] = tr_split_tile(
                    buf[:], lnf_hi, lnf_lo, t4 * 128, 512, waits=waits)
            b.wait("tensor", t_lnf)
            for m in range(4):
                bank = PSB[m]
                mms = []
                for ct in range(16):
                    w_h = whi[:, ct * C + m * 128: ct * C + (m + 1) * 128]
                    w_l = wlo[:, ct * C + m * 128: ct * C + (m + 1) * 128]
                    a_h = lnf_hi[:, ct * 512:(ct + 1) * 512]
                    a_l = lnf_lo[:, ct * 512:(ct + 1) * 512]
                    mms += [(w_h, a_h), (w_h, a_l), (w_l, a_h)]
                t_mm = mm_group(bank[:], mms, cons=(m == 3 and not do_rope))
                b.wait("vector", t_mm)
                b.do("vector", lambda e, bank=bank, m=m, ch=ch: e.tensor_copy(
                    out=dstf[:, m * toklen + ch * 512:
                             m * toklen + (ch + 1) * 512], in_=bank[:]))
                b.drain("vector")
            if do_rope:
                mms = []
                for ct in range(16):
                    mms += [(wkr_hi[:, ct * DR:(ct + 1) * DR],
                             lnf_hi[:, ct * 512:(ct + 1) * 512]),
                            (wkr_hi[:, ct * DR:(ct + 1) * DR],
                             lnf_lo[:, ct * 512:(ct + 1) * 512]),
                            (wkr_lo[:, ct * DR:(ct + 1) * DR],
                             lnf_hi[:, ct * 512:(ct + 1) * 512])]
                t_mm = mm_group(PSB[4][0:64, :], mms, cons=True)
                b.wait("vector", t_mm)
                b.do("vector", lambda e, ch=ch: e.tensor_copy(
                    out=krpre[:, ch * 512:(ch + 1) * 512],
                    in_=PSB[4][0:64, :]))
                b.drain("vector")
            t_evs = b.do("vector", lambda e: e.engine_nop(), mark=ve)
            t_prev_cons = t_mm
            b.wait("tensor", t_evs)
        return t_evs

    t_p1 = proj_chunks(x_b, 4, wdkv_hi, wdkv_lo, ckv_f, S, True, last_split)

    def rms_split(cf, chi_, clo_, g_t, toklen, t_start):
        prev_rt = None
        last = t_start
        for ch in range(toklen // 512):
            t_mm = None
            for m in range(4):
                sl = cf[:, m * toklen + ch * 512: m * toklen + (ch + 1) * 512]
                b.wait("vector", last)
                if t_mm is not None:
                    b.wait("vector", t_mm)
                b.do("vector", lambda e, sl=sl: e.tensor_tensor(
                    out=sqf[:], in0=sl, in1=sl, op=OP.mult))
                b.drain("vector")
                b.do("vector", lambda e: e.tensor_copy(out=sqb[:], in_=sqf[:]))
                b.drain("vector")
                t_sq = b.do("vector", lambda e: e.tensor_tensor(
                    out=sqb2[:], in0=sqf[:], in1=sqb[:], op=OP.subtract),
                    mark=ve)
                wts = [t_sq]
                if prev_rt is not None and m == 0:
                    wts.append(prev_rt)
                b.waits("tensor", wts)
                b.do("tensor", lambda e, s=(m == 0): e.matmul(
                    PSB[5][:], ones_b[:], sqb[:], start=s, stop=False))
                b.drain("tensor")
                t_mm = b.do("tensor", lambda e, t=(m == 3): e.matmul(
                    PSB[5][:], ones_b[:], sqb2[:], start=False, stop=t),
                    mark=te)
            b.wait("scalar", t_mm)
            b.do("scalar", lambda e: e.activation(out=rt1[:], in_=PSB[5][:],
                                                  func=AF.Ln,
                                                  bias=eps_rms_t[:, 0:1],
                                                  scale=1.0 / C))
            b.drain("scalar")
            prev_rt = b.do("scalar", lambda e: e.activation(
                out=rt1[:], in_=rt1[:], func=AF.Exp, scale=-0.5), mark=ac)
            b.wait("vector", prev_rt)
            for m in range(4):
                sl = cf[:, m * toklen + ch * 512: m * toklen + (ch + 1) * 512]
                off = m * toklen + ch * 512
                b.do("vector", lambda e, sl=sl, m=m, g_t=g_t:
                     e.scalar_tensor_tensor(
                    out=sqf[:], in0=sl, scalar=g_t[:, m:m + 1], in1=rt1[:],
                    op0=OP.mult, op1=OP.mult))
                b.drain("vector")
                b.do("vector", lambda e, off=off, chi_=chi_: e.tensor_copy(
                    out=chi_[:, off:off + 512], in_=sqf[:]))
                b.drain("vector")
                last = b.do("vector", lambda e, off=off, chi_=chi_,
                            clo_=clo_: e.tensor_tensor(
                    out=clo_[:, off:off + 512], in0=sqf[:],
                    in1=chi_[:, off:off + 512], op=OP.subtract), mark=ve)
        return last

    t_ckv = rms_split(ckv_f, ckv_hi, ckv_lo, gkv_t, S, t_p1)
    if "ckv" in taps:
        tt = tapt("ckv_hi", [128, 4 * S])
        cht.store(tt[:, :], ckv_hi[:], after=(t_ckv,))
        tt2 = tapt("ckv_lo", [128, 4 * S])
        cht.store(tt2[:, :], ckv_lo[:], after=(t_ckv,))

    b.barrier()
    SCR.reset()
    lnf_hi = SCR.alloc("lnf_hi", [128, 16 * 512], BF16)
    lnf_lo = SCR.alloc("lnf_lo", [128, 16 * 512], BF16)
    ln_tok = SCR.alloc("ln_tok", [128, D])
    wstage = SCR.alloc("wstage", [128, 2048])
    wdq_hi = SCR.alloc("wdq_hi", [128, 16 * 512], BF16)
    wdq_lo = SCR.alloc("wdq_lo", [128, 16 * 512], BF16)
    cq_f = SCR.alloc("cq_f", [128, 4 * TQ])

    last_split = None
    spl_hist = [None, None]
    for ct in range(16):
        stg = wstage[:, 0:C] if ct % 2 == 0 else wstage[:, C:2 * C]
        tl = chw.load(stg, w_dq[ct * 128:(ct + 1) * 128, :],
                      after=(spl_hist[ct % 2],) if spl_hist[ct % 2] else ())
        last_split = split3(wdq_hi[:, ct * C:(ct + 1) * C],
                            wdq_lo[:, ct * C:(ct + 1) * C], stg, (tl,))
        spl_hist[ct % 2] = last_split

    t_p1q = proj_chunks(xq, 2, wdq_hi, wdq_lo, cq_f, TQ, False, last_split)
    t_cq = rms_split(cq_f, cq_hi, cq_lo, gq_t, TQ, t_p1q)
    if "cq" in taps:
        tt = tapt("cq_hi", [128, 4 * TQ])
        cht.store(tt[:, :], cq_hi[:], after=(t_cq,))
        tt2 = tapt("cq_lo", [128, 4 * TQ])
        cht.store(tt2[:, :], cq_lo[:], after=(t_cq,))

    def rope_tiles(pre_ap, n_tiles, cos_t, sin_t, dst_fn, waits0=()):
        last2 = [None, None]
        last = None
        for i in range(n_tiles):
            par = i % 2
            bank = PS_TR if par == 0 else PS_TR2
            ra = rop_a if par == 0 else rop_c
            rbap = bank[:, 64:128]
            wts = list(waits0)
            if last2[par] is not None:
                wts.append(last2[par])
            b.waits("tensor", wts)
            t_t1 = b.do("tensor", lambda e, i=i, pre_ap=pre_ap, bank=bank:
                        e.transpose(
                bank[:, 0:64], pre_ap[:, i * 128:(i + 1) * 128],
                ident_f[0:64, 0:64]), mark=te)
            b.wait("vector", t_t1)
            prow = list(bank[:].ap[0])
            rot = pap(bank, 1, [prow, [2, 32], [-1, 2]])
            b.do("vector", lambda e, i=i, cos_t=cos_t, ra=ra, bank=bank:
                 e.tensor_tensor(
                out=ra[:], in0=bank[:, 0:64],
                in1=cos_t[:, i * DR:(i + 1) * DR], op=OP.mult))
            b.do("vector", lambda e, i=i, rot=rot, sin_t=sin_t, rbap=rbap:
                 e.tensor_tensor(
                out=rbap.rearrange("p (a c) -> p a c", c=2), in0=rot,
                in1=sin_t[:, i * DR:(i + 1) * DR].rearrange(
                    "p (a c) -> p a c", c=2), op=OP.mult))
            b.drain("vector")
            t_rp = b.do("vector", lambda e, ra=ra, rbap=rbap: e.tensor_tensor(
                out=ra[:], in0=ra[:], in1=rbap, op=OP.add),
                mark=ve)
            b.wait("tensor", t_rp)
            t_t2 = b.do("tensor", lambda e, ra=ra, bank=bank: e.transpose(
                bank[0:64, 128:256], ra[:], ident_f[:]), mark=te)
            last = dst_fn(i, t_t2, bank)
            last2[par] = last
        return last

    def krp_dst(i, t_tr, bank):
        b.wait("vector", t_tr)
        b.do("vector", lambda e, i=i, bank=bank: e.tensor_copy(
            out=krp[0:64, i * 128:(i + 1) * 128], in_=bank[0:64, 128:256]))
        b.drain("vector")
        return b.do("vector", lambda e, i=i, bank=bank: e.tensor_tensor(
            out=krp[64:128, i * 128:(i + 1) * 128], in0=bank[0:64, 128:256],
            in1=krp[0:64, i * 128:(i + 1) * 128], op=OP.subtract), mark=ve)

    t_krp = rope_tiles(krpre, 16, cosk_t, sink_t, krp_dst,
                       waits0=(t_cq, t_cosk))
    if "krp" in taps:
        tt = tapt("krp", [128, S])
        cht.store(tt[:, :], krp[:], after=(t_krp,))

    if stage <= 2:
        if cht.c.n:
            b.wait("gpsimd", (cht.c.sem, cht.c.n))
        with nc.Block() as block:
            b.realize(block)
        return nc, tap_dr

    b.barrier()
    RB3 = Region(b, 61 * KBY, 125 * KBY)
    SCR = Region(b, 125 * KBY, SEND)
    oa_hi = RB3.alloc("oa_hi", [128, H * TQ], BF16)
    oa_lo = RB3.alloc("oa_lo", [128, H * TQ], BF16)
    qposb_t = SCR.alloc("qposb_t", [128, TQ])
    masks16 = SCR.alloc("masks16", [128, 16 * 512], BF16)
    cosq_t = SCR.alloc("cosq_t", [128, 8 * DR])
    sinq_t = SCR.alloc("sinq_t", [128, 8 * DR])
    v2_hi = SCR.alloc("v2_hi", [128, 16 * 256], BF16)
    v2_lo = SCR.alloc("v2_lo", [128, 16 * 256], BF16)
    kh_hi = SCR.alloc("kh_hi", [128, S], BF16)
    kh_lo = SCR.alloc("kh_lo", [128, S], BF16)
    qn_hi = SCR.alloc("qn_hi", [128, TQ], BF16)
    qn_lo = SCR.alloc("qn_lo", [128, TQ], BF16)
    qr_pack = SCR.alloc("qr_pack", [128, TQ], BF16)
    qr_lo = SCR.alloc("qr_lo", [64, TQ], BF16)
    p_f = [SCR.alloc(f"p_f{i}", [128, 512]) for i in range(2)]
    p_mm = SCR.alloc("p_mm", [128, 512])
    p_hi = [SCR.alloc(f"p_hi{i}", [128, 512], BF16) for i in range(2)]
    p_lo = [SCR.alloc(f"p_lo{i}", [128, 512], BF16) for i in range(2)]
    wsg_off = b.base + SCR.start + ((SCR.cur + 63) & ~63)
    wsg = SCR.alloc("wsg", [128, 1024])
    wh_a = SCR.alloc("wh_a", [128, 1024], BF16)
    wl_a = SCR.alloc("wl_a", [128, 1024], BF16)
    wsq = wsg
    wh_b = SCR.alloc("wh_b", [128, 512], BF16)
    wl_b = SCR.alloc("wl_b", [128, 512], BF16)
    wh_c = SCR.alloc("wh_c", [128, 768], BF16)
    wl_c = SCR.alloc("wl_c", [128, 768], BF16)
    wsqq = SCR.alloc("wsqq", [128, 768])
    qr_pre = nc.alloc_sbuf_tensor_at(b.uname("qr_pre"), [64, TQ], F32,
                                     offset=wsg_off)
    wsk = nc.alloc_sbuf_tensor_at(b.uname("wsk"), [128, 512], F32,
                                  offset=b.base + SCR.start)

    t_qpos = chm.load(qposb_t[:], qpos_b[:, :])
    t_cosq = None
    for i in range(8):
        chm.load(cosq_t[:, i * DR:(i + 1) * DR], cos_q[i * 128:(i + 1) * 128, :])
        t_cosq = chm.load(sinq_t[:, i * DR:(i + 1) * DR],
                          sinS_q[i * 128:(i + 1) * 128, :])
    b.wait("vector", t_qpos)
    for kb in range(16):
        qoff = 0 if kb < 8 else 512
        b.do("vector", lambda e, kb=kb, qoff=qoff: e.tensor_scalar(
            out=masks16[:, kb * 512:(kb + 1) * 512],
            in0=qposb_t[:, qoff:qoff + 512], scalar1=thr[:, kb:kb + 1],
            scalar2=None, op0=OP.is_ge))
    b.drain("vector")
    t_masks = b.do("vector", lambda e: e.engine_nop(), mark=ve)

    t_exp = [None, None]
    t_pcons = [None, None]
    t_oa_last = None
    t_oa_bank = [None, None]
    t_v_read = None
    t_hcons = None
    t_vgen_cons = None
    t_qrmm_prev = None
    t_kh3 = None
    t_rt_read = [None]

    chains = {"v": [None], "k": [None], "q": [None]}
    wtok = {}
    pend_splits = []

    def enq_V(g, dest_free, at=None):
        aft = tuple(t for t in (chains["v"][0],) + dest_free
                    if t is not None)
        tlw = chw.load(wsg[:],
                       pap(w_uv.tensor, g * 256,
                           [[D, 128], [128 * D, 4], [1, 256]]),
                       after=aft)

        def spl(tlw=tlw):
            tok = split3(wh_a[:], wl_a[:], wsg[:], (tlw,))
            chains["v"][0] = tok
            wtok["v"] = tok
        if at is None:
            spl()
        else:
            pend_splits.append((at[0], at[1], spl))

    def enq_K(g, h2, dest_free, at=None):
        aft = tuple(t for t in (chains["k"][0],) + dest_free
                    if t is not None)
        tlw = chw.load(wsk[:],
                       pap(w_uknope.tensor, (g * 2 + h2) * 128,
                           [[D, 128], [128 * D, 4], [1, 128]]),
                       after=aft)

        def spl(tlw=tlw):
            tok = split3(wh_b[:], wl_b[:], wsk[:], (tlw,))
            chains["k"][0] = tok
            wtok["k"] = tok
        if at is None:
            spl()
        else:
            pend_splits.append((at[0], at[1], spl))

    def enq_Q(g, h2, dest_free, at=None):
        aft = tuple(t for t in (chains["q"][0],) + dest_free
                    if t is not None)
        tlw = chw.load(wsqq[:],
                       pap(w_uq.tensor, (g * 2 + h2) * 192,
                           [[UQ, 128], [128 * UQ, 4], [1, 192]]),
                       after=aft)

        def spl(tlw=tlw):
            tok = split3(wh_c[:], wl_c[:], wsqq[:], (tlw,))
            chains["q"][0] = tok
            wtok["q"] = tok
        if at is None:
            spl()
        else:
            pend_splits.append((at[0], at[1], spl))

    RAW4 = Region(b, 9 * KBY, 61 * KBY)
    wstage4 = RAW4.alloc("wstage4", [128, 2048])
    wstage4b = RAW4.alloc("wstage4b", [128, 2048])
    wo_hi2 = RAW4.alloc("wo_hi2", [128, 4 * D], BF16)
    wo_lo2 = RAW4.alloc("wo_lo2", [128, 4 * D], BF16)
    tl_pre4 = [None, None]

    for grp in range(8):
        if grp == 0:
            enq_V(0, (t_ckv,))
            t_wv = wtok["v"]
        else:
            t_wv = wtok["v"]
        ev_hist = [None, None]
        for tb in range(16):
            bank = PSB[tb % 2]
            mms = []
            for ct in range(4):
                l_h = ckv_hi[:, ct * S + tb * 128: ct * S + (tb + 1) * 128]
                l_l = ckv_lo[:, ct * S + tb * 128: ct * S + (tb + 1) * 128]
                r_h = wh_a[:, ct * 256:(ct + 1) * 256]
                r_l = wl_a[:, ct * 256:(ct + 1) * 256]
                mms += [(l_h, r_h), (l_h, r_l), (l_l, r_h)]
            wts = [t_wv]
            if t_v_read is not None:
                wts.append(t_v_read)
            if ev_hist[tb % 2] is not None:
                wts.append(ev_hist[tb % 2])
            t_mm = mm_group(bank[0:128, 0:256], mms, waits=wts,
                            cons=(tb == 15))
            if tb == 15:
                t_vgen_cons = t_mm
            ev_hist[tb % 2] = evac_split(bank[0:128, 0:256],
                                 v2_hi[:, tb * 256:(tb + 1) * 256],
                                 v2_lo[:, tb * 256:(tb + 1) * 256], t_mm)
        t_v2 = ev_hist[15 % 2]

        for h2 in range(2):
            h = grp * 2 + h2
            if grp == 0 and h2 == 0:
                enq_K(0, 0, (t_cq, t_masks))
                enq_Q(0, 0, (t_cq, t_ckv))
            t_wk, t_wq = wtok["k"], wtok["q"]
            ev_kh = [None, None]
            for tc in range(4):
                bank = PSB[2 + tc % 2]
                mms = []
                for ct in range(4):
                    w_h = wh_b[:, ct * 128:(ct + 1) * 128]
                    w_l = wl_b[:, ct * 128:(ct + 1) * 128]
                    a_h = ckv_hi[:, ct * S + tc * 512: ct * S + (tc + 1) * 512]
                    a_l = ckv_lo[:, ct * S + tc * 512: ct * S + (tc + 1) * 512]
                    mms += [(w_h, a_h), (w_h, a_l), (w_l, a_h)]
                wts = [t_wk]
                if t_hcons is not None:
                    wts.append(t_hcons)
                if t_oa_bank[tc % 2] is not None:
                    wts.append(t_oa_bank[tc % 2])
                if ev_kh[tc % 2] is not None:
                    wts.append(ev_kh[tc % 2])
                t_mm = mm_group(bank[:], mms, waits=wts, cons=(tc == 3))
                ev_kh[tc % 2] = evac_split(
                    bank[:], kh_hi[:, tc * 512:(tc + 1) * 512],
                    kh_lo[:, tc * 512:(tc + 1) * 512], t_mm)
                if tc == 3:
                    t_kh3 = t_mm
            b.wait("vector", ev_kh[0])
            t_kh = ev_kh[1]

            ev_q = None
            for qc in range(2):
                bank = PSB[2 + qc]
                mms = []
                for ct in range(4):
                    w_h = wh_c[:, ct * 192: ct * 192 + 128]
                    w_l = wl_c[:, ct * 192: ct * 192 + 128]
                    a_h = cq_hi[:, ct * TQ + qc * 512: ct * TQ + (qc + 1) * 512]
                    a_l = cq_lo[:, ct * TQ + qc * 512: ct * TQ + (qc + 1) * 512]
                    mms += [(w_h, a_h), (w_h, a_l), (w_l, a_h)]
                wts = [t_wq, t_kh]
                if t_hcons is not None:
                    wts.append(t_hcons)
                if t_oa_bank[qc] is not None:
                    wts.append(t_oa_bank[qc])
                if ev_q is not None:
                    wts.append(ev_q)
                t_mm = mm_group(bank[:], mms, waits=wts, cons=(qc == 1))
                ev_q = evac_split(bank[:], qn_hi[:, qc * 512:(qc + 1) * 512],
                                  qn_lo[:, qc * 512:(qc + 1) * 512], t_mm)
            t_qr = None
            for qc in range(2):
                mms = []
                for ct in range(4):
                    w_h = wh_c[:, ct * 192 + 128: ct * 192 + 192]
                    w_l = wl_c[:, ct * 192 + 128: ct * 192 + 192]
                    a_h = cq_hi[:, ct * TQ + qc * 512: ct * TQ + (qc + 1) * 512]
                    a_l = cq_lo[:, ct * TQ + qc * 512: ct * TQ + (qc + 1) * 512]
                    mms += [(w_h, a_h), (w_h, a_l), (w_l, a_h)]
                wts = [ev_q]
                if t_hcons is not None:
                    wts.append(t_hcons)
                if t_qr is not None:
                    wts.append(t_qr)
                t_mm = mm_group(PSB[4][0:64, :], mms, waits=wts,
                                cons=(qc == 1))
                if qc == 1:
                    t_qrmm_prev = t_mm
                b.wait("vector", t_mm)
                b.do("vector", lambda e, qc=qc: e.tensor_copy(
                    out=qr_pre[:, qc * 512:(qc + 1) * 512],
                    in_=PSB[4][0:64, :]))
                b.drain("vector")
                t_qr = b.do("vector", lambda e: e.engine_nop(), mark=ve)

            def qr_dst(i, t_tr, bank):
                b.wait("vector", t_tr)
                b.do("vector", lambda e, i=i, bank=bank: e.tensor_copy(
                    out=qr_pack[0:64, i * 128:(i + 1) * 128],
                    in_=bank[0:64, 128:256]))
                b.drain("vector")
                return b.do("vector", lambda e, i=i, bank=bank:
                            e.tensor_tensor(
                    out=qr_lo[:, i * 128:(i + 1) * 128],
                    in0=bank[0:64, 128:256],
                    in1=qr_pack[0:64, i * 128:(i + 1) * 128],
                    op=OP.subtract), mark=ve)

            t_qrope = rope_tiles(qr_pre, 8, cosq_t, sinq_t, qr_dst,
                                 waits0=(t_qr, t_cosq, t_hcons) if t_hcons
                                 else (t_qr, t_cosq))
            b.wait("vector", t_qrope)
            t_qrope = b.do("vector", lambda e: e.tensor_copy(
                out=qr_pack[64:128, :], in_=qr_pack[0:64, :]), mark=ve)

            if h2 == 0:
                enq_K(grp, 1, (t_kh3,), at=(0, 4))
                enq_Q(grp, 1, (t_qrmm_prev,), at=(1, 2))
            elif grp < 7:
                enq_V(grp + 1, (t_vgen_cons, t_qrope), at=(0, 4))
                enq_K(grp + 1, 0, (t_kh3,), at=(1, 2))
                enq_Q(grp + 1, 0, (t_qrmm_prev,), at=(1, 8))
            else:
                tl_pre4[0] = chw.load(wstage4[:], w_o[0:128, :],
                                      after=(t_kh3,))
                tl_pre4[1] = chw.load(wstage4b[:], w_o[128:256, :],
                                      after=(t_kh3,))

            for cidx, nkb in ((0, KB_LO), (1, KB_HI)):
                qoff = cidx * 512
                out_bank = PSB[2 + cidx]
                den_bank = PSB[5]
                t_av = None
                pend = None

                def emit_av(kb, par, t_p, nkb=nkb, out_bank=out_bank,
                            den_bank=den_bank, h2=h2):
                    b.wait("tensor", t_p)
                    if kb == 0 and t_rt_read[0] is not None:
                        b.wait("tensor", t_rt_read[0])
                    first = (kb == 0)
                    last_kb = (kb == nkb - 1)
                    vof = kb * 256 + h2 * 128
                    b.do("tensor", lambda e, par=par, first=first,
                         db=den_bank: e.matmul(
                        db[:], ones_b[:], p_hi[par][:], start=first,
                        stop=False))
                    b.do("tensor", lambda e, par=par, last=last_kb,
                         db=den_bank: e.matmul(
                        db[:], ones_b[:], p_lo[par][:], start=False,
                        stop=last))
                    b.do("tensor", lambda e, par=par, first=first, vof=vof,
                         ob=out_bank: e.matmul(
                        ob[:], v2_hi[:, vof:vof + 128],
                        p_hi[par][:], start=first, stop=False))
                    b.do("tensor", lambda e, par=par, vof=vof,
                         ob=out_bank: e.matmul(
                        ob[:], v2_hi[:, vof:vof + 128], p_lo[par][:],
                        start=False, stop=False))
                    b.drain("tensor")
                    return b.do("tensor", lambda e, par=par, vof=vof,
                                last=last_kb, ob=out_bank: e.matmul(
                        ob[:], v2_lo[:, vof:vof + 128], p_hi[par][:],
                        start=False, stop=last), mark=te)

                for kb in range(nkb):
                    par = kb % 2
                    sbank = PSB[par]
                    wts = [t_qrope, t_v2, t_kh, t_masks]
                    if t_exp[par] is not None:
                        wts.append(t_exp[par])
                    b.waits("tensor", wts)
                    b.do("tensor", lambda e, kb=kb, qoff=qoff, sbank=sbank:
                         e.matmul(sbank[:],
                                  kh_hi[:, kb * 128:(kb + 1) * 128],
                                  qn_hi[:, qoff:qoff + 512], start=True,
                                  stop=False))
                    b.do("tensor", lambda e, kb=kb, qoff=qoff, sbank=sbank:
                         e.matmul(sbank[:],
                                  kh_hi[:, kb * 128:(kb + 1) * 128],
                                  qn_lo[:, qoff:qoff + 512], start=False,
                                  stop=False))
                    b.do("tensor", lambda e, kb=kb, qoff=qoff, sbank=sbank:
                         e.matmul(sbank[:],
                                  kh_lo[:, kb * 128:(kb + 1) * 128],
                                  qn_hi[:, qoff:qoff + 512], start=False,
                                  stop=False))
                    b.do("tensor", lambda e, kb=kb, qoff=qoff, sbank=sbank:
                         e.matmul(sbank[:], krp[:, kb * 128:(kb + 1) * 128],
                                  qr_pack[:, qoff:qoff + 512], start=False,
                                  stop=False))
                    if cidx == 1 and kb == nkb - 1:
                        b.drain("tensor")
                    t_s = b.do("tensor", lambda e, kb=kb, qoff=qoff,
                               sbank=sbank: e.matmul(
                        sbank[:], krp[0:64, kb * 128:(kb + 1) * 128],
                        qr_lo[:, qoff:qoff + 512], start=False, stop=True),
                        mark=te)
                    if cidx == 1 and kb == nkb - 1:
                        t_hcons = t_s
                    masked = (cidx == 0) or (kb >= 8)
                    b.wait("scalar", t_s)
                    if masked:
                        t_e = b.do("scalar", lambda e, sbank=sbank, par=par:
                                   e.activation(out=p_f[par][:],
                                                in_=sbank[:],
                                                func=AF.Exp, scale=SCALE),
                                   mark=ac)
                        t_exp[par] = t_e
                        b.wait("vector", t_e)
                        if t_pcons[par] is not None:
                            b.wait("vector", t_pcons[par])
                        b.do("vector", lambda e, par=par, kb=kb:
                             e.scalar_tensor_tensor(
                                 out=p_mm[:], in0=p_f[par][:], scalar=1.0,
                                 in1=masks16[:, kb * 512:(kb + 1) * 512],
                                 op0=OP.mult, op1=OP.mult))
                        b.drain("vector")
                        b.do("vector", lambda e, par=par:
                             e.tensor_copy(out=p_hi[par][:], in_=p_mm[:]))
                        b.drain("vector")
                        t_p = b.do("vector", lambda e, par=par:
                                   e.tensor_tensor(out=p_lo[par][:],
                                                   in0=p_mm[:],
                                                   in1=p_hi[par][:],
                                                   op=OP.subtract), mark=ve)
                    else:
                        b.do("scalar", lambda e, sbank=sbank, par=par:
                             e.activation(out=p_f[par][:], in_=sbank[:],
                                          func=AF.Exp, scale=SCALE))
                        if t_pcons[par] is not None:
                            b.wait("scalar", t_pcons[par])
                        b.drain("scalar")
                        t_e = b.do("scalar", lambda e, sbank=sbank, par=par:
                                   e.activation(out=p_hi[par][:],
                                                in_=sbank[:],
                                                func=AF.Exp, scale=SCALE),
                                   mark=ac)
                        t_exp[par] = t_e
                        b.wait("vector", t_e)
                        if t_pcons[par] is not None:
                            b.wait("vector", t_pcons[par])
                        t_p = b.do("vector", lambda e, par=par:
                                   e.tensor_tensor(out=p_lo[par][:],
                                                   in0=p_f[par][:],
                                                   in1=p_hi[par][:],
                                                   op=OP.subtract), mark=ve)
                    for ps in [p for p in pend_splits
                               if p[0] == cidx and p[1] == kb]:
                        ps[2]()
                        pend_splits.remove(ps)
                    if pend is not None:
                        t_av = emit_av(*pend)
                        t_pcons[pend[1]] = t_av
                    pend = (kb, par, t_p)
                t_av = emit_av(*pend)
                t_pcons[pend[1]] = t_av
                b.wait("scalar", t_av)
                b.do("scalar", lambda e, den_bank=den_bank: e.activation(
                    out=rt1[:], in_=den_bank[:], func=AF.Ln, scale=1.0))
                b.drain("scalar")
                t_rt = b.do("scalar", lambda e: e.activation(
                    out=rt1[:], in_=rt1[:], func=AF.Exp, scale=-1.0),
                    mark=ac)
                t_rt_read[0] = t_rt
                b.wait("vector", t_av)
                b.wait("vector", t_rt)
                oa_off = h * TQ + qoff
                b.do("vector", lambda e, out_bank=out_bank: e.tensor_tensor(
                    out=sqf[:], in0=out_bank[:], in1=rt1[:], op=OP.mult))
                b.drain("vector")
                b.do("vector", lambda e, oa_off=oa_off: e.tensor_copy(
                    out=oa_hi[:, oa_off:oa_off + 512], in_=sqf[:]))
                b.drain("vector")
                t_oa = b.do("vector", lambda e, oa_off=oa_off:
                            e.tensor_tensor(
                    out=oa_lo[:, oa_off:oa_off + 512], in0=sqf[:],
                    in1=oa_hi[:, oa_off:oa_off + 512], op=OP.subtract),
                    mark=ve)
                t_oa_last = t_oa
                t_oa_bank[cidx] = t_oa
        t_v_read = t_av

    if "oa" in taps:
        tt = tapt("oa_hi", [128, H * TQ])
        cht.store(tt[:, :], oa_hi[:], after=(t_oa_last,))
        tt2 = tapt("oa_lo", [128, H * TQ])
        cht.store(tt2[:, :], oa_lo[:], after=(t_oa_last,))
    if "attdbg" in taps:
        for nm, buf, sh in (("kh_hi", kh_hi, [128, S]), ("kh_lo", kh_lo, [128, S]),
                            ("qn_hi", qn_hi, [128, TQ]), ("qn_lo", qn_lo, [128, TQ]),
                            ("qr_pack", qr_pack, [128, TQ]), ("qr_lo", qr_lo, [64, TQ]),
                            ("v2_hi", v2_hi, [128, 16 * 256]),
                            ("v2_lo", v2_lo, [128, 16 * 256]),
                            ("krp", krp, [128, S]),
                            ("masks", masks16, [128, 16 * 512]),
                            ("p_f", p_f[1], [128, 512]),
                            ("p_hi", p_hi[1], [128, 512]),
                            ("p_lo", p_lo[1], [128, 512]),
                            ("rt1", rt1, [128, 512]),
                            ("sqf", sqf, [128, 512])):
            tt = tapt(nm, sh)
            cht.store(tt[tuple(slice(None) for _ in sh)], buf[:],
                      after=(t_oa_last,))

    if stage <= 3:
        if cht.c.n:
            b.wait("gpsimd", (cht.c.sem, cht.c.n))
        with nc.Block() as block:
            b.realize(block)
        return nc, tap_dr

    b.barrier()
    SCR.reset()
    acc_tok = SCR.alloc("acc_tok", [128, 4 * D])
    wo_hi = SCR.alloc("wo_hi", [128, 4 * D], BF16)
    wo_lo = SCR.alloc("wo_lo", [128, 4 * D], BF16)
    wo_pairs = [(wo_hi, wo_lo), (wo_hi2, wo_lo2)]
    stg4 = [wstage4, wstage4b]

    t_acc_last = None
    t_accst = None
    spl_stg = [None, None]
    t_wo_cons2 = [None, None]
    spl_cur = [None] * 4
    tl_next = [None] * 4
    pend4 = []

    def wo_load(gidx, r4, aft):
        return chw.load(stg4[r4 % 2][:],
                        w_o[(gidx % 4) * 512 + r4 * 128:
                            (gidx % 4) * 512 + (r4 + 1) * 128, :],
                        after=tuple(t for t in aft if t is not None))

    def wo_split(gidx, r4, tl):
        hi, lo = wo_pairs[gidx % 2]
        tok = split3(hi[:, r4 * D:(r4 + 1) * D],
                     lo[:, r4 * D:(r4 + 1) * D], stg4[r4 % 2][:],
                     tuple(t for t in (tl, t_wo_cons2[gidx % 2])
                           if t is not None))
        spl_stg[r4 % 2] = tok
        return tok

    for gidx in range(8):
        qhalf, grp4 = gidx // 4, gidx % 4
        if gidx == 0:
            spl_cur[0] = wo_split(0, 0, tl_pre4[0])
            tl = wo_load(0, 2, (spl_stg[0],))
            spl_cur[1] = wo_split(0, 1, tl_pre4[1])
            tl3 = wo_load(0, 3, (spl_stg[1],))
            spl_cur[2] = wo_split(0, 2, tl)
            spl_cur[3] = wo_split(0, 3, tl3)
        if gidx < 7:
            tl_next[0] = wo_load(gidx + 1, 0, (spl_stg[0],))
            tl_next[1] = wo_load(gidx + 1, 1, (spl_stg[1],))

            def mk_early(g, r, tl):
                def fn():
                    tok = wo_split(g, r, tl)
                    tl_next[r + 2] = wo_load(g, r + 2, (tok,))
                    return tok
                return fn

            def mk_late(g, r):
                def fn():
                    return wo_split(g, r, tl_next[r])
                return fn

            pend4 = [(5, 0, mk_early(gidx + 1, 0, tl_next[0])),
                     (8, 1, mk_early(gidx + 1, 1, tl_next[1])),
                     (11, 2, mk_late(gidx + 1, 2)),
                     (14, 3, mk_late(gidx + 1, 3))]
        else:
            pend4 = []
        nxt = [None] * 4
        ev_prev = None
        for qt4 in range(4):
            qt = qhalf * 4 + qt4
            for nch in range(4):
                j = qt4 * 4 + nch
                bank = PSB[j % 2]
                hi, lo = wo_pairs[gidx % 2]
                mms = []
                pre = {}
                for ct4 in range(4):
                    hh = grp4 * 4 + ct4
                    pre[len(mms)] = (spl_cur[ct4],)
                    l_h = oa_hi[:, hh * TQ + qt * 128:
                                hh * TQ + (qt + 1) * 128]
                    l_l = oa_lo[:, hh * TQ + qt * 128:
                                hh * TQ + (qt + 1) * 128]
                    r_h = hi[:, ct4 * D + nch * 512:
                             ct4 * D + (nch + 1) * 512]
                    r_l = lo[:, ct4 * D + nch * 512:
                             ct4 * D + (nch + 1) * 512]
                    mms += [(l_h, r_h), (l_h, r_l), (l_l, r_h)]
                wts = []
                if ev_prev is not None:
                    wts.append(ev_prev)
                t_mm = mm_group(bank[:], mms, waits=wts, pre=pre,
                                cons=(j == 15))
                if j == 15:
                    t_wo_cons2[gidx % 2] = t_mm
                b.wait("vector", t_mm)
                if grp4 == 0 and t_accst is not None:
                    b.wait("vector", t_accst)
                off = qt4 * D + nch * 512
                if grp4 == 0:
                    ev_prev = b.do("vector", lambda e, bank=bank, off=off:
                                   e.tensor_copy(
                        out=acc_tok[:, off:off + 512], in_=bank[:]),
                        mark=ve)
                else:
                    ev_prev = b.do("vector", lambda e, bank=bank, off=off:
                                   e.tensor_tensor(
                        out=acc_tok[:, off:off + 512], in0=bank[:],
                        in1=acc_tok[:, off:off + 512], op=OP.add),
                        mark=ve)
                for at, idx, fn in [p for p in pend4 if p[0] == j]:
                    nxt[idx] = fn()
                    pend4.remove((at, idx, fn))
        spl_cur = nxt
        t_acc_last = ev_prev
        if grp4 == 3:
            for qt4 in range(4):
                qt = qhalf * 4 + qt4
                t_accst = chs.store(acc_scr[qt * 128:(qt + 1) * 128, :],
                                    acc_tok[:, qt4 * D:(qt4 + 1) * D],
                                    after=(t_acc_last,))
    if "acc" in taps:
        tt = tapt("acc", [128, 4 * D])
        cht.store(tt[:, :], acc_tok[:], after=(t_accst,))

    if stage <= 4:
        b.wait("scalar", (chs.c.sem, chs.c.n))
        if cht.c.n:
            b.wait("gpsimd", (cht.c.sem, cht.c.n))
        with nc.Block() as block:
            b.realize(block)
        return nc, tap_dr

    b.barrier()
    BIG.cur = 0
    l2f_hi = BIG.alloc("l2f_hi", [128, 16 * 512], BF16)
    l2f_lo = BIG.alloc("l2f_lo", [128, 16 * 512], BF16)
    hdn_hi = BIG.alloc("hdn_hi", [128, 22 * 512], BF16)
    hdn_lo = BIG.alloc("hdn_lo", [128, 22 * 512], BF16)
    dwT_blk = BIG.alloc("dwT_blk", [128, 22 * 256], BF16)
    dw_nat = BIG.alloc("dw_nat", [128, FI], BF16)
    fw_nat2 = [BIG.alloc(f"fw_nat{i}", [128, 2 * D], BF16) for i in range(2)]
    fwT2 = [BIG.alloc(f"fwT{i}", [128, D], BF16) for i in range(2)]
    dn_buf = BIG.alloc("dn_buf", [128, 4 * D])
    tmp_a = BIG.alloc("tmp_a", [128, 2048])
    tmp_b = BIG.alloc("tmp_b", [128, 2048])
    tmp_c = BIG.alloc("tmp_c", [128, 2048])
    xq_tile = BIG.alloc("xq_tile", [128, D])
    dsb_t = BIG.alloc("dsb_t", [128, D])
    gates_t = BIG.alloc("gates_t", [128, 64])
    upss_t = BIG.alloc("upss_t", [128, 64])

    chm.load(dsb_t[:], ds_b[:, :])
    chm.load(gates_t[:, 0:44], gate_s_t[:, :])
    t_meta5 = chm.load(upss_t[:, 0:44], up_s_t[:, :])
    b.wait("vector", t_meta5)

    t_x1st = [None] * 8
    t_l2f_used = None
    t_prev_store = [None]
    t_tmpa_read = [None]
    t_l2f_cell = [None]
    t_trguard = [None]
    pend_tr = [None]

    def emit_qt_ln(qt, tr_now=True):
        wts = []
        if t_prev_store[0] is not None:
            wts.append(t_prev_store[0])
        t_la = chs.load(tmp_a[:], acc_scr[qt * 128:(qt + 1) * 128, :],
                        after=(t_tmpa_read[0],) if t_tmpa_read[0] else ())
        tl = chx.load(xq_tile[:], xq[qt * 128:(qt + 1) * 128, :], after=wts)
        b.wait("vector", tl)
        b.wait("vector", t_la)
        t_x1 = ln_stats_apply(tmp_a[:], tmp_b[:], D, res_ap=xq_tile[:])
        t_tmpa_read[0] = t_x1
        t_st = chs.store(x1_scr[qt * 128:(qt + 1) * 128, :], tmp_b[:],
                         after=(t_x1,))
        t_x1st[qt] = t_st
        t_prev_store[0] = t_st
        t_ln2 = ln_stats_apply(tmp_b[:], tmp_c[:], D)

        def do_tr():
            wts = [t_ln2]
            if t_l2f_used is not None and qt % 4 == 0:
                wts.append(t_l2f_used)
            t_e, _ = tr_split_tile(tmp_c[:], l2f_hi, l2f_lo,
                                   (qt % 4) * 128, 512, waits=wts)
            t_l2f_cell[0] = t_e
            t_trguard[0] = t_e
        if tr_now:
            do_tr()
            return None
        return do_tr

    for sub in range(2):
        if sub == 0:
            for qt in range(4):
                emit_qt_ln(qt)
        t_l2f_e = t_l2f_cell[0]
        t_dn = [None] * 4
        for half in range(2):
            t_hdn_last = None
            t_gu_read = None
            b.wait("tensor", t_l2f_e)
            units = []
            for m22 in range(22):
                m = half * 22 + m22
                units.append((m, m22, "g", gate_w, gwT_scr))
                units.append((m, m22, "u", up_w, uwT_scr))
            t_mm_hist = [None, None]
            t_scr_hist = [None, None]
            t_fw_hist = [None, None]
            t_prev_evac = [None]
            t_wT_tok = [None] * len(units)

            def stage_unit(i):
                m, m22, wname, wsrc_d, scr = units[i]
                pb = i % 2
                if sub == 0:
                    awts = tuple(t for t in (t_fw_hist[pb],) if t)
                    tg = chw.load(fw_nat2[pb][:],
                                  wsrc_d[m * 128:(m + 1) * 128, :],
                                  after=awts)
                    prow_fw = list(fw_nat2[pb][:].ap[0])

                    def fw_hi(dct, pb=pb, prow=prow_fw):
                        return pap(fw_nat2[pb], 2 * dct * 128 + 1,
                                   [prow, [2, 128]])
                    t_wT = None
                    for hf in range(2):
                        b.wait("tensor", tg)
                        if t_wT is not None:
                            b.wait("tensor", t_wT)
                        elif t_prev_evac[0] is not None:
                            b.wait("tensor", t_prev_evac[0])
                        if t_mm_hist[pb] is not None:
                            b.wait("tensor", t_mm_hist[pb])
                        for j in range(7):
                            dct = hf * 8 + j
                            b.do("tensor", lambda e, j=j, dct=dct:
                                 e.transpose(
                                PSBF[:, j * 128:(j + 1) * 128],
                                fw_hi(dct), ident_b[:]))
                        b.drain("tensor")
                        t_tr = b.do("tensor", lambda e, dct=hf * 8 + 7:
                                    e.transpose(
                            PSBF[:, 7 * 128:8 * 128],
                            fw_hi(dct), ident_b[:]), mark=te)
                        t_fw_hist[pb] = t_tr
                        b.wait("vector", t_tr)
                        if t_scr_hist[pb] is not None:
                            b.wait("vector", t_scr_hist[pb])
                        if t_mm_hist[pb] is not None:
                            b.wait("vector", t_mm_hist[pb])
                        t_wT = b.do("vector", lambda e, hf=hf, pb=pb:
                                    e.tensor_copy(
                            out=fwT2[pb][:, hf * 1024:(hf + 1) * 1024],
                            in_=PSBF[:]), mark=ve)
                    t_scr_hist[pb] = chs.store(
                        scr[m * 128:(m + 1) * 128, :], fwT2[pb][:],
                        after=(t_wT,))
                    t_prev_evac[0] = t_wT
                    return t_wT
                else:
                    awts = tuple(t for t in (t_mm_hist[pb],) if t)
                    return chs.load(fwT2[pb][:],
                                    scr[m * 128:(m + 1) * 128, :],
                                    after=awts)

            t_wT_tok[0] = stage_unit(0)
            for i, (m, m22, wname, wsrc_d, scr) in enumerate(units):
                pb = i % 2
                if i + 1 < len(units):
                    t_wT_tok[i + 1] = stage_unit(i + 1)
                bank = PSB[0] if wname == "g" else PSB[1]
                mms = []
                for ct in range(16):
                    wt_sl = fwT2[pb][:, ct * 128:(ct + 1) * 128]
                    mms += [(wt_sl, l2f_hi[:, ct * 512:(ct + 1) * 512]),
                            (wt_sl, l2f_lo[:, ct * 512:(ct + 1) * 512])]
                wts2 = [t_wT_tok[i]]
                if t_hdn_last is not None:
                    wts2.append(t_hdn_last)
                t_mmg = mm_group(bank[:], mms, waits=wts2, cons=True)
                t_mm_hist[pb] = t_mmg
                if wname == "g":
                    t_g = t_mmg
                    continue
                t_u = t_mmg
                t_gu_read = t_u
                b.wait("vector", t_g)
                b.do("vector", lambda e, m=m: e.tensor_scalar(
                    out=tmp_c[:, 0:512], in0=PSB[0][:],
                    scalar1=gates_t[:, m:m + 1], scalar2=None, op0=OP.mult))
                b.drain("vector")
                t_t1 = b.do("vector", lambda e: e.engine_nop(), mark=ve)
                b.wait("scalar", t_t1)
                t_sg = b.do("scalar", lambda e: e.activation(
                    out=tmp_c[:, 512:1024], in_=tmp_c[:, 0:512],
                    func=AF.Sigmoid, scale=1.0), mark=ac)
                b.wait("vector", t_sg)
                b.wait("vector", t_u)
                b.do("vector", lambda e, m=m: e.scalar_tensor_tensor(
                    out=tmp_c[:, 1024:1536], in0=PSB[1][:],
                    scalar=upss_t[:, m:m + 1], in1=tmp_c[:, 512:1024],
                    op0=OP.mult, op1=OP.mult))
                b.drain("vector")
                b.do("vector", lambda e: e.tensor_tensor(
                    out=tmp_c[:, 0:512], in0=tmp_c[:, 0:512],
                    in1=tmp_c[:, 1024:1536], op=OP.mult))
                b.drain("vector")
                b.do("vector", lambda e, m22=m22: e.tensor_copy(
                    out=hdn_hi[:, m22 * 512:(m22 + 1) * 512],
                    in_=tmp_c[:, 0:512]))
                b.drain("vector")
                t_hdn_last = b.do("vector", lambda e, m22=m22:
                                  e.tensor_tensor(
                    out=hdn_lo[:, m22 * 512:(m22 + 1) * 512],
                    in0=tmp_c[:, 0:512],
                    in1=hdn_hi[:, m22 * 512:(m22 + 1) * 512],
                    op=OP.subtract), mark=ve)
            if half == 1:
                t_l2f_used = t_gu_read

            b.wait("vector", (cho.c.sem, cho.c.n) if cho.c.n else None)
            t_blk_read = None
            t_dw_rnd = [None] * 3
            t_dscr = None
            for nch in range(8):
                if sub == 0 and half == 1 and nch % 2 == 1:
                    if pend_tr[0] is not None:
                        pend_tr[0]()
                    pend_tr[0] = emit_qt_ln(4 + nch // 2, tr_now=False)
                scr_off = (half * 8 + nch) * 22 * 128
                scr_view = pap(dwT_scr.tensor, scr_off * 256,
                               [[256, 128], [128 * 256, 22], [1, 256]])
                blk_view = dwT_blk[:].rearrange("p (c f) -> p c f", f=256)
                if sub == 0:
                    t_blk = None
                    for j2 in range(2):
                        hrow = nch * 256 + j2 * 128
                        tdw_r = [None] * 3
                        for r in range(3):
                            c0, c1 = 2048 * r, min(FI, 2048 * (r + 1))
                            awts = tuple(t for t in (t_dw_rnd[r],) if t)
                            tdw_r[r] = chw.load(
                                dw_nat[:, c0:c1],
                                down_w[hrow:hrow + 128,
                                       half * FI + c0:half * FI + c1],
                                after=awts)
                        prow_dw = list(dw_nat[:].ap[0])

                        def dw_hi(ct, prow=prow_dw):
                            return pap(dw_nat, 2 * ct * 128 + 1,
                                       [prow, [2, 128]])
                        for rnd in range(3):
                            jmax = min(8, 22 - rnd * 8)
                            b.wait("tensor", tdw_r[rnd])
                            if t_trguard[0] is not None:
                                b.wait("tensor", t_trguard[0])
                                t_trguard[0] = None
                            if t_blk is not None:
                                b.wait("tensor", t_blk)
                            for j in range(jmax - 1):
                                ct = rnd * 8 + j
                                b.do("tensor", lambda e, j=j, ct=ct:
                                     e.transpose(
                                    PSBF[:, j * 128:(j + 1) * 128],
                                    dw_hi(ct), ident_b[:]))
                            b.drain("tensor")
                            t_tr = b.do("tensor", lambda e, j=jmax - 1,
                                        ct=rnd * 8 + jmax - 1: e.transpose(
                                PSBF[:, j * 128:(j + 1) * 128],
                                dw_hi(ct), ident_b[:]), mark=te)
                            t_dw_rnd[rnd] = t_tr
                            b.wait("vector", t_tr)
                            if t_dscr is not None:
                                b.wait("vector", t_dscr)
                            prow = list(dwT_blk[:].ap[0])
                            dview = pap(dwT_blk,
                                        (rnd * 8) * 256 + j2 * 128,
                                        [prow, [256, jmax], [1, 128]])
                            pview = PSBF[:, 0:jmax * 128].rearrange(
                                "p (a f) -> p a f", a=jmax)
                            t_blk = b.do("vector", lambda e, dv=dview,
                                         pv=pview: e.tensor_copy(
                                out=dv, in_=pv), mark=ve)
                    t_dscr = chs.store(scr_view, blk_view, after=(t_blk,))
                    t_blkr = t_blk
                else:
                    awts = tuple(t for t in (t_blk_read,) if t)
                    t_blkr = chs.load(blk_view, scr_view, after=awts)
                for qt4 in range(4):
                    bank = PSB[2 + qt4 % 2]
                    mms = []
                    for ct in range(22):
                        l_h = hdn_hi[:, ct * 512 + qt4 * 128:
                                     ct * 512 + (qt4 + 1) * 128]
                        l_l = hdn_lo[:, ct * 512 + qt4 * 128:
                                     ct * 512 + (qt4 + 1) * 128]
                        r = dwT_blk[:, ct * 256:(ct + 1) * 256]
                        mms += [(l_h, r), (l_l, r)]
                    wts3 = [t_blkr, t_hdn_last]
                    if t_dn[qt4] is not None:
                        wts3.append(t_dn[qt4])
                    if t_dn[(qt4 + 2) % 4] is not None:
                        wts3.append(t_dn[(qt4 + 2) % 4])
                    t_mm = mm_group(bank[0:128, 0:256], mms, waits=wts3,
                                    cons=(qt4 == 3))
                    b.wait("vector", t_mm)
                    off = qt4 * D + nch * 256
                    if half == 0:
                        t_dn[qt4] = b.do("vector", lambda e, bank=bank,
                                         off=off: e.tensor_copy(
                            out=dn_buf[:, off:off + 256],
                            in_=bank[0:128, 0:256]), mark=ve)
                    else:
                        t_dn[qt4] = b.do("vector", lambda e, bank=bank,
                                         off=off: e.tensor_tensor(
                            out=dn_buf[:, off:off + 256],
                            in0=bank[0:128, 0:256],
                            in1=dn_buf[:, off:off + 256], op=OP.add),
                            mark=ve)
                t_blk_read = t_mm

        if pend_tr[0] is not None:
            pend_tr[0]()
            pend_tr[0] = None

        for qt4 in range(4):
            qtg = sub * 4 + qt4
            b.wait("vector", t_dn[qt4])
            tx1 = chx.load(tmp_a[:], x1_scr[qtg * 128:(qtg + 1) * 128, :],
                           after=(t_x1st[qtg], t_tmpa_read[0]))
            b.wait("vector", tx1)
            dn_sl = dn_buf[:, qt4 * D:(qt4 + 1) * D]
            b.do("vector", lambda e, dn_sl=dn_sl: e.tensor_tensor(
                out=dn_sl, in0=dn_sl, in1=dsb_t[:], op=OP.mult))
            b.drain("vector")
            t_out = ln_stats_apply(dn_sl, dn_sl, D, res_ap=tmp_a[:])
            t_tmpa_read[0] = t_out
            t_prev_store[0] = cho.store(out_q[qtg * 128:(qtg + 1) * 128, :],
                                        dn_sl, after=(t_out,))

    b.wait("sync", (cho.c.sem, cho.c.n))
    if cht.c.n:
        b.wait("gpsimd", (cht.c.sem, cht.c.n))
    b.wait("scalar", (chs.c.sem, chs.c.n))

    with nc.Block() as block:
        b.realize(block)
    return nc, tap_dr


def _rope_tables_f32():
    t = (np.arange(0, DR, 2).astype(np.float32) / np.float32(DR))
    freqs = (np.float32(1.0) / (np.float32(10000.0) ** t)).astype(np.float32)
    ang = (np.arange(S, dtype=np.float32)[:, None] * freqs).astype(np.float32)
    ang = np.repeat(ang, 2, axis=-1)
    cos = np.cos(ang).astype(np.float32)
    sin = np.sin(ang).astype(np.float32)
    sinS = sin.copy()
    sinS[:, 0::2] *= np.float32(-1.0)
    return cos, sinS


def make_in_maps(inputs):
    f32 = lambda k: np.ascontiguousarray(np.asarray(inputs[k], dtype=np.float32))
    x = f32("L0_attn_x")
    cos, sinS = _rope_tables_f32()
    base = {
        "w_dq": f32("L0_attn_w_dq"), "w_uq": f32("L0_attn_w_uq"),
        "w_dkv": f32("L0_attn_w_dkv"), "w_ukrope": f32("L0_attn_w_ukrope"),
        "w_uv": f32("L0_attn_w_uv"), "w_uknope": f32("L0_attn_w_uknope"),
        "w_o": f32("L0_attn_w_o"),
        "gate_w": np.ascontiguousarray(f32("L0_ffn_gate_w")).view(
            ml_dtypes.bfloat16),
        "up_w": np.ascontiguousarray(f32("L0_ffn_up_w")).view(
            ml_dtypes.bfloat16),
        "down_w": np.ascontiguousarray(f32("L0_ffn_down_w")).view(
            ml_dtypes.bfloat16),
        "g_q_t": np.ascontiguousarray(f32("L0_attn_g_q").reshape(4, 128).T),
        "g_kv_t": np.ascontiguousarray(f32("L0_attn_g_kv").reshape(4, 128).T),
        "gate_s_t": np.ascontiguousarray(
            f32("L0_ffn_gate_s").reshape(44, 128).T),
        "up_s_t": np.ascontiguousarray(f32("L0_ffn_up_s").reshape(44, 128).T),
        "ds_b": np.ascontiguousarray(
            np.broadcast_to(f32("L0_ffn_down_s"), (128, D))),
        "cos_k": cos, "sinS_k": sinS,
    }
    in_maps, own_rows = [], []
    for cid in range(8):
        bb, par = cid // 2, cid % 2
        own = np.r_[par * 512:(par + 1) * 512, (3 - par) * 512:(4 - par) * 512]
        own_rows.append((bb, own))
        m = dict(base)
        m["x_b"] = np.ascontiguousarray(x[bb])
        m["xq"] = np.ascontiguousarray(x[bb][own])
        m["qpos_b"] = np.ascontiguousarray(
            np.broadcast_to(own.astype(np.float32), (128, TQ)))
        m["cos_q"] = np.ascontiguousarray(cos[own])
        m["sinS_q"] = np.ascontiguousarray(sinS[own])
        in_maps.append(m)
    return in_maps, own_rows


_PROG = {}


def kernel(**inputs):
    if "full" not in _PROG:
        _PROG["full"] = build_program(stage=9)
    nc, _ = _PROG["full"]
    in_maps, own_rows = make_in_maps(inputs)
    res = run_bass_kernel_spmd(nc, in_maps, core_ids=list(range(8)))
    out = np.empty((B, S, D), dtype=np.float32)
    for cid in range(8):
        bb, own = own_rows[cid]
        out[bb][own] = np.asarray(res.results[cid]["out_q"], dtype=np.float32)
    return out


# version 17
# speedup vs baseline: 1.0002x; 1.0002x over previous
import math
import numpy as np
import ml_dtypes
import concourse.bass as bass
import concourse.mybir as mybir
from concourse.bass_utils import run_bass_kernel_spmd

F32 = mybir.dt.float32
BF16 = mybir.dt.bfloat16
AF = mybir.ActivationFunctionType
OP = mybir.AluOpType

B, S, D = 4, 2048, 2048
H, DN, DR = 16, 128, 64
C = 512
UQ = 3072
FI = 5632
TQ = 1024
KB_LO, KB_HI = 8, 16
SCALE = 1.0 / math.sqrt(DN + DR)
EPS_LN = 1e-5
EPS_RMS = 1e-6

ENGINES = ("sync", "scalar", "vector", "tensor", "gpsimd")
_DTB = {F32: 4, BF16: 2}


class Ctr:
    def __init__(self, sem):
        self.sem = sem
        self.n = 0


class Build:
    def __init__(self, nc):
        self.nc = nc
        self.ops = {e: [] for e in ENGINES}
        self.last_wait = {e: {} for e in ENGINES}
        self.dirty = {e: False for e in ENGINES}
        self._ctrs = {}
        self._names = {}
        self.base = (nc.sbuf_base + 63) & ~63

    def ctr(self, name):
        c = self._ctrs.get(name)
        if c is None:
            c = Ctr(self.nc.alloc_semaphore(name))
            self._ctrs[name] = c
        return c

    def wait(self, e, tok):
        if tok is None:
            return
        sem, v = tok
        key = id(sem)
        if self.last_wait[e].get(key, -1) >= v:
            return
        self.last_wait[e][key] = v
        self.ops[e].append(lambda eng: eng.wait_ge(sem, v))

    def waits(self, e, toks):
        for t in toks:
            self.wait(e, t)

    def do(self, e, fn, mark=None, amt=1):
        if mark is not None:
            mark.n += amt
            tok = (mark.sem, mark.n)

            def wrapped(eng, fn=fn, sem=mark.sem, k=amt):
                fn(eng).then_inc(sem, k)
            self.ops[e].append(wrapped)
            if e in ("vector", "scalar", "gpsimd"):
                self.ops[e].append(lambda eng: eng.drain())
                self.dirty[e] = False
            else:
                self.dirty[e] = True
            return tok
        self.ops[e].append(lambda eng: fn(eng))
        self.dirty[e] = True
        return None

    def drain(self, e):
        if not self.dirty[e]:
            return
        self.ops[e].append(lambda eng: eng.drain())
        self.dirty[e] = False

    def uname(self, name):
        k = self._names.get(name, 0)
        self._names[name] = k + 1
        return name if k == 0 else f"{name}_{k}"

    def barrier(self):
        toks = []
        for name, c in self._ctrs.items():
            if c.n:
                toks.append((c.sem, c.n))
        marks = [
            self.do("vector", lambda e: e.engine_nop(), mark=self.ctr("ve")),
            self.do("scalar", lambda e: e.copy(out=self._bar_t[0:1, 0:1],
                                               in_=self._bar_t[0:1, 0:1]),
                    mark=self.ctr("ac")),
            self.do("gpsimd", lambda e: e.engine_nop(), mark=self.ctr("gp")),
        ]
        alltoks = toks + marks
        for e in ENGINES:
            self.waits(e, alltoks)

    def realize(self, block):
        for e in ENGINES:
            ops = self.ops[e]

            def runner(eng, ops=ops):
                for f in ops:
                    f(eng)
            getattr(block, e)(runner)


class Region:
    def __init__(self, b, start, end):
        self.b = b
        self.start = start
        self.end = end
        self.cur = 0

    def alloc(self, name, shape, dt=F32):
        nbytes = shape[-1] * _DTB[dt]
        off = (self.cur + 63) & ~63
        assert off + nbytes <= self.end - self.start, \
            f"region overflow: {name} needs {nbytes}B at {off}, " \
            f"cap {self.end - self.start}"
        self.cur = off + nbytes
        return self.b.nc.alloc_sbuf_tensor_at(
            self.b.uname(name), shape, dt,
            offset=self.b.base + self.start + off)

    def reset(self):
        self.cur = 0


class DmaChan:
    def __init__(self, b, name, engine="sync"):
        self.b = b
        self.e = engine
        self.c = b.ctr("dma_" + name)

    def xfer(self, out_ap, in_ap, after=()):
        b = self.b
        if self.c.n:
            b.wait(self.e, (self.c.sem, self.c.n))
        for t in after:
            b.wait(self.e, t)
        return b.do(self.e, lambda eng: eng.dma_start(out=out_ap, in_=in_ap),
                    mark=self.c, amt=16)

    load = xfer
    store = xfer


def build_program(stage=9, taps=()):
    nc = bass.Bass(trn_type="TRN2", target_bir_lowering=False)
    b = Build(nc)
    taps = set(taps)
    tap_dr = {}

    def tapt(name, shape):
        t = nc.dram_tensor("tap_" + name, shape, F32, kind="ExternalOutput").ap()
        tap_dr[name] = t
        return t

    def din(name, shape, dt=F32):
        return nc.dram_tensor(name, shape, dt, kind="ExternalInput").ap()

    x_b = din("x_b", [S, D])
    xq = din("xq", [TQ, D])
    w_dq = din("w_dq", [D, C])
    w_uq = din("w_uq", [C, UQ])
    w_dkv = din("w_dkv", [D, C])
    w_ukrope = din("w_ukrope", [D, DR])
    w_uv = din("w_uv", [C, D])
    w_uknope = din("w_uknope", [C, D])
    w_o = din("w_o", [D, D])
    gate_w = din("gate_w", [FI, 2 * D], BF16)
    up_w = din("up_w", [FI, 2 * D], BF16)
    down_w = din("down_w", [D, 2 * FI], BF16)
    g_q_t = din("g_q_t", [128, 4])
    g_kv_t = din("g_kv_t", [128, 4])
    gate_s_t = din("gate_s_t", [128, 44])
    up_s_t = din("up_s_t", [128, 44])
    ds_b = din("ds_b", [128, D])
    qpos_b = din("qpos_b", [128, TQ])
    cos_k = din("cos_k", [S, DR])
    sinS_k = din("sinS_k", [S, DR])
    cos_q = din("cos_q", [TQ, DR])
    sinS_q = din("sinS_q", [TQ, DR])

    out_q = nc.dram_tensor("out_q", [TQ, D], F32, kind="ExternalOutput").ap()
    x1_scr = nc.dram_tensor("x1_scr", [TQ, D], F32, kind="Internal").ap()
    acc_scr = nc.dram_tensor("acc_scr", [TQ, D], F32, kind="Internal").ap()
    gwT_scr = nc.dram_tensor("gwT_scr", [44 * 128, D], BF16, kind="Internal").ap()
    uwT_scr = nc.dram_tensor("uwT_scr", [44 * 128, D], BF16, kind="Internal").ap()
    dwT_scr = nc.dram_tensor("dwT_scr", [2 * 8 * 22 * 128, 256], BF16,
                             kind="Internal").ap()

    KBY = 1024
    ST = Region(b, 0, 9 * KBY)
    RA = Region(b, 9 * KBY, 61 * KBY)
    RB = Region(b, 61 * KBY, 93 * KBY)
    SEND = (229344 - 64) - b.base
    SCR = Region(b, 93 * KBY, SEND)
    BIG = Region(b, 9 * KBY, SEND)

    iota_p = ST.alloc("iota_p", [128, 16])
    iota_row = ST.alloc("iota_row", [128, 128])
    eps_ln_t = ST.alloc("eps_ln_t", [128, 16])
    eps_rms_t = ST.alloc("eps_rms_t", [128, 16])
    ident_f = ST.alloc("ident_f", [128, 128])
    ident_b = ST.alloc("ident_b", [128, 128], BF16)
    ones_b = ST.alloc("ones_b", [128, 128], BF16)
    gq_t = ST.alloc("gq_t", [128, 16])
    gkv_t = ST.alloc("gkv_t", [128, 16])
    thr = ST.alloc("thr", [128, 16])
    stat6 = ST.alloc("stat6", [128, 32])
    mv2 = ST.alloc("mv2", [128, 16])
    negm = ST.alloc("negm", [128, 16])
    rsq1 = ST.alloc("rsq1", [128, 16])
    rt1 = ST.alloc("rt1", [128, 512])
    sqf = ST.alloc("sqf", [128, 512])
    sqb = ST.alloc("sqb", [128, 512], BF16)
    sqb2 = ST.alloc("sqb2", [128, 512], BF16)
    rop_a = ST.alloc("rop_a", [128, 64])
    rop_c = ST.alloc("rop_c", [128, 64])
    b._bar_t = rsq1

    ckv_hi = RA.alloc("ckv_hi", [128, 4 * S], BF16)
    ckv_lo = RA.alloc("ckv_lo", [128, 4 * S], BF16)
    cq_hi = RA.alloc("cq_hi", [128, 4 * TQ], BF16)
    cq_lo = RA.alloc("cq_lo", [128, 4 * TQ], BF16)
    krp = RA.alloc("krp", [128, S], BF16)

    PSB = [nc.alloc_psum_tensor(f"psb{i}", [128, 512], F32) for i in range(6)]
    PS_TR2 = nc.alloc_psum_tensor("ps_tr2", [128, 512], F32)
    nc.bump_psum(banks=-1)
    PSBF = nc.alloc_psum_tensor("psbf", [128, 1024], BF16)
    PS_TR = nc.alloc_psum_tensor("ps_tr", [128, 512], F32)

    chx = DmaChan(b, "x", "sync")
    chw = DmaChan(b, "w", "sync")
    chm = DmaChan(b, "meta", "sync")
    cho = DmaChan(b, "out", "sync")
    chs = DmaChan(b, "scr", "scalar")
    chc = DmaChan(b, "cast", "gpsimd")
    cht = DmaChan(b, "tap", "gpsimd")

    ve = b.ctr("ve")
    ac = b.ctr("ac")
    te = b.ctr("te")
    gp = b.ctr("gp")

    def pap(handle, offset, dims):
        return bass.AP(handle, offset, dims)

    b.do("gpsimd", lambda e: e.iota(iota_p[:, 0:1], pattern=[[0, 1]], base=0,
                                    channel_multiplier=1,
                                    allow_small_or_imprecise_dtypes=True))
    b.do("gpsimd", lambda e: e.iota(iota_row[:], pattern=[[1, 128]], base=0,
                                    channel_multiplier=0,
                                    allow_small_or_imprecise_dtypes=True))
    b.do("gpsimd", lambda e: e.memset(ones_b[:], 1.0))
    b.do("gpsimd", lambda e: e.memset(eps_ln_t[:, 0:1], EPS_LN))
    b.do("gpsimd", lambda e: e.memset(eps_rms_t[:, 0:1], EPS_RMS))
    b.drain("gpsimd")
    t_const = b.do("gpsimd", lambda e: e.engine_nop(), mark=gp)

    b.wait("vector", t_const)
    b.do("vector", lambda e: e.tensor_scalar(out=ident_f[:], in0=iota_row[:],
                                             scalar1=iota_p[:, 0:1],
                                             scalar2=None, op0=OP.is_equal))
    b.drain("vector")
    b.do("vector", lambda e: e.tensor_copy(out=ident_b[:], in_=ident_f[:]))
    for kb in range(16):
        b.do("vector", lambda e, kb=kb: e.tensor_scalar(
            out=thr[:, kb:kb + 1], in0=iota_p[:, 0:1],
            scalar1=float(128 * kb), scalar2=None, op0=OP.add))
    b.drain("vector")
    t_ident = b.do("vector", lambda e: e.engine_nop(), mark=ve)

    chm.load(gq_t[:, 0:4], g_q_t[:, :])
    chm.load(gkv_t[:, 0:4], g_kv_t[:, :])

    def ln_stats_apply(src_ap, dst_ap, width, res_ap=None):
        nparts = width // 512
        for i in range(nparts):
            b.do("vector", lambda e, i=i: e.bn_stats(
                out=stat6[:, i * 6:(i + 1) * 6],
                in_=src_ap[:, i * 512:(i + 1) * 512]))
        b.drain("vector")
        b.do("vector", lambda e: e.bn_aggr(out=mv2[:, 0:2],
                                           in_=stat6[:, 0:6 * nparts]))
        b.drain("vector")
        b.do("vector", lambda e: e.tensor_scalar(
            out=negm[:, 0:1], in0=mv2[:, 0:1], scalar1=-1.0, scalar2=None,
            op0=OP.mult))
        t_v = b.do("vector", lambda e: e.engine_nop(), mark=ve)
        b.wait("scalar", t_v)
        b.do("scalar", lambda e: e.activation(out=rsq1[:, 0:1],
                                              in_=mv2[:, 1:2], func=AF.Ln,
                                              bias=eps_ln_t[:, 0:1],
                                              scale=1.0))
        b.drain("scalar")
        t_ac = b.do("scalar", lambda e: e.activation(
            out=rsq1[:, 0:1], in_=rsq1[:, 0:1], func=AF.Exp, scale=-0.5),
            mark=ac)
        b.wait("vector", t_ac)
        b.do("vector", lambda e: e.tensor_tensor(
            out=negm[:, 1:2], in0=negm[:, 0:1], in1=rsq1[:, 0:1],
            op=OP.mult))
        b.drain("vector")
        t_nm = b.do("vector", lambda e: e.engine_nop(), mark=ve)
        b.wait("scalar", t_nm)
        b.do("scalar", lambda e: e.activation(
            out=dst_ap, in_=src_ap, func=AF.Identity,
            scale=rsq1[:, 0:1], bias=negm[:, 1:2]))
        b.drain("scalar")
        t_ap = b.do("scalar", lambda e: e.copy(out=rsq1[0:1, 0:1],
                                               in_=rsq1[0:1, 0:1]), mark=ac)
        if res_ap is not None:
            b.wait("vector", t_ap)
            b.do("vector", lambda e: e.tensor_tensor(
                out=dst_ap, in0=dst_ap, in1=res_ap, op=OP.add))
            b.drain("vector")
            return b.do("vector", lambda e: e.engine_nop(), mark=ve)
        return t_ap

    def mm_group(bank_ap, mms, waits=(), cons=False, pre=None):
        b.waits("tensor", waits)
        n = len(mms)
        tok = None
        for i, (l, r) in enumerate(mms):
            if pre and i in pre:
                b.waits("tensor", tuple(t for t in pre[i] if t is not None))
            s, t = (i == 0), (i == n - 1)
            if t:
                if cons:
                    b.drain("tensor")
                tok = b.do("tensor", lambda e, l=l, r=r, s=s:
                           e.matmul(bank_ap, l, r, start=s, stop=True),
                           mark=te)
            else:
                b.do("tensor", lambda e, l=l, r=r, s=s:
                     e.matmul(bank_ap, l, r, start=s, stop=False))
        return tok

    def split3(hi_ap, lo_ap, src_ap, waits=()):
        b.waits("vector", waits)
        b.do("vector", lambda e: e.tensor_copy(out=hi_ap, in_=src_ap))
        b.drain("vector")
        return b.do("vector", lambda e: e.tensor_tensor(
            out=lo_ap, in0=src_ap, in1=hi_ap, op=OP.subtract), mark=ve)

    def evac_split(bank_ap, hi_ap, lo_ap, t_mm):
        b.wait("scalar", t_mm)
        b.do("scalar", lambda e: e.activation(out=hi_ap, in_=bank_ap,
                                              func=AF.Identity, scale=1.0))
        b.drain("scalar")
        t_hi = b.do("scalar", lambda e: e.copy(out=rsq1[0:1, 0:1],
                                               in_=rsq1[0:1, 0:1]), mark=ac)
        b.wait("vector", t_mm)
        b.wait("vector", t_hi)
        return b.do("vector", lambda e: e.tensor_tensor(
            out=lo_ap, in0=bank_ap, in1=hi_ap, op=OP.subtract), mark=ve)

    def tr_split_tile(src_f32, dst_hi, dst_lo, col_off, col_stride, waits=(),
                      nbanks=2):
        t_ev2 = [None, None]
        t_ev = t_te = None
        for rnd in range(4):
            par = rnd % nbanks
            bank = PS_TR if par == 0 else PS_TR2
            b.waits("tensor", waits)
            if t_ev2[par] is not None:
                b.wait("tensor", t_ev2[par])
            for j in range(3):
                dct = rnd * 4 + j
                b.do("tensor", lambda e, j=j, dct=dct, bank=bank: e.transpose(
                    bank[:, j * 128:(j + 1) * 128],
                    src_f32[:, dct * 128:(dct + 1) * 128], ident_f[:]))
            b.drain("tensor")
            t_te = b.do("tensor", lambda e, dct=rnd * 4 + 3, bank=bank:
                        e.transpose(
                bank[:, 3 * 128:4 * 128],
                src_f32[:, dct * 128:(dct + 1) * 128], ident_f[:]), mark=te)
            prow = list(dst_hi[:].ap[0])
            off = (rnd * 4) * col_stride + col_off
            hi_view = pap(dst_hi, off, [prow, [col_stride, 4], [1, 128]])
            lo_view = pap(dst_lo, off, [prow, [col_stride, 4], [1, 128]])
            ps_view = bank[:].rearrange("p (a f) -> p a f", a=4)
            b.wait("scalar", t_te)
            b.do("scalar", lambda e, hv=hi_view, pv=ps_view: e.activation(
                out=hv, in_=pv, func=AF.Identity, scale=1.0))
            b.drain("scalar")
            t_hi = b.do("scalar", lambda e: e.copy(out=rsq1[0:1, 0:1],
                                                   in_=rsq1[0:1, 0:1]),
                        mark=ac)
            b.wait("vector", t_te)
            b.wait("vector", t_hi)
            t_ev = b.do("vector", lambda e, lv=lo_view, pv=ps_view,
                        hv=hi_view: e.tensor_tensor(
                out=lv, in0=pv, in1=hv, op=OP.subtract), mark=ve)
            t_ev2[par] = t_ev
        return t_ev, t_te

    lnf_hi = SCR.alloc("lnf_hi", [128, 16 * 512], BF16)
    lnf_lo = SCR.alloc("lnf_lo", [128, 16 * 512], BF16)
    ln_tok = SCR.alloc("ln_tok", [128, D])
    wstage = SCR.alloc("wstage", [128, 2048])
    wdkv_hi = SCR.alloc("wdkv_hi", [128, 16 * 512], BF16)
    ln_tok_b = RB.alloc("ln_tok_b", [128, D])
    wdkv_lo = SCR.alloc("wdkv_lo", [128, 16 * 512], BF16)
    ckv_f = SCR.alloc("ckv_f", [128, 4 * S])
    krpre = RB.alloc("krpre", [64, S])
    cosk_t = RB.alloc("cosk_t", [128, 16 * DR])
    sink_t = RB.alloc("sink_t", [128, 16 * DR])
    wkr_hi = RB.alloc("wkr_hi", [128, 16 * DR], BF16)
    wkr_lo = RB.alloc("wkr_lo", [128, 16 * DR], BF16)

    t_cosk = None
    for i in range(16):
        chm.load(cosk_t[:, i * DR:(i + 1) * DR], cos_k[i * 128:(i + 1) * 128, :])
        t_cosk = chm.load(sink_t[:, i * DR:(i + 1) * DR],
                          sinS_k[i * 128:(i + 1) * 128, :])

    last_split = t_ident
    for wsrc, whi, wlo, cols in ((w_dkv, wdkv_hi, wdkv_lo, C),
                                 (w_ukrope, wkr_hi, wkr_lo, DR)):
        spl_hist = [last_split, last_split]
        for ct in range(16):
            stg = wstage[:, 0:cols] if ct % 2 == 0 else wstage[:, cols:2 * cols]
            tl = chw.load(stg, wsrc[ct * 128:(ct + 1) * 128, :],
                          after=(spl_hist[ct % 2],))
            last_split = split3(whi[:, ct * cols:(ct + 1) * cols],
                                wlo[:, ct * cols:(ct + 1) * cols], stg, (tl,))
            spl_hist[ct % 2] = last_split

    def proj_chunks(src, nchunks, whi, wlo, dstf, toklen, do_rope, t_w):
        t_prev_cons = None
        t_evs = None
        t_tr_hist = [None, None]
        for ch in range(nchunks):
            t_lnf = None
            for t4 in range(4):
                ti = ch * 4 + t4
                buf = ln_tok if ti % 2 == 0 else ln_tok_b
                tl = chx.load(buf[:], src[ch * 512 + t4 * 128:
                                          ch * 512 + (t4 + 1) * 128, :],
                              after=(t_tr_hist[ti % 2],)
                              if t_tr_hist[ti % 2] else ())
                b.wait("vector", tl)
                t_ln = ln_stats_apply(buf[:], buf[:], D)
                waits = [t_w, t_ln]
                if t4 == 0 and t_prev_cons is not None:
                    waits.append(t_prev_cons)
                t_lnf, t_tr_hist[ti % 2] = tr_split_tile(
                    buf[:], lnf_hi, lnf_lo, t4 * 128, 512, waits=waits)
            b.wait("tensor", t_lnf)
            for m in range(4):
                bank = PSB[m]
                mms = []
                for ct in range(16):
                    w_h = whi[:, ct * C + m * 128: ct * C + (m + 1) * 128]
                    w_l = wlo[:, ct * C + m * 128: ct * C + (m + 1) * 128]
                    a_h = lnf_hi[:, ct * 512:(ct + 1) * 512]
                    a_l = lnf_lo[:, ct * 512:(ct + 1) * 512]
                    mms += [(w_h, a_h), (w_h, a_l), (w_l, a_h)]
                t_mm = mm_group(bank[:], mms, cons=(m == 3 and not do_rope))
                b.wait("vector", t_mm)
                b.do("vector", lambda e, bank=bank, m=m, ch=ch: e.tensor_copy(
                    out=dstf[:, m * toklen + ch * 512:
                             m * toklen + (ch + 1) * 512], in_=bank[:]))
                b.drain("vector")
            if do_rope:
                mms = []
                for ct in range(16):
                    mms += [(wkr_hi[:, ct * DR:(ct + 1) * DR],
                             lnf_hi[:, ct * 512:(ct + 1) * 512]),
                            (wkr_hi[:, ct * DR:(ct + 1) * DR],
                             lnf_lo[:, ct * 512:(ct + 1) * 512]),
                            (wkr_lo[:, ct * DR:(ct + 1) * DR],
                             lnf_hi[:, ct * 512:(ct + 1) * 512])]
                t_mm = mm_group(PSB[4][0:64, :], mms, cons=True)
                b.wait("vector", t_mm)
                b.do("vector", lambda e, ch=ch: e.tensor_copy(
                    out=krpre[:, ch * 512:(ch + 1) * 512],
                    in_=PSB[4][0:64, :]))
                b.drain("vector")
            t_evs = b.do("vector", lambda e: e.engine_nop(), mark=ve)
            t_prev_cons = t_mm
            b.wait("tensor", t_evs)
        return t_evs

    t_p1 = proj_chunks(x_b, 4, wdkv_hi, wdkv_lo, ckv_f, S, True, last_split)

    def rms_split(cf, chi_, clo_, g_t, toklen, t_start):
        prev_rt = None
        last = t_start
        for ch in range(toklen // 512):
            t_mm = None
            for m in range(4):
                sl = cf[:, m * toklen + ch * 512: m * toklen + (ch + 1) * 512]
                b.wait("vector", last)
                if t_mm is not None:
                    b.wait("vector", t_mm)
                b.do("vector", lambda e, sl=sl: e.tensor_tensor(
                    out=sqf[:], in0=sl, in1=sl, op=OP.mult))
                b.drain("vector")
                b.do("vector", lambda e: e.tensor_copy(out=sqb[:], in_=sqf[:]))
                b.drain("vector")
                t_sq = b.do("vector", lambda e: e.tensor_tensor(
                    out=sqb2[:], in0=sqf[:], in1=sqb[:], op=OP.subtract),
                    mark=ve)
                wts = [t_sq]
                if prev_rt is not None and m == 0:
                    wts.append(prev_rt)
                b.waits("tensor", wts)
                b.do("tensor", lambda e, s=(m == 0): e.matmul(
                    PSB[5][:], ones_b[:], sqb[:], start=s, stop=False))
                b.drain("tensor")
                t_mm = b.do("tensor", lambda e, t=(m == 3): e.matmul(
                    PSB[5][:], ones_b[:], sqb2[:], start=False, stop=t),
                    mark=te)
            b.wait("scalar", t_mm)
            b.do("scalar", lambda e: e.activation(out=rt1[:], in_=PSB[5][:],
                                                  func=AF.Ln,
                                                  bias=eps_rms_t[:, 0:1],
                                                  scale=1.0 / C))
            b.drain("scalar")
            prev_rt = b.do("scalar", lambda e: e.activation(
                out=rt1[:], in_=rt1[:], func=AF.Exp, scale=-0.5), mark=ac)
            b.wait("vector", prev_rt)
            for m in range(4):
                sl = cf[:, m * toklen + ch * 512: m * toklen + (ch + 1) * 512]
                off = m * toklen + ch * 512
                b.do("vector", lambda e, sl=sl, m=m, g_t=g_t:
                     e.scalar_tensor_tensor(
                    out=sqf[:], in0=sl, scalar=g_t[:, m:m + 1], in1=rt1[:],
                    op0=OP.mult, op1=OP.mult))
                b.drain("vector")
                b.do("vector", lambda e, off=off, chi_=chi_: e.tensor_copy(
                    out=chi_[:, off:off + 512], in_=sqf[:]))
                b.drain("vector")
                last = b.do("vector", lambda e, off=off, chi_=chi_,
                            clo_=clo_: e.tensor_tensor(
                    out=clo_[:, off:off + 512], in0=sqf[:],
                    in1=chi_[:, off:off + 512], op=OP.subtract), mark=ve)
        return last

    t_ckv = rms_split(ckv_f, ckv_hi, ckv_lo, gkv_t, S, t_p1)
    if "ckv" in taps:
        tt = tapt("ckv_hi", [128, 4 * S])
        cht.store(tt[:, :], ckv_hi[:], after=(t_ckv,))
        tt2 = tapt("ckv_lo", [128, 4 * S])
        cht.store(tt2[:, :], ckv_lo[:], after=(t_ckv,))

    b.barrier()
    SCR.reset()
    lnf_hi = SCR.alloc("lnf_hi", [128, 16 * 512], BF16)
    lnf_lo = SCR.alloc("lnf_lo", [128, 16 * 512], BF16)
    ln_tok = SCR.alloc("ln_tok", [128, D])
    wstage = SCR.alloc("wstage", [128, 2048])
    wdq_hi = SCR.alloc("wdq_hi", [128, 16 * 512], BF16)
    wdq_lo = SCR.alloc("wdq_lo", [128, 16 * 512], BF16)
    cq_f = SCR.alloc("cq_f", [128, 4 * TQ])

    last_split = None
    spl_hist = [None, None]
    for ct in range(16):
        stg = wstage[:, 0:C] if ct % 2 == 0 else wstage[:, C:2 * C]
        tl = chw.load(stg, w_dq[ct * 128:(ct + 1) * 128, :],
                      after=(spl_hist[ct % 2],) if spl_hist[ct % 2] else ())
        last_split = split3(wdq_hi[:, ct * C:(ct + 1) * C],
                            wdq_lo[:, ct * C:(ct + 1) * C], stg, (tl,))
        spl_hist[ct % 2] = last_split

    t_p1q = proj_chunks(xq, 2, wdq_hi, wdq_lo, cq_f, TQ, False, last_split)
    t_cq = rms_split(cq_f, cq_hi, cq_lo, gq_t, TQ, t_p1q)
    if "cq" in taps:
        tt = tapt("cq_hi", [128, 4 * TQ])
        cht.store(tt[:, :], cq_hi[:], after=(t_cq,))
        tt2 = tapt("cq_lo", [128, 4 * TQ])
        cht.store(tt2[:, :], cq_lo[:], after=(t_cq,))

    def rope_tiles(pre_ap, n_tiles, cos_t, sin_t, dst_fn, waits0=()):
        last2 = [None, None]
        last = None
        for i in range(n_tiles):
            par = i % 2
            bank = PS_TR if par == 0 else PS_TR2
            ra = rop_a if par == 0 else rop_c
            rbap = bank[:, 64:128]
            wts = list(waits0)
            if last2[par] is not None:
                wts.append(last2[par])
            b.waits("tensor", wts)
            t_t1 = b.do("tensor", lambda e, i=i, pre_ap=pre_ap, bank=bank:
                        e.transpose(
                bank[:, 0:64], pre_ap[:, i * 128:(i + 1) * 128],
                ident_f[0:64, 0:64]), mark=te)
            b.wait("vector", t_t1)
            prow = list(bank[:].ap[0])
            rot = pap(bank, 1, [prow, [2, 32], [-1, 2]])
            b.do("vector", lambda e, i=i, cos_t=cos_t, ra=ra, bank=bank:
                 e.tensor_tensor(
                out=ra[:], in0=bank[:, 0:64],
                in1=cos_t[:, i * DR:(i + 1) * DR], op=OP.mult))
            b.do("vector", lambda e, i=i, rot=rot, sin_t=sin_t, rbap=rbap:
                 e.tensor_tensor(
                out=rbap.rearrange("p (a c) -> p a c", c=2), in0=rot,
                in1=sin_t[:, i * DR:(i + 1) * DR].rearrange(
                    "p (a c) -> p a c", c=2), op=OP.mult))
            b.drain("vector")
            t_rp = b.do("vector", lambda e, ra=ra, rbap=rbap: e.tensor_tensor(
                out=ra[:], in0=ra[:], in1=rbap, op=OP.add),
                mark=ve)
            b.wait("tensor", t_rp)
            t_t2 = b.do("tensor", lambda e, ra=ra, bank=bank: e.transpose(
                bank[0:64, 128:256], ra[:], ident_f[:]), mark=te)
            last = dst_fn(i, t_t2, bank)
            last2[par] = last
        return last

    def krp_dst(i, t_tr, bank):
        b.wait("vector", t_tr)
        b.do("vector", lambda e, i=i, bank=bank: e.tensor_copy(
            out=krp[0:64, i * 128:(i + 1) * 128], in_=bank[0:64, 128:256]))
        b.drain("vector")
        return b.do("vector", lambda e, i=i, bank=bank: e.tensor_tensor(
            out=krp[64:128, i * 128:(i + 1) * 128], in0=bank[0:64, 128:256],
            in1=krp[0:64, i * 128:(i + 1) * 128], op=OP.subtract), mark=ve)

    t_krp = rope_tiles(krpre, 16, cosk_t, sink_t, krp_dst,
                       waits0=(t_cq, t_cosk))
    if "krp" in taps:
        tt = tapt("krp", [128, S])
        cht.store(tt[:, :], krp[:], after=(t_krp,))

    if stage <= 2:
        if cht.c.n:
            b.wait("gpsimd", (cht.c.sem, cht.c.n))
        with nc.Block() as block:
            b.realize(block)
        return nc, tap_dr

    b.barrier()
    RB3 = Region(b, 61 * KBY, 125 * KBY)
    SCR = Region(b, 125 * KBY, SEND)
    oa_hi = RB3.alloc("oa_hi", [128, H * TQ], BF16)
    oa_lo = RB3.alloc("oa_lo", [128, H * TQ], BF16)
    qposb_t = SCR.alloc("qposb_t", [128, TQ])
    masks16 = SCR.alloc("masks16", [128, 16 * 512], BF16)
    cosq_t = SCR.alloc("cosq_t", [128, 8 * DR])
    sinq_t = SCR.alloc("sinq_t", [128, 8 * DR])
    v2_hi = SCR.alloc("v2_hi", [128, 16 * 256], BF16)
    v2_lo = SCR.alloc("v2_lo", [128, 16 * 256], BF16)
    kh_hi = SCR.alloc("kh_hi", [128, S], BF16)
    kh_lo = SCR.alloc("kh_lo", [128, S], BF16)
    qn_hi = SCR.alloc("qn_hi", [128, TQ], BF16)
    qn_lo = SCR.alloc("qn_lo", [128, TQ], BF16)
    qr_pack = SCR.alloc("qr_pack", [128, TQ], BF16)
    qr_lo = SCR.alloc("qr_lo", [64, TQ], BF16)
    p_f = [SCR.alloc(f"p_f{i}", [128, 512]) for i in range(2)]
    p_mm = SCR.alloc("p_mm", [128, 512])
    p_hi = [SCR.alloc(f"p_hi{i}", [128, 512], BF16) for i in range(2)]
    p_lo = [SCR.alloc(f"p_lo{i}", [128, 512], BF16) for i in range(2)]
    wsg_off = b.base + SCR.start + ((SCR.cur + 63) & ~63)
    wsg = SCR.alloc("wsg", [128, 1024])
    wh_a = SCR.alloc("wh_a", [128, 1024], BF16)
    wl_a = SCR.alloc("wl_a", [128, 1024], BF16)
    wsq = wsg
    wh_b = SCR.alloc("wh_b", [128, 512], BF16)
    wl_b = SCR.alloc("wl_b", [128, 512], BF16)
    wh_c = SCR.alloc("wh_c", [128, 768], BF16)
    wl_c = SCR.alloc("wl_c", [128, 768], BF16)
    wsqq = SCR.alloc("wsqq", [128, 768])
    qr_pre = nc.alloc_sbuf_tensor_at(b.uname("qr_pre"), [64, TQ], F32,
                                     offset=wsg_off)
    wsk = nc.alloc_sbuf_tensor_at(b.uname("wsk"), [128, 512], F32,
                                  offset=b.base + SCR.start)

    t_qpos = chm.load(qposb_t[:], qpos_b[:, :])
    t_cosq = None
    for i in range(8):
        chm.load(cosq_t[:, i * DR:(i + 1) * DR], cos_q[i * 128:(i + 1) * 128, :])
        t_cosq = chm.load(sinq_t[:, i * DR:(i + 1) * DR],
                          sinS_q[i * 128:(i + 1) * 128, :])
    b.wait("vector", t_qpos)
    for kb in range(16):
        qoff = 0 if kb < 8 else 512
        b.do("vector", lambda e, kb=kb, qoff=qoff: e.tensor_scalar(
            out=masks16[:, kb * 512:(kb + 1) * 512],
            in0=qposb_t[:, qoff:qoff + 512], scalar1=thr[:, kb:kb + 1],
            scalar2=None, op0=OP.is_ge))
    b.drain("vector")
    t_masks = b.do("vector", lambda e: e.engine_nop(), mark=ve)

    t_exp = [None, None]
    t_pcons = [None, None]
    t_oa_last = None
    t_oa_bank = [None, None]
    t_v_read = None
    t_hcons = None
    t_vgen_cons = None
    t_qrmm_prev = None
    t_kh3 = None
    t_rt_read = [None]

    chains = {"v": [None], "k": [None], "q": [None]}
    wtok = {}
    pend_splits = []

    def enq_V(g, dest_free, at=None):
        aft = tuple(t for t in (chains["v"][0],) + dest_free
                    if t is not None)
        tlw = chw.load(wsg[:],
                       pap(w_uv.tensor, g * 256,
                           [[D, 128], [128 * D, 4], [1, 256]]),
                       after=aft)

        def spl(tlw=tlw):
            tok = split3(wh_a[:], wl_a[:], wsg[:], (tlw,))
            chains["v"][0] = tok
            wtok["v"] = tok
        if at is None:
            spl()
        else:
            pend_splits.append((at[0], at[1], spl))

    def enq_K(g, h2, dest_free, at=None):
        aft = tuple(t for t in (chains["k"][0],) + dest_free
                    if t is not None)
        tlw = chw.load(wsk[:],
                       pap(w_uknope.tensor, (g * 2 + h2) * 128,
                           [[D, 128], [128 * D, 4], [1, 128]]),
                       after=aft)

        def spl(tlw=tlw):
            tok = split3(wh_b[:], wl_b[:], wsk[:], (tlw,))
            chains["k"][0] = tok
            wtok["k"] = tok
        if at is None:
            spl()
        else:
            pend_splits.append((at[0], at[1], spl))

    def enq_Q(g, h2, dest_free, at=None):
        aft = tuple(t for t in (chains["q"][0],) + dest_free
                    if t is not None)
        tlw = chw.load(wsqq[:],
                       pap(w_uq.tensor, (g * 2 + h2) * 192,
                           [[UQ, 128], [128 * UQ, 4], [1, 192]]),
                       after=aft)

        def spl(tlw=tlw):
            tok = split3(wh_c[:], wl_c[:], wsqq[:], (tlw,))
            chains["q"][0] = tok
            wtok["q"] = tok
        if at is None:
            spl()
        else:
            pend_splits.append((at[0], at[1], spl))

    RAW4 = Region(b, 9 * KBY, 61 * KBY)
    wstage4 = RAW4.alloc("wstage4", [128, 2048])
    wstage4b = RAW4.alloc("wstage4b", [128, 2048])
    wo_hi2 = RAW4.alloc("wo_hi2", [128, 4 * D], BF16)
    wo_lo2 = RAW4.alloc("wo_lo2", [128, 4 * D], BF16)
    tl_pre4 = [None, None]

    for grp in range(8):
        if grp == 0:
            enq_V(0, (t_ckv,))
            t_wv = wtok["v"]
        else:
            t_wv = wtok["v"]
        ev_hist = [None, None]
        for tb in range(16):
            bank = PSB[tb % 2]
            mms = []
            for ct in range(4):
                l_h = ckv_hi[:, ct * S + tb * 128: ct * S + (tb + 1) * 128]
                l_l = ckv_lo[:, ct * S + tb * 128: ct * S + (tb + 1) * 128]
                r_h = wh_a[:, ct * 256:(ct + 1) * 256]
                r_l = wl_a[:, ct * 256:(ct + 1) * 256]
                mms += [(l_h, r_h), (l_h, r_l), (l_l, r_h)]
            wts = [t_wv]
            if t_v_read is not None:
                wts.append(t_v_read)
            if ev_hist[tb % 2] is not None:
                wts.append(ev_hist[tb % 2])
            t_mm = mm_group(bank[0:128, 0:256], mms, waits=wts,
                            cons=(tb == 15))
            if tb == 15:
                t_vgen_cons = t_mm
            ev_hist[tb % 2] = evac_split(bank[0:128, 0:256],
                                 v2_hi[:, tb * 256:(tb + 1) * 256],
                                 v2_lo[:, tb * 256:(tb + 1) * 256], t_mm)
        t_v2 = ev_hist[15 % 2]

        for h2 in range(2):
            h = grp * 2 + h2
            if grp == 0 and h2 == 0:
                enq_K(0, 0, (t_cq, t_masks))
                enq_Q(0, 0, (t_cq, t_ckv))
            t_wk, t_wq = wtok["k"], wtok["q"]
            ev_kh = [None, None]
            for tc in range(4):
                bank = PSB[2 + tc % 2]
                mms = []
                for ct in range(4):
                    w_h = wh_b[:, ct * 128:(ct + 1) * 128]
                    w_l = wl_b[:, ct * 128:(ct + 1) * 128]
                    a_h = ckv_hi[:, ct * S + tc * 512: ct * S + (tc + 1) * 512]
                    a_l = ckv_lo[:, ct * S + tc * 512: ct * S + (tc + 1) * 512]
                    mms += [(w_h, a_h), (w_h, a_l), (w_l, a_h)]
                wts = [t_wk]
                if t_hcons is not None:
                    wts.append(t_hcons)
                if t_oa_bank[tc % 2] is not None:
                    wts.append(t_oa_bank[tc % 2])
                if ev_kh[tc % 2] is not None:
                    wts.append(ev_kh[tc % 2])
                t_mm = mm_group(bank[:], mms, waits=wts, cons=(tc == 3))
                ev_kh[tc % 2] = evac_split(
                    bank[:], kh_hi[:, tc * 512:(tc + 1) * 512],
                    kh_lo[:, tc * 512:(tc + 1) * 512], t_mm)
                if tc == 3:
                    t_kh3 = t_mm
            b.wait("vector", ev_kh[0])
            t_kh = ev_kh[1]

            ev_q = None
            for qc in range(2):
                bank = PSB[2 + qc]
                mms = []
                for ct in range(4):
                    w_h = wh_c[:, ct * 192: ct * 192 + 128]
                    w_l = wl_c[:, ct * 192: ct * 192 + 128]
                    a_h = cq_hi[:, ct * TQ + qc * 512: ct * TQ + (qc + 1) * 512]
                    a_l = cq_lo[:, ct * TQ + qc * 512: ct * TQ + (qc + 1) * 512]
                    mms += [(w_h, a_h), (w_h, a_l), (w_l, a_h)]
                wts = [t_wq, t_kh]
                if t_hcons is not None:
                    wts.append(t_hcons)
                if t_oa_bank[qc] is not None:
                    wts.append(t_oa_bank[qc])
                if ev_q is not None:
                    wts.append(ev_q)
                t_mm = mm_group(bank[:], mms, waits=wts, cons=(qc == 1))
                ev_q = evac_split(bank[:], qn_hi[:, qc * 512:(qc + 1) * 512],
                                  qn_lo[:, qc * 512:(qc + 1) * 512], t_mm)
            t_qr = None
            for qc in range(2):
                mms = []
                for ct in range(4):
                    w_h = wh_c[:, ct * 192 + 128: ct * 192 + 192]
                    w_l = wl_c[:, ct * 192 + 128: ct * 192 + 192]
                    a_h = cq_hi[:, ct * TQ + qc * 512: ct * TQ + (qc + 1) * 512]
                    a_l = cq_lo[:, ct * TQ + qc * 512: ct * TQ + (qc + 1) * 512]
                    mms += [(w_h, a_h), (w_h, a_l), (w_l, a_h)]
                wts = [ev_q]
                if t_hcons is not None:
                    wts.append(t_hcons)
                if t_qr is not None:
                    wts.append(t_qr)
                t_mm = mm_group(PSB[4][0:64, :], mms, waits=wts,
                                cons=(qc == 1))
                if qc == 1:
                    t_qrmm_prev = t_mm
                b.wait("vector", t_mm)
                b.do("vector", lambda e, qc=qc: e.tensor_copy(
                    out=qr_pre[:, qc * 512:(qc + 1) * 512],
                    in_=PSB[4][0:64, :]))
                b.drain("vector")
                t_qr = b.do("vector", lambda e: e.engine_nop(), mark=ve)

            def qr_dst(i, t_tr, bank):
                b.wait("vector", t_tr)
                b.do("vector", lambda e, i=i, bank=bank: e.tensor_copy(
                    out=qr_pack[0:64, i * 128:(i + 1) * 128],
                    in_=bank[0:64, 128:256]))
                b.drain("vector")
                return b.do("vector", lambda e, i=i, bank=bank:
                            e.tensor_tensor(
                    out=qr_lo[:, i * 128:(i + 1) * 128],
                    in0=bank[0:64, 128:256],
                    in1=qr_pack[0:64, i * 128:(i + 1) * 128],
                    op=OP.subtract), mark=ve)

            t_qrope = rope_tiles(qr_pre, 8, cosq_t, sinq_t, qr_dst,
                                 waits0=(t_qr, t_cosq, t_hcons) if t_hcons
                                 else (t_qr, t_cosq))
            b.wait("vector", t_qrope)
            t_qrope = b.do("vector", lambda e: e.tensor_copy(
                out=qr_pack[64:128, :], in_=qr_pack[0:64, :]), mark=ve)

            if h2 == 0:
                enq_K(grp, 1, (t_kh3,), at=(0, 4))
                enq_Q(grp, 1, (t_qrmm_prev,), at=(1, 2))
            elif grp < 7:
                enq_V(grp + 1, (t_vgen_cons, t_qrope), at=(0, 4))
                enq_K(grp + 1, 0, (t_kh3,), at=(1, 2))
                enq_Q(grp + 1, 0, (t_qrmm_prev,), at=(1, 8))
            else:
                tl_pre4[0] = chw.load(wstage4[:], w_o[0:128, :],
                                      after=(t_kh3,))
                tl_pre4[1] = chw.load(wstage4b[:], w_o[128:256, :],
                                      after=(t_kh3,))

            for cidx, nkb in ((0, KB_LO), (1, KB_HI)):
                qoff = cidx * 512
                out_bank = PSB[2 + cidx]
                den_bank = PSB[5]
                t_av = None
                pend = None

                def emit_av(kb, par, t_p, nkb=nkb, out_bank=out_bank,
                            den_bank=den_bank, h2=h2):
                    b.wait("tensor", t_p)
                    if kb == 0 and t_rt_read[0] is not None:
                        b.wait("tensor", t_rt_read[0])
                    first = (kb == 0)
                    last_kb = (kb == nkb - 1)
                    vof = kb * 256 + h2 * 128
                    b.do("tensor", lambda e, par=par, first=first,
                         db=den_bank: e.matmul(
                        db[:], ones_b[:], p_hi[par][:], start=first,
                        stop=False))
                    b.do("tensor", lambda e, par=par, last=last_kb,
                         db=den_bank: e.matmul(
                        db[:], ones_b[:], p_lo[par][:], start=False,
                        stop=last))
                    b.do("tensor", lambda e, par=par, first=first, vof=vof,
                         ob=out_bank: e.matmul(
                        ob[:], v2_hi[:, vof:vof + 128],
                        p_hi[par][:], start=first, stop=False))
                    b.do("tensor", lambda e, par=par, vof=vof,
                         ob=out_bank: e.matmul(
                        ob[:], v2_hi[:, vof:vof + 128], p_lo[par][:],
                        start=False, stop=False))
                    b.drain("tensor")
                    return b.do("tensor", lambda e, par=par, vof=vof,
                                last=last_kb, ob=out_bank: e.matmul(
                        ob[:], v2_lo[:, vof:vof + 128], p_hi[par][:],
                        start=False, stop=last), mark=te)

                for kb in range(nkb):
                    par = kb % 2
                    sbank = PSB[par]
                    wts = [t_qrope, t_v2, t_kh, t_masks]
                    if t_exp[par] is not None:
                        wts.append(t_exp[par])
                    b.waits("tensor", wts)
                    b.do("tensor", lambda e, kb=kb, qoff=qoff, sbank=sbank:
                         e.matmul(sbank[:],
                                  kh_hi[:, kb * 128:(kb + 1) * 128],
                                  qn_hi[:, qoff:qoff + 512], start=True,
                                  stop=False))
                    b.do("tensor", lambda e, kb=kb, qoff=qoff, sbank=sbank:
                         e.matmul(sbank[:],
                                  kh_hi[:, kb * 128:(kb + 1) * 128],
                                  qn_lo[:, qoff:qoff + 512], start=False,
                                  stop=False))
                    b.do("tensor", lambda e, kb=kb, qoff=qoff, sbank=sbank:
                         e.matmul(sbank[:],
                                  kh_lo[:, kb * 128:(kb + 1) * 128],
                                  qn_hi[:, qoff:qoff + 512], start=False,
                                  stop=False))
                    b.do("tensor", lambda e, kb=kb, qoff=qoff, sbank=sbank:
                         e.matmul(sbank[:], krp[:, kb * 128:(kb + 1) * 128],
                                  qr_pack[:, qoff:qoff + 512], start=False,
                                  stop=False))
                    if cidx == 1 and kb == nkb - 1:
                        b.drain("tensor")
                    t_s = b.do("tensor", lambda e, kb=kb, qoff=qoff,
                               sbank=sbank: e.matmul(
                        sbank[:], krp[0:64, kb * 128:(kb + 1) * 128],
                        qr_lo[:, qoff:qoff + 512], start=False, stop=True),
                        mark=te)
                    if cidx == 1 and kb == nkb - 1:
                        t_hcons = t_s
                    masked = (cidx == 0) or (kb >= 8)
                    b.wait("scalar", t_s)
                    if masked:
                        t_e = b.do("scalar", lambda e, sbank=sbank, par=par:
                                   e.activation(out=p_f[par][:],
                                                in_=sbank[:],
                                                func=AF.Exp, scale=SCALE),
                                   mark=ac)
                        t_exp[par] = t_e
                        b.wait("vector", t_e)
                        if t_pcons[par] is not None:
                            b.wait("vector", t_pcons[par])
                        b.do("vector", lambda e, par=par, kb=kb:
                             e.scalar_tensor_tensor(
                                 out=p_mm[:], in0=p_f[par][:], scalar=1.0,
                                 in1=masks16[:, kb * 512:(kb + 1) * 512],
                                 op0=OP.mult, op1=OP.mult))
                        b.drain("vector")
                        b.do("vector", lambda e, par=par:
                             e.tensor_copy(out=p_hi[par][:], in_=p_mm[:]))
                        b.drain("vector")
                        t_p = b.do("vector", lambda e, par=par:
                                   e.tensor_tensor(out=p_lo[par][:],
                                                   in0=p_mm[:],
                                                   in1=p_hi[par][:],
                                                   op=OP.subtract), mark=ve)
                    else:
                        b.do("scalar", lambda e, sbank=sbank, par=par:
                             e.activation(out=p_f[par][:], in_=sbank[:],
                                          func=AF.Exp, scale=SCALE))
                        if t_pcons[par] is not None:
                            b.wait("scalar", t_pcons[par])
                        b.drain("scalar")
                        t_e = b.do("scalar", lambda e, sbank=sbank, par=par:
                                   e.activation(out=p_hi[par][:],
                                                in_=sbank[:],
                                                func=AF.Exp, scale=SCALE),
                                   mark=ac)
                        t_exp[par] = t_e
                        b.wait("vector", t_e)
                        if t_pcons[par] is not None:
                            b.wait("vector", t_pcons[par])
                        t_p = b.do("vector", lambda e, par=par:
                                   e.tensor_tensor(out=p_lo[par][:],
                                                   in0=p_f[par][:],
                                                   in1=p_hi[par][:],
                                                   op=OP.subtract), mark=ve)
                    for ps in [p for p in pend_splits
                               if p[0] == cidx and p[1] == kb]:
                        ps[2]()
                        pend_splits.remove(ps)
                    if pend is not None:
                        t_av = emit_av(*pend)
                        t_pcons[pend[1]] = t_av
                    pend = (kb, par, t_p)
                t_av = emit_av(*pend)
                t_pcons[pend[1]] = t_av
                b.wait("scalar", t_av)
                b.do("scalar", lambda e, den_bank=den_bank: e.activation(
                    out=rt1[:], in_=den_bank[:], func=AF.Ln, scale=1.0))
                b.drain("scalar")
                t_rt = b.do("scalar", lambda e: e.activation(
                    out=rt1[:], in_=rt1[:], func=AF.Exp, scale=-1.0),
                    mark=ac)
                t_rt_read[0] = t_rt
                b.wait("vector", t_av)
                b.wait("vector", t_rt)
                oa_off = h * TQ + qoff
                b.do("vector", lambda e, out_bank=out_bank: e.tensor_tensor(
                    out=sqf[:], in0=out_bank[:], in1=rt1[:], op=OP.mult))
                b.drain("vector")
                b.do("vector", lambda e, oa_off=oa_off: e.tensor_copy(
                    out=oa_hi[:, oa_off:oa_off + 512], in_=sqf[:]))
                b.drain("vector")
                t_oa = b.do("vector", lambda e, oa_off=oa_off:
                            e.tensor_tensor(
                    out=oa_lo[:, oa_off:oa_off + 512], in0=sqf[:],
                    in1=oa_hi[:, oa_off:oa_off + 512], op=OP.subtract),
                    mark=ve)
                t_oa_last = t_oa
                t_oa_bank[cidx] = t_oa
        t_v_read = t_av

    if "oa" in taps:
        tt = tapt("oa_hi", [128, H * TQ])
        cht.store(tt[:, :], oa_hi[:], after=(t_oa_last,))
        tt2 = tapt("oa_lo", [128, H * TQ])
        cht.store(tt2[:, :], oa_lo[:], after=(t_oa_last,))
    if "attdbg" in taps:
        for nm, buf, sh in (("kh_hi", kh_hi, [128, S]), ("kh_lo", kh_lo, [128, S]),
                            ("qn_hi", qn_hi, [128, TQ]), ("qn_lo", qn_lo, [128, TQ]),
                            ("qr_pack", qr_pack, [128, TQ]), ("qr_lo", qr_lo, [64, TQ]),
                            ("v2_hi", v2_hi, [128, 16 * 256]),
                            ("v2_lo", v2_lo, [128, 16 * 256]),
                            ("krp", krp, [128, S]),
                            ("masks", masks16, [128, 16 * 512]),
                            ("p_f", p_f[1], [128, 512]),
                            ("p_hi", p_hi[1], [128, 512]),
                            ("p_lo", p_lo[1], [128, 512]),
                            ("rt1", rt1, [128, 512]),
                            ("sqf", sqf, [128, 512])):
            tt = tapt(nm, sh)
            cht.store(tt[tuple(slice(None) for _ in sh)], buf[:],
                      after=(t_oa_last,))

    if stage <= 3:
        if cht.c.n:
            b.wait("gpsimd", (cht.c.sem, cht.c.n))
        with nc.Block() as block:
            b.realize(block)
        return nc, tap_dr

    b.barrier()
    SCR.reset()
    acc_tok = SCR.alloc("acc_tok", [128, 4 * D])
    wo_hi = SCR.alloc("wo_hi", [128, 4 * D], BF16)
    wo_lo = SCR.alloc("wo_lo", [128, 4 * D], BF16)
    wo_pairs = [(wo_hi, wo_lo), (wo_hi2, wo_lo2)]
    stg4 = [wstage4, wstage4b]

    t_acc_last = None
    t_accst = None
    spl_stg = [None, None]
    t_wo_cons2 = [None, None]
    spl_cur = [None] * 4
    tl_next = [None] * 4
    pend4 = []

    def wo_load(gidx, r4, aft):
        return chw.load(stg4[r4 % 2][:],
                        w_o[(gidx % 4) * 512 + r4 * 128:
                            (gidx % 4) * 512 + (r4 + 1) * 128, :],
                        after=tuple(t for t in aft if t is not None))

    def wo_split(gidx, r4, tl):
        hi, lo = wo_pairs[gidx % 2]
        tok = split3(hi[:, r4 * D:(r4 + 1) * D],
                     lo[:, r4 * D:(r4 + 1) * D], stg4[r4 % 2][:],
                     tuple(t for t in (tl, t_wo_cons2[gidx % 2])
                           if t is not None))
        spl_stg[r4 % 2] = tok
        return tok

    for gidx in range(8):
        qhalf, grp4 = gidx // 4, gidx % 4
        if gidx == 0:
            spl_cur[0] = wo_split(0, 0, tl_pre4[0])
            tl = wo_load(0, 2, (spl_stg[0],))
            spl_cur[1] = wo_split(0, 1, tl_pre4[1])
            tl3 = wo_load(0, 3, (spl_stg[1],))
            spl_cur[2] = wo_split(0, 2, tl)
            spl_cur[3] = wo_split(0, 3, tl3)
        if gidx < 7:
            tl_next[0] = wo_load(gidx + 1, 0, (spl_stg[0],))
            tl_next[1] = wo_load(gidx + 1, 1, (spl_stg[1],))

            def mk_early(g, r, tl):
                def fn():
                    tok = wo_split(g, r, tl)
                    tl_next[r + 2] = wo_load(g, r + 2, (tok,))
                    return tok
                return fn

            def mk_late(g, r):
                def fn():
                    return wo_split(g, r, tl_next[r])
                return fn

            pend4 = [(5, 0, mk_early(gidx + 1, 0, tl_next[0])),
                     (8, 1, mk_early(gidx + 1, 1, tl_next[1])),
                     (11, 2, mk_late(gidx + 1, 2)),
                     (14, 3, mk_late(gidx + 1, 3))]
        else:
            pend4 = []
        nxt = [None] * 4
        ev_prev = None
        for qt4 in range(4):
            qt = qhalf * 4 + qt4
            for nch in range(4):
                j = qt4 * 4 + nch
                bank = PSB[j % 2]
                hi, lo = wo_pairs[gidx % 2]
                mms = []
                pre = {}
                for ct4 in range(4):
                    hh = grp4 * 4 + ct4
                    pre[len(mms)] = (spl_cur[ct4],)
                    l_h = oa_hi[:, hh * TQ + qt * 128:
                                hh * TQ + (qt + 1) * 128]
                    l_l = oa_lo[:, hh * TQ + qt * 128:
                                hh * TQ + (qt + 1) * 128]
                    r_h = hi[:, ct4 * D + nch * 512:
                             ct4 * D + (nch + 1) * 512]
                    r_l = lo[:, ct4 * D + nch * 512:
                             ct4 * D + (nch + 1) * 512]
                    mms += [(l_h, r_h), (l_h, r_l), (l_l, r_h)]
                wts = []
                if ev_prev is not None:
                    wts.append(ev_prev)
                t_mm = mm_group(bank[:], mms, waits=wts, pre=pre,
                                cons=(j == 15))
                if j == 15:
                    t_wo_cons2[gidx % 2] = t_mm
                b.wait("vector", t_mm)
                if grp4 == 0 and t_accst is not None:
                    b.wait("vector", t_accst)
                off = qt4 * D + nch * 512
                if grp4 == 0:
                    ev_prev = b.do("vector", lambda e, bank=bank, off=off:
                                   e.tensor_copy(
                        out=acc_tok[:, off:off + 512], in_=bank[:]),
                        mark=ve)
                else:
                    ev_prev = b.do("vector", lambda e, bank=bank, off=off:
                                   e.tensor_tensor(
                        out=acc_tok[:, off:off + 512], in0=bank[:],
                        in1=acc_tok[:, off:off + 512], op=OP.add),
                        mark=ve)
                for at, idx, fn in [p for p in pend4 if p[0] == j]:
                    nxt[idx] = fn()
                    pend4.remove((at, idx, fn))
        spl_cur = nxt
        t_acc_last = ev_prev
        if grp4 == 3:
            for qt4 in range(4):
                qt = qhalf * 4 + qt4
                t_accst = chs.store(acc_scr[qt * 128:(qt + 1) * 128, :],
                                    acc_tok[:, qt4 * D:(qt4 + 1) * D],
                                    after=(t_acc_last,))
    if "acc" in taps:
        tt = tapt("acc", [128, 4 * D])
        cht.store(tt[:, :], acc_tok[:], after=(t_accst,))

    if stage <= 4:
        b.wait("scalar", (chs.c.sem, chs.c.n))
        if cht.c.n:
            b.wait("gpsimd", (cht.c.sem, cht.c.n))
        with nc.Block() as block:
            b.realize(block)
        return nc, tap_dr

    b.barrier()
    BIG.cur = 0
    l2f_hi = BIG.alloc("l2f_hi", [128, 16 * 512], BF16)
    l2f_lo = BIG.alloc("l2f_lo", [128, 16 * 512], BF16)
    hdn_hi = BIG.alloc("hdn_hi", [128, 22 * 512], BF16)
    hdn_lo = BIG.alloc("hdn_lo", [128, 22 * 512], BF16)
    dwT_blk = BIG.alloc("dwT_blk", [128, 22 * 256], BF16)
    dw_nat = BIG.alloc("dw_nat", [128, FI], BF16)
    fw_nat2 = [BIG.alloc(f"fw_nat{i}", [128, 2 * D], BF16) for i in range(2)]
    fwT2 = [BIG.alloc(f"fwT{i}", [128, D], BF16) for i in range(2)]
    dn_buf = BIG.alloc("dn_buf", [128, 4 * D])
    tmp_a = BIG.alloc("tmp_a", [128, 2048])
    tmp_b = BIG.alloc("tmp_b", [128, 2048])
    tmp_c = BIG.alloc("tmp_c", [128, 2048])
    xq_tile = BIG.alloc("xq_tile", [128, D])
    dsb_t = BIG.alloc("dsb_t", [128, D])
    gates_t = BIG.alloc("gates_t", [128, 64])
    upss_t = BIG.alloc("upss_t", [128, 64])

    chm.load(dsb_t[:], ds_b[:, :])
    chm.load(gates_t[:, 0:44], gate_s_t[:, :])
    t_meta5 = chm.load(upss_t[:, 0:44], up_s_t[:, :])
    b.wait("vector", t_meta5)

    t_x1st = [None] * 8
    t_l2f_used = None
    t_prev_store = [None]
    t_tmpa_read = [None]
    t_l2f_cell = [None]
    t_trguard = [None]
    pend_tr = [None]

    def emit_qt_ln(qt, tr_now=True):
        wts = []
        if t_prev_store[0] is not None:
            wts.append(t_prev_store[0])
        t_la = chs.load(tmp_a[:], acc_scr[qt * 128:(qt + 1) * 128, :],
                        after=(t_tmpa_read[0],) if t_tmpa_read[0] else ())
        tl = chx.load(xq_tile[:], xq[qt * 128:(qt + 1) * 128, :], after=wts)
        b.wait("vector", tl)
        b.wait("vector", t_la)
        t_x1 = ln_stats_apply(tmp_a[:], tmp_b[:], D, res_ap=xq_tile[:])
        t_tmpa_read[0] = t_x1
        t_st = chs.store(x1_scr[qt * 128:(qt + 1) * 128, :], tmp_b[:],
                         after=(t_x1,))
        t_x1st[qt] = t_st
        t_prev_store[0] = t_st
        t_ln2 = ln_stats_apply(tmp_b[:], tmp_c[:], D)

        def do_tr():
            wts = [t_ln2]
            if t_l2f_used is not None and qt % 4 == 0:
                wts.append(t_l2f_used)
            t_e, _ = tr_split_tile(tmp_c[:], l2f_hi, l2f_lo,
                                   (qt % 4) * 128, 512, waits=wts)
            t_l2f_cell[0] = t_e
            t_trguard[0] = t_e
        if tr_now:
            do_tr()
            return None
        return do_tr

    t_mm_hist = [None, None]
    t_scr_hist = [None, None]
    t_fw_hist = [None, None]
    t_prev_evac = [None]

    for sub in range(2):
        if sub == 0:
            for qt in range(4):
                emit_qt_ln(qt)
        t_l2f_e = t_l2f_cell[0]
        t_dn = [None] * 4
        for half in range(2):
            t_hdn_last = None
            t_gu_read = None
            b.wait("tensor", t_l2f_e)
            units = []
            for m22 in range(22):
                m = half * 22 + m22
                units.append((m, m22, "g", gate_w, gwT_scr))
                units.append((m, m22, "u", up_w, uwT_scr))
            t_wT_tok = [None] * len(units)

            def stage_unit(i):
                m, m22, wname, wsrc_d, scr = units[i]
                pb = i % 2
                if sub == 0:
                    awts = tuple(t for t in (t_fw_hist[pb],) if t)
                    tg = chw.load(fw_nat2[pb][:],
                                  wsrc_d[m * 128:(m + 1) * 128, :],
                                  after=awts)
                    prow_fw = list(fw_nat2[pb][:].ap[0])

                    def fw_hi(dct, pb=pb, prow=prow_fw):
                        return pap(fw_nat2[pb], 2 * dct * 128 + 1,
                                   [prow, [2, 128]])
                    t_wT = None
                    for hf in range(2):
                        b.wait("tensor", tg)
                        if t_wT is not None:
                            b.wait("tensor", t_wT)
                        elif t_prev_evac[0] is not None:
                            b.wait("tensor", t_prev_evac[0])
                        if t_mm_hist[pb] is not None:
                            b.wait("tensor", t_mm_hist[pb])
                        for j in range(7):
                            dct = hf * 8 + j
                            b.do("tensor", lambda e, j=j, dct=dct:
                                 e.transpose(
                                PSBF[:, j * 128:(j + 1) * 128],
                                fw_hi(dct), ident_b[:]))
                        b.drain("tensor")
                        t_tr = b.do("tensor", lambda e, dct=hf * 8 + 7:
                                    e.transpose(
                            PSBF[:, 7 * 128:8 * 128],
                            fw_hi(dct), ident_b[:]), mark=te)
                        t_fw_hist[pb] = t_tr
                        b.wait("vector", t_tr)
                        if t_scr_hist[pb] is not None:
                            b.wait("vector", t_scr_hist[pb])
                        if t_mm_hist[pb] is not None:
                            b.wait("vector", t_mm_hist[pb])
                        t_wT = b.do("vector", lambda e, hf=hf, pb=pb:
                                    e.tensor_copy(
                            out=fwT2[pb][:, hf * 1024:(hf + 1) * 1024],
                            in_=PSBF[:]), mark=ve)
                    t_scr_hist[pb] = chs.store(
                        scr[m * 128:(m + 1) * 128, :], fwT2[pb][:],
                        after=(t_wT,))
                    t_prev_evac[0] = t_wT
                    return t_wT
                else:
                    awts = tuple(t for t in (t_mm_hist[pb],) if t)
                    return chm.load(fwT2[pb][:],
                                    scr[m * 128:(m + 1) * 128, :],
                                    after=awts)

            t_wT_tok[0] = stage_unit(0)
            for i, (m, m22, wname, wsrc_d, scr) in enumerate(units):
                pb = i % 2
                if i + 1 < len(units):
                    t_wT_tok[i + 1] = stage_unit(i + 1)
                bank = PSB[0] if wname == "g" else PSB[1]
                mms = []
                for ct in range(16):
                    wt_sl = fwT2[pb][:, ct * 128:(ct + 1) * 128]
                    mms += [(wt_sl, l2f_hi[:, ct * 512:(ct + 1) * 512]),
                            (wt_sl, l2f_lo[:, ct * 512:(ct + 1) * 512])]
                wts2 = [t_wT_tok[i]]
                if t_hdn_last is not None:
                    wts2.append(t_hdn_last)
                t_mmg = mm_group(bank[:], mms, waits=wts2, cons=True)
                t_mm_hist[pb] = t_mmg
                if wname == "g":
                    t_g = t_mmg
                    continue
                t_u = t_mmg
                t_gu_read = t_u
                b.wait("vector", t_g)
                b.do("vector", lambda e, m=m: e.tensor_scalar(
                    out=tmp_c[:, 0:512], in0=PSB[0][:],
                    scalar1=gates_t[:, m:m + 1], scalar2=None, op0=OP.mult))
                b.drain("vector")
                t_t1 = b.do("vector", lambda e: e.engine_nop(), mark=ve)
                b.wait("scalar", t_t1)
                t_sg = b.do("scalar", lambda e: e.activation(
                    out=tmp_c[:, 512:1024], in_=tmp_c[:, 0:512],
                    func=AF.Sigmoid, scale=1.0), mark=ac)
                b.wait("vector", t_sg)
                b.wait("vector", t_u)
                b.do("vector", lambda e, m=m: e.scalar_tensor_tensor(
                    out=tmp_c[:, 1024:1536], in0=PSB[1][:],
                    scalar=upss_t[:, m:m + 1], in1=tmp_c[:, 512:1024],
                    op0=OP.mult, op1=OP.mult))
                b.drain("vector")
                b.do("vector", lambda e: e.tensor_tensor(
                    out=tmp_c[:, 0:512], in0=tmp_c[:, 0:512],
                    in1=tmp_c[:, 1024:1536], op=OP.mult))
                b.drain("vector")
                b.do("vector", lambda e, m22=m22: e.tensor_copy(
                    out=hdn_hi[:, m22 * 512:(m22 + 1) * 512],
                    in_=tmp_c[:, 0:512]))
                b.drain("vector")
                t_hdn_last = b.do("vector", lambda e, m22=m22:
                                  e.tensor_tensor(
                    out=hdn_lo[:, m22 * 512:(m22 + 1) * 512],
                    in0=tmp_c[:, 0:512],
                    in1=hdn_hi[:, m22 * 512:(m22 + 1) * 512],
                    op=OP.subtract), mark=ve)
            if half == 1:
                t_l2f_used = t_gu_read

            b.wait("vector", (cho.c.sem, cho.c.n) if cho.c.n else None)
            t_blk_read = None
            t_dw_rnd = [None] * 3
            t_dscr = None
            for nch in range(8):
                if sub == 0 and half == 1 and nch % 2 == 1:
                    if pend_tr[0] is not None:
                        pend_tr[0]()
                    pend_tr[0] = emit_qt_ln(4 + nch // 2, tr_now=False)
                scr_off = (half * 8 + nch) * 22 * 128
                scr_view = pap(dwT_scr.tensor, scr_off * 256,
                               [[256, 128], [128 * 256, 22], [1, 256]])
                blk_view = dwT_blk[:].rearrange("p (c f) -> p c f", f=256)
                if sub == 0:
                    t_blk = None
                    for j2 in range(2):
                        hrow = nch * 256 + j2 * 128
                        tdw_r = [None] * 3
                        for r in range(3):
                            c0, c1 = 2048 * r, min(FI, 2048 * (r + 1))
                            awts = tuple(t for t in (t_dw_rnd[r],) if t)
                            tdw_r[r] = chw.load(
                                dw_nat[:, c0:c1],
                                down_w[hrow:hrow + 128,
                                       half * FI + c0:half * FI + c1],
                                after=awts)
                        prow_dw = list(dw_nat[:].ap[0])

                        def dw_hi(ct, prow=prow_dw):
                            return pap(dw_nat, 2 * ct * 128 + 1,
                                       [prow, [2, 128]])
                        for rnd in range(3):
                            jmax = min(8, 22 - rnd * 8)
                            b.wait("tensor", tdw_r[rnd])
                            if t_trguard[0] is not None:
                                b.wait("tensor", t_trguard[0])
                                t_trguard[0] = None
                            if t_blk is not None:
                                b.wait("tensor", t_blk)
                            for j in range(jmax - 1):
                                ct = rnd * 8 + j
                                b.do("tensor", lambda e, j=j, ct=ct:
                                     e.transpose(
                                    PSBF[:, j * 128:(j + 1) * 128],
                                    dw_hi(ct), ident_b[:]))
                            b.drain("tensor")
                            t_tr = b.do("tensor", lambda e, j=jmax - 1,
                                        ct=rnd * 8 + jmax - 1: e.transpose(
                                PSBF[:, j * 128:(j + 1) * 128],
                                dw_hi(ct), ident_b[:]), mark=te)
                            t_dw_rnd[rnd] = t_tr
                            b.wait("vector", t_tr)
                            if t_dscr is not None:
                                b.wait("vector", t_dscr)
                            prow = list(dwT_blk[:].ap[0])
                            dview = pap(dwT_blk,
                                        (rnd * 8) * 256 + j2 * 128,
                                        [prow, [256, jmax], [1, 128]])
                            pview = PSBF[:, 0:jmax * 128].rearrange(
                                "p (a f) -> p a f", a=jmax)
                            t_blk = b.do("vector", lambda e, dv=dview,
                                         pv=pview: e.tensor_copy(
                                out=dv, in_=pv), mark=ve)
                    t_dscr = chs.store(scr_view, blk_view, after=(t_blk,))
                    t_blkr = t_blk
                else:
                    awts = tuple(t for t in (t_blk_read,) if t)
                    t_blkr = chs.load(blk_view, scr_view, after=awts)
                for qt4 in range(4):
                    bank = PSB[2 + qt4 % 2]
                    mms = []
                    for ct in range(22):
                        l_h = hdn_hi[:, ct * 512 + qt4 * 128:
                                     ct * 512 + (qt4 + 1) * 128]
                        l_l = hdn_lo[:, ct * 512 + qt4 * 128:
                                     ct * 512 + (qt4 + 1) * 128]
                        r = dwT_blk[:, ct * 256:(ct + 1) * 256]
                        mms += [(l_h, r), (l_l, r)]
                    wts3 = [t_blkr, t_hdn_last]
                    if t_dn[qt4] is not None:
                        wts3.append(t_dn[qt4])
                    if t_dn[(qt4 + 2) % 4] is not None:
                        wts3.append(t_dn[(qt4 + 2) % 4])
                    t_mm = mm_group(bank[0:128, 0:256], mms, waits=wts3,
                                    cons=(qt4 == 3))
                    b.wait("vector", t_mm)
                    off = qt4 * D + nch * 256
                    if half == 0:
                        t_dn[qt4] = b.do("vector", lambda e, bank=bank,
                                         off=off: e.tensor_copy(
                            out=dn_buf[:, off:off + 256],
                            in_=bank[0:128, 0:256]), mark=ve)
                    else:
                        t_dn[qt4] = b.do("vector", lambda e, bank=bank,
                                         off=off: e.tensor_tensor(
                            out=dn_buf[:, off:off + 256],
                            in0=bank[0:128, 0:256],
                            in1=dn_buf[:, off:off + 256], op=OP.add),
                            mark=ve)
                t_blk_read = t_mm

        if pend_tr[0] is not None:
            pend_tr[0]()
            pend_tr[0] = None

        for qt4 in range(4):
            qtg = sub * 4 + qt4
            b.wait("vector", t_dn[qt4])
            tx1 = chx.load(tmp_a[:], x1_scr[qtg * 128:(qtg + 1) * 128, :],
                           after=(t_x1st[qtg], t_tmpa_read[0]))
            b.wait("vector", tx1)
            dn_sl = dn_buf[:, qt4 * D:(qt4 + 1) * D]
            b.do("vector", lambda e, dn_sl=dn_sl: e.tensor_tensor(
                out=dn_sl, in0=dn_sl, in1=dsb_t[:], op=OP.mult))
            b.drain("vector")
            t_out = ln_stats_apply(dn_sl, dn_sl, D, res_ap=tmp_a[:])
            t_tmpa_read[0] = t_out
            t_prev_store[0] = cho.store(out_q[qtg * 128:(qtg + 1) * 128, :],
                                        dn_sl, after=(t_out,))

    b.wait("sync", (cho.c.sem, cho.c.n))
    if cht.c.n:
        b.wait("gpsimd", (cht.c.sem, cht.c.n))
    b.wait("scalar", (chs.c.sem, chs.c.n))

    with nc.Block() as block:
        b.realize(block)
    return nc, tap_dr


def _rope_tables_f32():
    t = (np.arange(0, DR, 2).astype(np.float32) / np.float32(DR))
    freqs = (np.float32(1.0) / (np.float32(10000.0) ** t)).astype(np.float32)
    ang = (np.arange(S, dtype=np.float32)[:, None] * freqs).astype(np.float32)
    ang = np.repeat(ang, 2, axis=-1)
    cos = np.cos(ang).astype(np.float32)
    sin = np.sin(ang).astype(np.float32)
    sinS = sin.copy()
    sinS[:, 0::2] *= np.float32(-1.0)
    return cos, sinS


def make_in_maps(inputs):
    f32 = lambda k: np.ascontiguousarray(np.asarray(inputs[k], dtype=np.float32))
    x = f32("L0_attn_x")
    cos, sinS = _rope_tables_f32()
    base = {
        "w_dq": f32("L0_attn_w_dq"), "w_uq": f32("L0_attn_w_uq"),
        "w_dkv": f32("L0_attn_w_dkv"), "w_ukrope": f32("L0_attn_w_ukrope"),
        "w_uv": f32("L0_attn_w_uv"), "w_uknope": f32("L0_attn_w_uknope"),
        "w_o": f32("L0_attn_w_o"),
        "gate_w": np.ascontiguousarray(f32("L0_ffn_gate_w")).view(
            ml_dtypes.bfloat16),
        "up_w": np.ascontiguousarray(f32("L0_ffn_up_w")).view(
            ml_dtypes.bfloat16),
        "down_w": np.ascontiguousarray(f32("L0_ffn_down_w")).view(
            ml_dtypes.bfloat16),
        "g_q_t": np.ascontiguousarray(f32("L0_attn_g_q").reshape(4, 128).T),
        "g_kv_t": np.ascontiguousarray(f32("L0_attn_g_kv").reshape(4, 128).T),
        "gate_s_t": np.ascontiguousarray(
            f32("L0_ffn_gate_s").reshape(44, 128).T),
        "up_s_t": np.ascontiguousarray(f32("L0_ffn_up_s").reshape(44, 128).T),
        "ds_b": np.ascontiguousarray(
            np.broadcast_to(f32("L0_ffn_down_s"), (128, D))),
        "cos_k": cos, "sinS_k": sinS,
    }
    in_maps, own_rows = [], []
    for cid in range(8):
        bb, par = cid // 2, cid % 2
        own = np.r_[par * 512:(par + 1) * 512, (3 - par) * 512:(4 - par) * 512]
        own_rows.append((bb, own))
        m = dict(base)
        m["x_b"] = np.ascontiguousarray(x[bb])
        m["xq"] = np.ascontiguousarray(x[bb][own])
        m["qpos_b"] = np.ascontiguousarray(
            np.broadcast_to(own.astype(np.float32), (128, TQ)))
        m["cos_q"] = np.ascontiguousarray(cos[own])
        m["sinS_q"] = np.ascontiguousarray(sinS[own])
        in_maps.append(m)
    return in_maps, own_rows


_PROG = {}


def kernel(**inputs):
    if "full" not in _PROG:
        _PROG["full"] = build_program(stage=9)
    nc, _ = _PROG["full"]
    in_maps, own_rows = make_in_maps(inputs)
    res = run_bass_kernel_spmd(nc, in_maps, core_ids=list(range(8)))
    out = np.empty((B, S, D), dtype=np.float32)
    for cid in range(8):
        bb, own = own_rows[cid]
        out[bb][own] = np.asarray(res.results[cid]["out_q"], dtype=np.float32)
    return out


# version 18
# speedup vs baseline: 1.0221x; 1.0219x over previous
import math
import numpy as np
import ml_dtypes
import concourse.bass as bass
import concourse.mybir as mybir
from concourse.bass_utils import run_bass_kernel_spmd

F32 = mybir.dt.float32
BF16 = mybir.dt.bfloat16
AF = mybir.ActivationFunctionType
OP = mybir.AluOpType

B, S, D = 4, 2048, 2048
H, DN, DR = 16, 128, 64
C = 512
UQ = 3072
FI = 5632
TQ = 1024
KB_LO, KB_HI = 8, 16
SCALE = 1.0 / math.sqrt(DN + DR)
EPS_LN = 1e-5
EPS_RMS = 1e-6

ENGINES = ("sync", "scalar", "vector", "tensor", "gpsimd")
_DTB = {F32: 4, BF16: 2}


class Ctr:
    def __init__(self, sem):
        self.sem = sem
        self.n = 0


class Build:
    def __init__(self, nc):
        self.nc = nc
        self.ops = {e: [] for e in ENGINES}
        self.last_wait = {e: {} for e in ENGINES}
        self.dirty = {e: False for e in ENGINES}
        self._ctrs = {}
        self._names = {}
        self.base = (nc.sbuf_base + 63) & ~63

    def ctr(self, name):
        c = self._ctrs.get(name)
        if c is None:
            c = Ctr(self.nc.alloc_semaphore(name))
            self._ctrs[name] = c
        return c

    def wait(self, e, tok):
        if tok is None:
            return
        sem, v = tok
        key = id(sem)
        if self.last_wait[e].get(key, -1) >= v:
            return
        self.last_wait[e][key] = v
        self.ops[e].append(lambda eng: eng.wait_ge(sem, v))

    def waits(self, e, toks):
        for t in toks:
            self.wait(e, t)

    def do(self, e, fn, mark=None, amt=1):
        if mark is not None:
            mark.n += amt
            tok = (mark.sem, mark.n)

            def wrapped(eng, fn=fn, sem=mark.sem, k=amt):
                fn(eng).then_inc(sem, k)
            self.ops[e].append(wrapped)
            if e in ("vector", "scalar", "gpsimd"):
                self.ops[e].append(lambda eng: eng.drain())
                self.dirty[e] = False
            else:
                self.dirty[e] = True
            return tok
        self.ops[e].append(lambda eng: fn(eng))
        self.dirty[e] = True
        return None

    def drain(self, e):
        if not self.dirty[e]:
            return
        self.ops[e].append(lambda eng: eng.drain())
        self.dirty[e] = False

    def uname(self, name):
        k = self._names.get(name, 0)
        self._names[name] = k + 1
        return name if k == 0 else f"{name}_{k}"

    def barrier(self):
        toks = []
        for name, c in self._ctrs.items():
            if c.n:
                toks.append((c.sem, c.n))
        marks = [
            self.do("vector", lambda e: e.engine_nop(), mark=self.ctr("ve")),
            self.do("scalar", lambda e: e.copy(out=self._bar_t[0:1, 0:1],
                                               in_=self._bar_t[0:1, 0:1]),
                    mark=self.ctr("ac")),
            self.do("gpsimd", lambda e: e.engine_nop(), mark=self.ctr("gp")),
        ]
        alltoks = toks + marks
        for e in ENGINES:
            self.waits(e, alltoks)

    def realize(self, block):
        for e in ENGINES:
            ops = self.ops[e]

            def runner(eng, ops=ops):
                for f in ops:
                    f(eng)
            getattr(block, e)(runner)


class Region:
    def __init__(self, b, start, end):
        self.b = b
        self.start = start
        self.end = end
        self.cur = 0

    def alloc(self, name, shape, dt=F32):
        nbytes = shape[-1] * _DTB[dt]
        off = (self.cur + 63) & ~63
        assert off + nbytes <= self.end - self.start, \
            f"region overflow: {name} needs {nbytes}B at {off}, " \
            f"cap {self.end - self.start}"
        self.cur = off + nbytes
        return self.b.nc.alloc_sbuf_tensor_at(
            self.b.uname(name), shape, dt,
            offset=self.b.base + self.start + off)

    def reset(self):
        self.cur = 0


class DmaChan:
    def __init__(self, b, name, engine="sync"):
        self.b = b
        self.e = engine
        self.c = b.ctr("dma_" + name)

    def xfer(self, out_ap, in_ap, after=()):
        b = self.b
        if self.c.n:
            b.wait(self.e, (self.c.sem, self.c.n))
        for t in after:
            b.wait(self.e, t)
        return b.do(self.e, lambda eng: eng.dma_start(out=out_ap, in_=in_ap),
                    mark=self.c, amt=16)

    load = xfer
    store = xfer


def build_program(stage=9, taps=()):
    nc = bass.Bass(trn_type="TRN2", target_bir_lowering=False)
    b = Build(nc)
    taps = set(taps)
    tap_dr = {}

    def tapt(name, shape):
        t = nc.dram_tensor("tap_" + name, shape, F32, kind="ExternalOutput").ap()
        tap_dr[name] = t
        return t

    def din(name, shape, dt=F32):
        return nc.dram_tensor(name, shape, dt, kind="ExternalInput").ap()

    x_b = din("x_b", [S, D])
    xq = din("xq", [TQ, D])
    w_dq = din("w_dq", [D, C])
    w_uq = din("w_uq", [C, UQ])
    w_dkv = din("w_dkv", [D, C])
    w_ukrope = din("w_ukrope", [D, DR])
    w_uv = din("w_uv", [C, D])
    w_uknope = din("w_uknope", [C, D])
    w_o = din("w_o", [D, D])
    gate_w = din("gate_w", [FI, 2 * D], BF16)
    up_w = din("up_w", [FI, 2 * D], BF16)
    down_w = din("down_w", [D, 2 * FI], BF16)
    g_q_t = din("g_q_t", [128, 4])
    g_kv_t = din("g_kv_t", [128, 4])
    gate_s_t = din("gate_s_t", [128, 44])
    up_s_t = din("up_s_t", [128, 44])
    ds_b = din("ds_b", [128, D])
    qpos_b = din("qpos_b", [128, TQ])
    cos_k = din("cos_k", [S, DR])
    sinS_k = din("sinS_k", [S, DR])
    cos_q = din("cos_q", [TQ, DR])
    sinS_q = din("sinS_q", [TQ, DR])

    out_q = nc.dram_tensor("out_q", [TQ, D], F32, kind="ExternalOutput").ap()
    x1_scr = nc.dram_tensor("x1_scr", [TQ, D], F32, kind="Internal").ap()
    acc_scr = nc.dram_tensor("acc_scr", [TQ, D], F32, kind="Internal").ap()
    gwT_scr = nc.dram_tensor("gwT_scr", [44 * 128, D], BF16, kind="Internal").ap()
    uwT_scr = nc.dram_tensor("uwT_scr", [44 * 128, D], BF16, kind="Internal").ap()
    dwT_scr = nc.dram_tensor("dwT_scr", [2 * 8 * 22 * 128, 256], BF16,
                             kind="Internal").ap()

    KBY = 1024
    ST = Region(b, 0, 9 * KBY)
    RA = Region(b, 9 * KBY, 61 * KBY)
    RB = Region(b, 61 * KBY, 93 * KBY)
    SEND = (229344 - 64) - b.base
    SCR = Region(b, 93 * KBY, SEND)
    BIG = Region(b, 9 * KBY, SEND)

    iota_p = ST.alloc("iota_p", [128, 16])
    iota_row = ST.alloc("iota_row", [128, 128])
    eps_ln_t = ST.alloc("eps_ln_t", [128, 16])
    eps_rms_t = ST.alloc("eps_rms_t", [128, 16])
    ident_f = ST.alloc("ident_f", [128, 128])
    ident_b = ST.alloc("ident_b", [128, 128], BF16)
    ones_b = ST.alloc("ones_b", [128, 128], BF16)
    gq_t = ST.alloc("gq_t", [128, 16])
    gkv_t = ST.alloc("gkv_t", [128, 16])
    thr = ST.alloc("thr", [128, 16])
    stat6 = ST.alloc("stat6", [128, 32])
    mv2 = ST.alloc("mv2", [128, 16])
    negm = ST.alloc("negm", [128, 16])
    rsq1 = ST.alloc("rsq1", [128, 16])
    rt1 = ST.alloc("rt1", [128, 512])
    sqf = ST.alloc("sqf", [128, 512])
    sqb = ST.alloc("sqb", [128, 512], BF16)
    sqb2 = ST.alloc("sqb2", [128, 512], BF16)
    rop_a = ST.alloc("rop_a", [128, 64])
    rop_c = ST.alloc("rop_c", [128, 64])
    b._bar_t = rsq1

    ckv_hi = RA.alloc("ckv_hi", [128, 4 * S], BF16)
    ckv_lo = RA.alloc("ckv_lo", [128, 4 * S], BF16)
    cq_hi = RA.alloc("cq_hi", [128, 4 * TQ], BF16)
    cq_lo = RA.alloc("cq_lo", [128, 4 * TQ], BF16)
    krp = RA.alloc("krp", [128, S], BF16)

    PSB = [nc.alloc_psum_tensor(f"psb{i}", [128, 512], F32) for i in range(6)]
    PS_TR2 = nc.alloc_psum_tensor("ps_tr2", [128, 512], F32)
    nc.bump_psum(banks=-1)
    PSBF = nc.alloc_psum_tensor("psbf", [128, 1024], BF16)
    PS_TR = nc.alloc_psum_tensor("ps_tr", [128, 512], F32)

    chx = DmaChan(b, "x", "sync")
    chw = DmaChan(b, "w", "sync")
    chm = DmaChan(b, "meta", "sync")
    cho = DmaChan(b, "out", "sync")
    chs = DmaChan(b, "scr", "scalar")
    chc = DmaChan(b, "cast", "gpsimd")
    cht = DmaChan(b, "tap", "gpsimd")

    ve = b.ctr("ve")
    ac = b.ctr("ac")
    te = b.ctr("te")
    gp = b.ctr("gp")

    def pap(handle, offset, dims):
        return bass.AP(handle, offset, dims)

    b.do("gpsimd", lambda e: e.iota(iota_p[:, 0:1], pattern=[[0, 1]], base=0,
                                    channel_multiplier=1,
                                    allow_small_or_imprecise_dtypes=True))
    b.do("gpsimd", lambda e: e.iota(iota_row[:], pattern=[[1, 128]], base=0,
                                    channel_multiplier=0,
                                    allow_small_or_imprecise_dtypes=True))
    b.do("gpsimd", lambda e: e.memset(ones_b[:], 1.0))
    b.do("gpsimd", lambda e: e.memset(eps_ln_t[:, 0:1], EPS_LN))
    b.do("gpsimd", lambda e: e.memset(eps_rms_t[:, 0:1], EPS_RMS))
    b.drain("gpsimd")
    t_const = b.do("gpsimd", lambda e: e.engine_nop(), mark=gp)

    b.wait("vector", t_const)
    b.do("vector", lambda e: e.tensor_scalar(out=ident_f[:], in0=iota_row[:],
                                             scalar1=iota_p[:, 0:1],
                                             scalar2=None, op0=OP.is_equal))
    b.drain("vector")
    b.do("vector", lambda e: e.tensor_copy(out=ident_b[:], in_=ident_f[:]))
    for kb in range(16):
        b.do("vector", lambda e, kb=kb: e.tensor_scalar(
            out=thr[:, kb:kb + 1], in0=iota_p[:, 0:1],
            scalar1=float(128 * kb), scalar2=None, op0=OP.add))
    b.drain("vector")
    t_ident = b.do("vector", lambda e: e.engine_nop(), mark=ve)

    chm.load(gq_t[:, 0:4], g_q_t[:, :])
    chm.load(gkv_t[:, 0:4], g_kv_t[:, :])

    def ln_stats_apply(src_ap, dst_ap, width, res_ap=None):
        nparts = width // 512
        for i in range(nparts):
            b.do("vector", lambda e, i=i: e.bn_stats(
                out=stat6[:, i * 6:(i + 1) * 6],
                in_=src_ap[:, i * 512:(i + 1) * 512]))
        b.drain("vector")
        b.do("vector", lambda e: e.bn_aggr(out=mv2[:, 0:2],
                                           in_=stat6[:, 0:6 * nparts]))
        b.drain("vector")
        b.do("vector", lambda e: e.tensor_scalar(
            out=negm[:, 0:1], in0=mv2[:, 0:1], scalar1=-1.0, scalar2=None,
            op0=OP.mult))
        t_v = b.do("vector", lambda e: e.engine_nop(), mark=ve)
        b.wait("scalar", t_v)
        b.do("scalar", lambda e: e.activation(out=rsq1[:, 0:1],
                                              in_=mv2[:, 1:2], func=AF.Ln,
                                              bias=eps_ln_t[:, 0:1],
                                              scale=1.0))
        b.drain("scalar")
        t_ac = b.do("scalar", lambda e: e.activation(
            out=rsq1[:, 0:1], in_=rsq1[:, 0:1], func=AF.Exp, scale=-0.5),
            mark=ac)
        b.wait("vector", t_ac)
        b.do("vector", lambda e: e.tensor_tensor(
            out=negm[:, 1:2], in0=negm[:, 0:1], in1=rsq1[:, 0:1],
            op=OP.mult))
        b.drain("vector")
        t_nm = b.do("vector", lambda e: e.engine_nop(), mark=ve)
        b.wait("scalar", t_nm)
        b.do("scalar", lambda e: e.activation(
            out=dst_ap, in_=src_ap, func=AF.Identity,
            scale=rsq1[:, 0:1], bias=negm[:, 1:2]))
        b.drain("scalar")
        t_ap = b.do("scalar", lambda e: e.copy(out=rsq1[0:1, 0:1],
                                               in_=rsq1[0:1, 0:1]), mark=ac)
        if res_ap is not None:
            b.wait("vector", t_ap)
            b.do("vector", lambda e: e.tensor_tensor(
                out=dst_ap, in0=dst_ap, in1=res_ap, op=OP.add))
            b.drain("vector")
            return b.do("vector", lambda e: e.engine_nop(), mark=ve)
        return t_ap

    def mm_group(bank_ap, mms, waits=(), cons=False, pre=None):
        b.waits("tensor", waits)
        n = len(mms)
        tok = None
        for i, (l, r) in enumerate(mms):
            if pre and i in pre:
                b.waits("tensor", tuple(t for t in pre[i] if t is not None))
            s, t = (i == 0), (i == n - 1)
            if t:
                if cons:
                    b.drain("tensor")
                tok = b.do("tensor", lambda e, l=l, r=r, s=s:
                           e.matmul(bank_ap, l, r, start=s, stop=True),
                           mark=te)
            else:
                b.do("tensor", lambda e, l=l, r=r, s=s:
                     e.matmul(bank_ap, l, r, start=s, stop=False))
        return tok

    def split3(hi_ap, lo_ap, src_ap, waits=()):
        b.waits("vector", waits)
        b.do("vector", lambda e: e.tensor_copy(out=hi_ap, in_=src_ap))
        b.drain("vector")
        return b.do("vector", lambda e: e.tensor_tensor(
            out=lo_ap, in0=src_ap, in1=hi_ap, op=OP.subtract), mark=ve)

    def evac_split(bank_ap, hi_ap, lo_ap, t_mm):
        b.wait("scalar", t_mm)
        b.do("scalar", lambda e: e.activation(out=hi_ap, in_=bank_ap,
                                              func=AF.Identity, scale=1.0))
        b.drain("scalar")
        t_hi = b.do("scalar", lambda e: e.copy(out=rsq1[0:1, 0:1],
                                               in_=rsq1[0:1, 0:1]), mark=ac)
        b.wait("vector", t_mm)
        b.wait("vector", t_hi)
        return b.do("vector", lambda e: e.tensor_tensor(
            out=lo_ap, in0=bank_ap, in1=hi_ap, op=OP.subtract), mark=ve)

    def tr_split_tile(src_f32, dst_hi, dst_lo, col_off, col_stride, waits=(),
                      nbanks=2):
        t_ev2 = [None, None]
        t_ev = t_te = None
        for rnd in range(4):
            par = rnd % nbanks
            bank = PS_TR if par == 0 else PS_TR2
            b.waits("tensor", waits)
            if t_ev2[par] is not None:
                b.wait("tensor", t_ev2[par])
            for j in range(3):
                dct = rnd * 4 + j
                b.do("tensor", lambda e, j=j, dct=dct, bank=bank: e.transpose(
                    bank[:, j * 128:(j + 1) * 128],
                    src_f32[:, dct * 128:(dct + 1) * 128], ident_f[:]))
            b.drain("tensor")
            t_te = b.do("tensor", lambda e, dct=rnd * 4 + 3, bank=bank:
                        e.transpose(
                bank[:, 3 * 128:4 * 128],
                src_f32[:, dct * 128:(dct + 1) * 128], ident_f[:]), mark=te)
            prow = list(dst_hi[:].ap[0])
            off = (rnd * 4) * col_stride + col_off
            hi_view = pap(dst_hi, off, [prow, [col_stride, 4], [1, 128]])
            lo_view = pap(dst_lo, off, [prow, [col_stride, 4], [1, 128]])
            ps_view = bank[:].rearrange("p (a f) -> p a f", a=4)
            b.wait("scalar", t_te)
            b.do("scalar", lambda e, hv=hi_view, pv=ps_view: e.activation(
                out=hv, in_=pv, func=AF.Identity, scale=1.0))
            b.drain("scalar")
            t_hi = b.do("scalar", lambda e: e.copy(out=rsq1[0:1, 0:1],
                                                   in_=rsq1[0:1, 0:1]),
                        mark=ac)
            b.wait("vector", t_te)
            b.wait("vector", t_hi)
            t_ev = b.do("vector", lambda e, lv=lo_view, pv=ps_view,
                        hv=hi_view: e.tensor_tensor(
                out=lv, in0=pv, in1=hv, op=OP.subtract), mark=ve)
            t_ev2[par] = t_ev
        return t_ev, t_te

    lnf_hi = SCR.alloc("lnf_hi", [128, 16 * 512], BF16)
    lnf_lo = SCR.alloc("lnf_lo", [128, 16 * 512], BF16)
    ln_tok = SCR.alloc("ln_tok", [128, D])
    wstage = SCR.alloc("wstage", [128, 2048])
    wdkv_hi = SCR.alloc("wdkv_hi", [128, 16 * 512], BF16)
    ln_tok_b = RB.alloc("ln_tok_b", [128, D])
    wdkv_lo = SCR.alloc("wdkv_lo", [128, 16 * 512], BF16)
    ckv_f = SCR.alloc("ckv_f", [128, 4 * S])
    krpre = RB.alloc("krpre", [64, S])
    cosk_t = RB.alloc("cosk_t", [128, 16 * DR])
    sink_t = RB.alloc("sink_t", [128, 16 * DR])
    wkr_hi = RB.alloc("wkr_hi", [128, 16 * DR], BF16)
    wkr_lo = RB.alloc("wkr_lo", [128, 16 * DR], BF16)

    t_cosk = None
    for i in range(16):
        chm.load(cosk_t[:, i * DR:(i + 1) * DR], cos_k[i * 128:(i + 1) * 128, :])
        t_cosk = chm.load(sink_t[:, i * DR:(i + 1) * DR],
                          sinS_k[i * 128:(i + 1) * 128, :])

    last_split = t_ident
    for wsrc, whi, wlo, cols in ((w_dkv, wdkv_hi, wdkv_lo, C),
                                 (w_ukrope, wkr_hi, wkr_lo, DR)):
        spl_hist = [last_split, last_split]
        for ct in range(16):
            stg = wstage[:, 0:cols] if ct % 2 == 0 else wstage[:, cols:2 * cols]
            tl = chw.load(stg, wsrc[ct * 128:(ct + 1) * 128, :],
                          after=(spl_hist[ct % 2],))
            last_split = split3(whi[:, ct * cols:(ct + 1) * cols],
                                wlo[:, ct * cols:(ct + 1) * cols], stg, (tl,))
            spl_hist[ct % 2] = last_split

    def proj_chunks(src, nchunks, whi, wlo, dstf, toklen, do_rope, t_w):
        t_prev_cons = None
        t_evs = None
        t_tr_hist = [None, None]
        for ch in range(nchunks):
            t_lnf = None
            for t4 in range(4):
                ti = ch * 4 + t4
                buf = ln_tok if ti % 2 == 0 else ln_tok_b
                tl = chx.load(buf[:], src[ch * 512 + t4 * 128:
                                          ch * 512 + (t4 + 1) * 128, :],
                              after=(t_tr_hist[ti % 2],)
                              if t_tr_hist[ti % 2] else ())
                b.wait("vector", tl)
                t_ln = ln_stats_apply(buf[:], buf[:], D)
                waits = [t_w, t_ln]
                if t4 == 0 and t_prev_cons is not None:
                    waits.append(t_prev_cons)
                t_lnf, t_tr_hist[ti % 2] = tr_split_tile(
                    buf[:], lnf_hi, lnf_lo, t4 * 128, 512, waits=waits)
            b.wait("tensor", t_lnf)
            for m in range(4):
                bank = PSB[m]
                mms = []
                for ct in range(16):
                    w_h = whi[:, ct * C + m * 128: ct * C + (m + 1) * 128]
                    w_l = wlo[:, ct * C + m * 128: ct * C + (m + 1) * 128]
                    a_h = lnf_hi[:, ct * 512:(ct + 1) * 512]
                    a_l = lnf_lo[:, ct * 512:(ct + 1) * 512]
                    mms += [(w_h, a_h), (w_h, a_l), (w_l, a_h)]
                t_mm = mm_group(bank[:], mms, cons=(m == 3 and not do_rope))
                b.wait("vector", t_mm)
                b.do("vector", lambda e, bank=bank, m=m, ch=ch: e.tensor_copy(
                    out=dstf[:, m * toklen + ch * 512:
                             m * toklen + (ch + 1) * 512], in_=bank[:]))
                b.drain("vector")
            if do_rope:
                mms = []
                for ct in range(16):
                    mms += [(wkr_hi[:, ct * DR:(ct + 1) * DR],
                             lnf_hi[:, ct * 512:(ct + 1) * 512]),
                            (wkr_hi[:, ct * DR:(ct + 1) * DR],
                             lnf_lo[:, ct * 512:(ct + 1) * 512]),
                            (wkr_lo[:, ct * DR:(ct + 1) * DR],
                             lnf_hi[:, ct * 512:(ct + 1) * 512])]
                t_mm = mm_group(PSB[4][0:64, :], mms, cons=True)
                b.wait("vector", t_mm)
                b.do("vector", lambda e, ch=ch: e.tensor_copy(
                    out=krpre[:, ch * 512:(ch + 1) * 512],
                    in_=PSB[4][0:64, :]))
                b.drain("vector")
            t_evs = b.do("vector", lambda e: e.engine_nop(), mark=ve)
            t_prev_cons = t_mm
            b.wait("tensor", t_evs)
        return t_evs

    t_p1 = proj_chunks(x_b, 4, wdkv_hi, wdkv_lo, ckv_f, S, True, last_split)

    def rms_split(cf, chi_, clo_, g_t, toklen, t_start):
        prev_rt = None
        last = t_start
        for ch in range(toklen // 512):
            t_mm = None
            for m in range(4):
                sl = cf[:, m * toklen + ch * 512: m * toklen + (ch + 1) * 512]
                b.wait("vector", last)
                if t_mm is not None:
                    b.wait("vector", t_mm)
                b.do("vector", lambda e, sl=sl: e.tensor_tensor(
                    out=sqf[:], in0=sl, in1=sl, op=OP.mult))
                b.drain("vector")
                b.do("vector", lambda e: e.tensor_copy(out=sqb[:], in_=sqf[:]))
                b.drain("vector")
                t_sq = b.do("vector", lambda e: e.tensor_tensor(
                    out=sqb2[:], in0=sqf[:], in1=sqb[:], op=OP.subtract),
                    mark=ve)
                wts = [t_sq]
                if prev_rt is not None and m == 0:
                    wts.append(prev_rt)
                b.waits("tensor", wts)
                b.do("tensor", lambda e, s=(m == 0): e.matmul(
                    PSB[5][:], ones_b[:], sqb[:], start=s, stop=False))
                b.drain("tensor")
                t_mm = b.do("tensor", lambda e, t=(m == 3): e.matmul(
                    PSB[5][:], ones_b[:], sqb2[:], start=False, stop=t),
                    mark=te)
            b.wait("scalar", t_mm)
            b.do("scalar", lambda e: e.activation(out=rt1[:], in_=PSB[5][:],
                                                  func=AF.Ln,
                                                  bias=eps_rms_t[:, 0:1],
                                                  scale=1.0 / C))
            b.drain("scalar")
            prev_rt = b.do("scalar", lambda e: e.activation(
                out=rt1[:], in_=rt1[:], func=AF.Exp, scale=-0.5), mark=ac)
            b.wait("vector", prev_rt)
            for m in range(4):
                sl = cf[:, m * toklen + ch * 512: m * toklen + (ch + 1) * 512]
                off = m * toklen + ch * 512
                b.do("vector", lambda e, sl=sl, m=m, g_t=g_t:
                     e.scalar_tensor_tensor(
                    out=sqf[:], in0=sl, scalar=g_t[:, m:m + 1], in1=rt1[:],
                    op0=OP.mult, op1=OP.mult))
                b.drain("vector")
                b.do("vector", lambda e, off=off, chi_=chi_: e.tensor_copy(
                    out=chi_[:, off:off + 512], in_=sqf[:]))
                b.drain("vector")
                last = b.do("vector", lambda e, off=off, chi_=chi_,
                            clo_=clo_: e.tensor_tensor(
                    out=clo_[:, off:off + 512], in0=sqf[:],
                    in1=chi_[:, off:off + 512], op=OP.subtract), mark=ve)
        return last

    t_ckv = rms_split(ckv_f, ckv_hi, ckv_lo, gkv_t, S, t_p1)
    if "ckv" in taps:
        tt = tapt("ckv_hi", [128, 4 * S])
        cht.store(tt[:, :], ckv_hi[:], after=(t_ckv,))
        tt2 = tapt("ckv_lo", [128, 4 * S])
        cht.store(tt2[:, :], ckv_lo[:], after=(t_ckv,))

    b.barrier()
    SCR.reset()
    lnf_hi = SCR.alloc("lnf_hi", [128, 16 * 512], BF16)
    lnf_lo = SCR.alloc("lnf_lo", [128, 16 * 512], BF16)
    ln_tok = SCR.alloc("ln_tok", [128, D])
    wstage = SCR.alloc("wstage", [128, 2048])
    wdq_hi = SCR.alloc("wdq_hi", [128, 16 * 512], BF16)
    wdq_lo = SCR.alloc("wdq_lo", [128, 16 * 512], BF16)
    cq_f = SCR.alloc("cq_f", [128, 4 * TQ])

    last_split = None
    spl_hist = [None, None]
    for ct in range(16):
        stg = wstage[:, 0:C] if ct % 2 == 0 else wstage[:, C:2 * C]
        tl = chw.load(stg, w_dq[ct * 128:(ct + 1) * 128, :],
                      after=(spl_hist[ct % 2],) if spl_hist[ct % 2] else ())
        last_split = split3(wdq_hi[:, ct * C:(ct + 1) * C],
                            wdq_lo[:, ct * C:(ct + 1) * C], stg, (tl,))
        spl_hist[ct % 2] = last_split

    t_p1q = proj_chunks(xq, 2, wdq_hi, wdq_lo, cq_f, TQ, False, last_split)
    t_cq = rms_split(cq_f, cq_hi, cq_lo, gq_t, TQ, t_p1q)
    if "cq" in taps:
        tt = tapt("cq_hi", [128, 4 * TQ])
        cht.store(tt[:, :], cq_hi[:], after=(t_cq,))
        tt2 = tapt("cq_lo", [128, 4 * TQ])
        cht.store(tt2[:, :], cq_lo[:], after=(t_cq,))

    def rope_tiles(pre_ap, n_tiles, cos_t, sin_t, dst_fn, waits0=()):
        last2 = [None, None]
        last = None
        for i in range(n_tiles):
            par = i % 2
            bank = PS_TR if par == 0 else PS_TR2
            ra = rop_a if par == 0 else rop_c
            rbap = bank[:, 64:128]
            wts = list(waits0)
            if last2[par] is not None:
                wts.append(last2[par])
            b.waits("tensor", wts)
            t_t1 = b.do("tensor", lambda e, i=i, pre_ap=pre_ap, bank=bank:
                        e.transpose(
                bank[:, 0:64], pre_ap[:, i * 128:(i + 1) * 128],
                ident_f[0:64, 0:64]), mark=te)
            b.wait("vector", t_t1)
            prow = list(bank[:].ap[0])
            rot = pap(bank, 1, [prow, [2, 32], [-1, 2]])
            b.do("vector", lambda e, i=i, cos_t=cos_t, ra=ra, bank=bank:
                 e.tensor_tensor(
                out=ra[:], in0=bank[:, 0:64],
                in1=cos_t[:, i * DR:(i + 1) * DR], op=OP.mult))
            b.do("vector", lambda e, i=i, rot=rot, sin_t=sin_t, rbap=rbap:
                 e.tensor_tensor(
                out=rbap.rearrange("p (a c) -> p a c", c=2), in0=rot,
                in1=sin_t[:, i * DR:(i + 1) * DR].rearrange(
                    "p (a c) -> p a c", c=2), op=OP.mult))
            b.drain("vector")
            t_rp = b.do("vector", lambda e, ra=ra, rbap=rbap: e.tensor_tensor(
                out=ra[:], in0=ra[:], in1=rbap, op=OP.add),
                mark=ve)
            b.wait("tensor", t_rp)
            t_t2 = b.do("tensor", lambda e, ra=ra, bank=bank: e.transpose(
                bank[0:64, 128:256], ra[:], ident_f[:]), mark=te)
            last = dst_fn(i, t_t2, bank)
            last2[par] = last
        return last

    def krp_dst(i, t_tr, bank):
        b.wait("vector", t_tr)
        b.do("vector", lambda e, i=i, bank=bank: e.tensor_copy(
            out=krp[0:64, i * 128:(i + 1) * 128], in_=bank[0:64, 128:256]))
        b.drain("vector")
        return b.do("vector", lambda e, i=i, bank=bank: e.tensor_tensor(
            out=krp[64:128, i * 128:(i + 1) * 128], in0=bank[0:64, 128:256],
            in1=krp[0:64, i * 128:(i + 1) * 128], op=OP.subtract), mark=ve)

    t_krp = rope_tiles(krpre, 16, cosk_t, sink_t, krp_dst,
                       waits0=(t_cq, t_cosk))
    if "krp" in taps:
        tt = tapt("krp", [128, S])
        cht.store(tt[:, :], krp[:], after=(t_krp,))

    if stage <= 2:
        if cht.c.n:
            b.wait("gpsimd", (cht.c.sem, cht.c.n))
        with nc.Block() as block:
            b.realize(block)
        return nc, tap_dr

    b.barrier()
    RB3 = Region(b, 61 * KBY, 125 * KBY)
    SCR = Region(b, 125 * KBY, SEND)
    oa_hi = RB3.alloc("oa_hi", [128, H * TQ], BF16)
    oa_lo = RB3.alloc("oa_lo", [128, H * TQ], BF16)
    qposb_t = SCR.alloc("qposb_t", [128, TQ])
    masks16 = SCR.alloc("masks16", [128, 16 * 512], BF16)
    cosq_t = SCR.alloc("cosq_t", [128, 8 * DR])
    sinq_t = SCR.alloc("sinq_t", [128, 8 * DR])
    v2_hi = SCR.alloc("v2_hi", [128, 16 * 256], BF16)
    v2_lo = SCR.alloc("v2_lo", [128, 16 * 256], BF16)
    kh_hi = SCR.alloc("kh_hi", [128, S], BF16)
    kh_lo = SCR.alloc("kh_lo", [128, S], BF16)
    qn_hi = SCR.alloc("qn_hi", [128, TQ], BF16)
    qn_lo = SCR.alloc("qn_lo", [128, TQ], BF16)
    qr_pack = SCR.alloc("qr_pack", [128, TQ], BF16)
    qr_lo = SCR.alloc("qr_lo", [64, TQ], BF16)
    p_f = [SCR.alloc(f"p_f{i}", [128, 512]) for i in range(2)]
    p_mm = SCR.alloc("p_mm", [128, 512])
    p_hi = [SCR.alloc(f"p_hi{i}", [128, 512], BF16) for i in range(2)]
    p_lo = [SCR.alloc(f"p_lo{i}", [128, 512], BF16) for i in range(2)]
    wsg_off = b.base + SCR.start + ((SCR.cur + 63) & ~63)
    wsg = SCR.alloc("wsg", [128, 1024])
    wh_a = SCR.alloc("wh_a", [128, 1024], BF16)
    wl_a = SCR.alloc("wl_a", [128, 1024], BF16)
    wsq = wsg
    wh_b = SCR.alloc("wh_b", [128, 512], BF16)
    wl_b = SCR.alloc("wl_b", [128, 512], BF16)
    wh_c = SCR.alloc("wh_c", [128, 768], BF16)
    wl_c = SCR.alloc("wl_c", [128, 768], BF16)
    wsqq = SCR.alloc("wsqq", [128, 768])
    qr_pre = nc.alloc_sbuf_tensor_at(b.uname("qr_pre"), [64, TQ], F32,
                                     offset=wsg_off)
    wsk = nc.alloc_sbuf_tensor_at(b.uname("wsk"), [128, 512], F32,
                                  offset=b.base + SCR.start)

    t_qpos = chm.load(qposb_t[:], qpos_b[:, :])
    t_cosq = None
    for i in range(8):
        chm.load(cosq_t[:, i * DR:(i + 1) * DR], cos_q[i * 128:(i + 1) * 128, :])
        t_cosq = chm.load(sinq_t[:, i * DR:(i + 1) * DR],
                          sinS_q[i * 128:(i + 1) * 128, :])
    b.wait("vector", t_qpos)
    for kb in range(16):
        qoff = 0 if kb < 8 else 512
        b.do("vector", lambda e, kb=kb, qoff=qoff: e.tensor_scalar(
            out=masks16[:, kb * 512:(kb + 1) * 512],
            in0=qposb_t[:, qoff:qoff + 512], scalar1=thr[:, kb:kb + 1],
            scalar2=None, op0=OP.is_ge))
    b.drain("vector")
    t_masks = b.do("vector", lambda e: e.engine_nop(), mark=ve)

    t_exp = [None, None]
    t_pcons = [None, None]
    t_oa_last = None
    t_oa_bank = [None, None]
    t_v_read = None
    t_hcons = None
    t_vgen_cons = None
    t_qrmm_prev = None
    t_kh3 = None
    t_rt_read = [None]

    chains = {"v": [None], "k": [None], "q": [None]}
    wtok = {}
    pend_splits = []

    def enq_V(g, dest_free, at=None):
        aft = tuple(t for t in (chains["v"][0],) + dest_free
                    if t is not None)
        tlw = chw.load(wsg[:],
                       pap(w_uv.tensor, g * 256,
                           [[D, 128], [128 * D, 4], [1, 256]]),
                       after=aft)

        def spl(tlw=tlw):
            tok = split3(wh_a[:], wl_a[:], wsg[:], (tlw,))
            chains["v"][0] = tok
            wtok["v"] = tok
        if at is None:
            spl()
        else:
            pend_splits.append((at[0], at[1], spl))

    def enq_K(g, h2, dest_free, at=None):
        aft = tuple(t for t in (chains["k"][0],) + dest_free
                    if t is not None)
        tlw = chw.load(wsk[:],
                       pap(w_uknope.tensor, (g * 2 + h2) * 128,
                           [[D, 128], [128 * D, 4], [1, 128]]),
                       after=aft)

        def spl(tlw=tlw):
            tok = split3(wh_b[:], wl_b[:], wsk[:], (tlw,))
            chains["k"][0] = tok
            wtok["k"] = tok
        if at is None:
            spl()
        else:
            pend_splits.append((at[0], at[1], spl))

    def enq_Q(g, h2, dest_free, at=None):
        aft = tuple(t for t in (chains["q"][0],) + dest_free
                    if t is not None)
        tlw = chw.load(wsqq[:],
                       pap(w_uq.tensor, (g * 2 + h2) * 192,
                           [[UQ, 128], [128 * UQ, 4], [1, 192]]),
                       after=aft)

        def spl(tlw=tlw):
            tok = split3(wh_c[:], wl_c[:], wsqq[:], (tlw,))
            chains["q"][0] = tok
            wtok["q"] = tok
        if at is None:
            spl()
        else:
            pend_splits.append((at[0], at[1], spl))

    RAW4 = Region(b, 9 * KBY, 61 * KBY)
    wstage4 = RAW4.alloc("wstage4", [128, 2048])
    wstage4b = RAW4.alloc("wstage4b", [128, 2048])
    wo_hi2 = RAW4.alloc("wo_hi2", [128, 4 * D], BF16)
    wo_lo2 = RAW4.alloc("wo_lo2", [128, 4 * D], BF16)
    tl_pre4 = [None, None]

    for grp in range(8):
        if grp == 0:
            enq_V(0, (t_ckv,))
            t_wv = wtok["v"]
        else:
            t_wv = wtok["v"]
        ev_hist = [None, None]
        for tb in range(16):
            bank = PSB[tb % 2]
            mms = []
            for ct in range(4):
                l_h = ckv_hi[:, ct * S + tb * 128: ct * S + (tb + 1) * 128]
                l_l = ckv_lo[:, ct * S + tb * 128: ct * S + (tb + 1) * 128]
                r_h = wh_a[:, ct * 256:(ct + 1) * 256]
                r_l = wl_a[:, ct * 256:(ct + 1) * 256]
                mms += [(l_h, r_h), (l_h, r_l), (l_l, r_h)]
            wts = [t_wv]
            if t_v_read is not None:
                wts.append(t_v_read)
            if ev_hist[tb % 2] is not None:
                wts.append(ev_hist[tb % 2])
            t_mm = mm_group(bank[0:128, 0:256], mms, waits=wts,
                            cons=(tb == 15))
            if tb == 15:
                t_vgen_cons = t_mm
            ev_hist[tb % 2] = evac_split(bank[0:128, 0:256],
                                 v2_hi[:, tb * 256:(tb + 1) * 256],
                                 v2_lo[:, tb * 256:(tb + 1) * 256], t_mm)
        t_v2 = ev_hist[15 % 2]

        for h2 in range(2):
            h = grp * 2 + h2
            if grp == 0 and h2 == 0:
                enq_K(0, 0, (t_cq, t_masks))
                enq_Q(0, 0, (t_cq, t_ckv))
            t_wk, t_wq = wtok["k"], wtok["q"]
            ev_kh = [None, None]
            for tc in range(4):
                bank = PSB[2 + tc % 2]
                mms = []
                for ct in range(4):
                    w_h = wh_b[:, ct * 128:(ct + 1) * 128]
                    w_l = wl_b[:, ct * 128:(ct + 1) * 128]
                    a_h = ckv_hi[:, ct * S + tc * 512: ct * S + (tc + 1) * 512]
                    a_l = ckv_lo[:, ct * S + tc * 512: ct * S + (tc + 1) * 512]
                    mms += [(w_h, a_h), (w_h, a_l), (w_l, a_h)]
                wts = [t_wk]
                if t_hcons is not None:
                    wts.append(t_hcons)
                if t_oa_bank[tc % 2] is not None:
                    wts.append(t_oa_bank[tc % 2])
                if ev_kh[tc % 2] is not None:
                    wts.append(ev_kh[tc % 2])
                t_mm = mm_group(bank[:], mms, waits=wts, cons=(tc == 3))
                ev_kh[tc % 2] = evac_split(
                    bank[:], kh_hi[:, tc * 512:(tc + 1) * 512],
                    kh_lo[:, tc * 512:(tc + 1) * 512], t_mm)
                if tc == 3:
                    t_kh3 = t_mm
            b.wait("vector", ev_kh[0])
            t_kh = ev_kh[1]

            ev_q = None
            for qc in range(2):
                bank = PSB[2 + qc]
                mms = []
                for ct in range(4):
                    w_h = wh_c[:, ct * 192: ct * 192 + 128]
                    w_l = wl_c[:, ct * 192: ct * 192 + 128]
                    a_h = cq_hi[:, ct * TQ + qc * 512: ct * TQ + (qc + 1) * 512]
                    a_l = cq_lo[:, ct * TQ + qc * 512: ct * TQ + (qc + 1) * 512]
                    mms += [(w_h, a_h), (w_h, a_l), (w_l, a_h)]
                wts = [t_wq, t_kh]
                if t_hcons is not None:
                    wts.append(t_hcons)
                if t_oa_bank[qc] is not None:
                    wts.append(t_oa_bank[qc])
                if ev_q is not None:
                    wts.append(ev_q)
                t_mm = mm_group(bank[:], mms, waits=wts, cons=(qc == 1))
                ev_q = evac_split(bank[:], qn_hi[:, qc * 512:(qc + 1) * 512],
                                  qn_lo[:, qc * 512:(qc + 1) * 512], t_mm)
            t_qr = None
            for qc in range(2):
                mms = []
                for ct in range(4):
                    w_h = wh_c[:, ct * 192 + 128: ct * 192 + 192]
                    w_l = wl_c[:, ct * 192 + 128: ct * 192 + 192]
                    a_h = cq_hi[:, ct * TQ + qc * 512: ct * TQ + (qc + 1) * 512]
                    a_l = cq_lo[:, ct * TQ + qc * 512: ct * TQ + (qc + 1) * 512]
                    mms += [(w_h, a_h), (w_h, a_l), (w_l, a_h)]
                wts = [ev_q]
                if t_hcons is not None:
                    wts.append(t_hcons)
                if t_qr is not None:
                    wts.append(t_qr)
                t_mm = mm_group(PSB[4][0:64, :], mms, waits=wts,
                                cons=(qc == 1))
                if qc == 1:
                    t_qrmm_prev = t_mm
                b.wait("vector", t_mm)
                b.do("vector", lambda e, qc=qc: e.tensor_copy(
                    out=qr_pre[:, qc * 512:(qc + 1) * 512],
                    in_=PSB[4][0:64, :]))
                b.drain("vector")
                t_qr = b.do("vector", lambda e: e.engine_nop(), mark=ve)

            def qr_dst(i, t_tr, bank):
                b.wait("vector", t_tr)
                b.do("vector", lambda e, i=i, bank=bank: e.tensor_copy(
                    out=qr_pack[0:64, i * 128:(i + 1) * 128],
                    in_=bank[0:64, 128:256]))
                b.drain("vector")
                return b.do("vector", lambda e, i=i, bank=bank:
                            e.tensor_tensor(
                    out=qr_lo[:, i * 128:(i + 1) * 128],
                    in0=bank[0:64, 128:256],
                    in1=qr_pack[0:64, i * 128:(i + 1) * 128],
                    op=OP.subtract), mark=ve)

            t_qrope = rope_tiles(qr_pre, 8, cosq_t, sinq_t, qr_dst,
                                 waits0=(t_qr, t_cosq, t_hcons) if t_hcons
                                 else (t_qr, t_cosq))
            b.wait("vector", t_qrope)
            t_qrope = b.do("vector", lambda e: e.tensor_copy(
                out=qr_pack[64:128, :], in_=qr_pack[0:64, :]), mark=ve)

            if h2 == 0:
                enq_K(grp, 1, (t_kh3,), at=(0, 4))
                enq_Q(grp, 1, (t_qrmm_prev,), at=(1, 2))
            elif grp < 7:
                enq_V(grp + 1, (t_vgen_cons, t_qrope), at=(0, 4))
                enq_K(grp + 1, 0, (t_kh3,), at=(1, 2))
                enq_Q(grp + 1, 0, (t_qrmm_prev,), at=(1, 8))
            else:
                tl_pre4[0] = chw.load(wstage4[:], w_o[0:128, :],
                                      after=(t_kh3,))
                tl_pre4[1] = chw.load(wstage4b[:], w_o[128:256, :],
                                      after=(t_kh3,))

            for cidx, nkb in ((0, KB_LO), (1, KB_HI)):
                qoff = cidx * 512
                out_bank = PSB[2 + cidx]
                den_bank = PSB[5]
                t_av = None
                pend = None

                def emit_av(kb, par, t_p, nkb=nkb, out_bank=out_bank,
                            den_bank=den_bank, h2=h2):
                    b.wait("tensor", t_p)
                    if kb == 0 and t_rt_read[0] is not None:
                        b.wait("tensor", t_rt_read[0])
                    first = (kb == 0)
                    last_kb = (kb == nkb - 1)
                    vof = kb * 256 + h2 * 128
                    b.do("tensor", lambda e, par=par, first=first,
                         db=den_bank: e.matmul(
                        db[:], ones_b[:], p_hi[par][:], start=first,
                        stop=False))
                    b.do("tensor", lambda e, par=par, last=last_kb,
                         db=den_bank: e.matmul(
                        db[:], ones_b[:], p_lo[par][:], start=False,
                        stop=last))
                    b.do("tensor", lambda e, par=par, first=first, vof=vof,
                         ob=out_bank: e.matmul(
                        ob[:], v2_hi[:, vof:vof + 128],
                        p_hi[par][:], start=first, stop=False))
                    b.do("tensor", lambda e, par=par, vof=vof,
                         ob=out_bank: e.matmul(
                        ob[:], v2_hi[:, vof:vof + 128], p_lo[par][:],
                        start=False, stop=False))
                    b.drain("tensor")
                    return b.do("tensor", lambda e, par=par, vof=vof,
                                last=last_kb, ob=out_bank: e.matmul(
                        ob[:], v2_lo[:, vof:vof + 128], p_hi[par][:],
                        start=False, stop=last), mark=te)

                for kb in range(nkb):
                    par = kb % 2
                    sbank = PSB[par]
                    wts = [t_qrope, t_v2, t_kh, t_masks]
                    if t_exp[par] is not None:
                        wts.append(t_exp[par])
                    b.waits("tensor", wts)
                    b.do("tensor", lambda e, kb=kb, qoff=qoff, sbank=sbank:
                         e.matmul(sbank[:],
                                  kh_hi[:, kb * 128:(kb + 1) * 128],
                                  qn_hi[:, qoff:qoff + 512], start=True,
                                  stop=False))
                    b.do("tensor", lambda e, kb=kb, qoff=qoff, sbank=sbank:
                         e.matmul(sbank[:],
                                  kh_hi[:, kb * 128:(kb + 1) * 128],
                                  qn_lo[:, qoff:qoff + 512], start=False,
                                  stop=False))
                    b.do("tensor", lambda e, kb=kb, qoff=qoff, sbank=sbank:
                         e.matmul(sbank[:],
                                  kh_lo[:, kb * 128:(kb + 1) * 128],
                                  qn_hi[:, qoff:qoff + 512], start=False,
                                  stop=False))
                    b.do("tensor", lambda e, kb=kb, qoff=qoff, sbank=sbank:
                         e.matmul(sbank[:], krp[:, kb * 128:(kb + 1) * 128],
                                  qr_pack[:, qoff:qoff + 512], start=False,
                                  stop=False))
                    if cidx == 1 and kb == nkb - 1:
                        b.drain("tensor")
                    t_s = b.do("tensor", lambda e, kb=kb, qoff=qoff,
                               sbank=sbank: e.matmul(
                        sbank[:], krp[0:64, kb * 128:(kb + 1) * 128],
                        qr_lo[:, qoff:qoff + 512], start=False, stop=True),
                        mark=te)
                    if cidx == 1 and kb == nkb - 1:
                        t_hcons = t_s
                    masked = (cidx == 0) or (kb >= 8)
                    b.wait("scalar", t_s)
                    if masked:
                        t_e = b.do("scalar", lambda e, sbank=sbank, par=par:
                                   e.activation(out=p_f[par][:],
                                                in_=sbank[:],
                                                func=AF.Exp, scale=SCALE),
                                   mark=ac)
                        t_exp[par] = t_e
                        b.wait("vector", t_e)
                        if t_pcons[par] is not None:
                            b.wait("vector", t_pcons[par])
                        b.do("vector", lambda e, par=par, kb=kb:
                             e.scalar_tensor_tensor(
                                 out=p_mm[:], in0=p_f[par][:], scalar=1.0,
                                 in1=masks16[:, kb * 512:(kb + 1) * 512],
                                 op0=OP.mult, op1=OP.mult))
                        b.drain("vector")
                        b.do("vector", lambda e, par=par:
                             e.tensor_copy(out=p_hi[par][:], in_=p_mm[:]))
                        b.drain("vector")
                        t_p = b.do("vector", lambda e, par=par:
                                   e.tensor_tensor(out=p_lo[par][:],
                                                   in0=p_mm[:],
                                                   in1=p_hi[par][:],
                                                   op=OP.subtract), mark=ve)
                    else:
                        b.do("scalar", lambda e, sbank=sbank, par=par:
                             e.activation(out=p_f[par][:], in_=sbank[:],
                                          func=AF.Exp, scale=SCALE))
                        if t_pcons[par] is not None:
                            b.wait("scalar", t_pcons[par])
                        b.drain("scalar")
                        t_e = b.do("scalar", lambda e, sbank=sbank, par=par:
                                   e.activation(out=p_hi[par][:],
                                                in_=sbank[:],
                                                func=AF.Exp, scale=SCALE),
                                   mark=ac)
                        t_exp[par] = t_e
                        b.wait("vector", t_e)
                        if t_pcons[par] is not None:
                            b.wait("vector", t_pcons[par])
                        t_p = b.do("vector", lambda e, par=par:
                                   e.tensor_tensor(out=p_lo[par][:],
                                                   in0=p_f[par][:],
                                                   in1=p_hi[par][:],
                                                   op=OP.subtract), mark=ve)
                    for ps in [p for p in pend_splits
                               if p[0] == cidx and p[1] == kb]:
                        ps[2]()
                        pend_splits.remove(ps)
                    if pend is not None:
                        t_av = emit_av(*pend)
                        t_pcons[pend[1]] = t_av
                    pend = (kb, par, t_p)
                t_av = emit_av(*pend)
                t_pcons[pend[1]] = t_av
                b.wait("scalar", t_av)
                b.do("scalar", lambda e, den_bank=den_bank: e.activation(
                    out=rt1[:], in_=den_bank[:], func=AF.Ln, scale=1.0))
                b.drain("scalar")
                t_rt = b.do("scalar", lambda e: e.activation(
                    out=rt1[:], in_=rt1[:], func=AF.Exp, scale=-1.0),
                    mark=ac)
                t_rt_read[0] = t_rt
                b.wait("vector", t_av)
                b.wait("vector", t_rt)
                oa_off = h * TQ + qoff
                b.do("vector", lambda e, out_bank=out_bank: e.tensor_tensor(
                    out=sqf[:], in0=out_bank[:], in1=rt1[:], op=OP.mult))
                b.drain("vector")
                b.do("vector", lambda e, oa_off=oa_off: e.tensor_copy(
                    out=oa_hi[:, oa_off:oa_off + 512], in_=sqf[:]))
                b.drain("vector")
                t_oa = b.do("vector", lambda e, oa_off=oa_off:
                            e.tensor_tensor(
                    out=oa_lo[:, oa_off:oa_off + 512], in0=sqf[:],
                    in1=oa_hi[:, oa_off:oa_off + 512], op=OP.subtract),
                    mark=ve)
                t_oa_last = t_oa
                t_oa_bank[cidx] = t_oa
        t_v_read = t_av

    if "oa" in taps:
        tt = tapt("oa_hi", [128, H * TQ])
        cht.store(tt[:, :], oa_hi[:], after=(t_oa_last,))
        tt2 = tapt("oa_lo", [128, H * TQ])
        cht.store(tt2[:, :], oa_lo[:], after=(t_oa_last,))
    if "attdbg" in taps:
        for nm, buf, sh in (("kh_hi", kh_hi, [128, S]), ("kh_lo", kh_lo, [128, S]),
                            ("qn_hi", qn_hi, [128, TQ]), ("qn_lo", qn_lo, [128, TQ]),
                            ("qr_pack", qr_pack, [128, TQ]), ("qr_lo", qr_lo, [64, TQ]),
                            ("v2_hi", v2_hi, [128, 16 * 256]),
                            ("v2_lo", v2_lo, [128, 16 * 256]),
                            ("krp", krp, [128, S]),
                            ("masks", masks16, [128, 16 * 512]),
                            ("p_f", p_f[1], [128, 512]),
                            ("p_hi", p_hi[1], [128, 512]),
                            ("p_lo", p_lo[1], [128, 512]),
                            ("rt1", rt1, [128, 512]),
                            ("sqf", sqf, [128, 512])):
            tt = tapt(nm, sh)
            cht.store(tt[tuple(slice(None) for _ in sh)], buf[:],
                      after=(t_oa_last,))

    if stage <= 3:
        if cht.c.n:
            b.wait("gpsimd", (cht.c.sem, cht.c.n))
        with nc.Block() as block:
            b.realize(block)
        return nc, tap_dr

    b.barrier()
    SCR.reset()
    acc_tok = SCR.alloc("acc_tok", [128, 4 * D])
    wo_hi = SCR.alloc("wo_hi", [128, 4 * D], BF16)
    wo_lo = SCR.alloc("wo_lo", [128, 4 * D], BF16)
    wo_pairs = [(wo_hi, wo_lo), (wo_hi2, wo_lo2)]
    stg4 = [wstage4, wstage4b]

    t_acc_last = None
    t_accst = None
    spl_stg = [None, None]
    t_wo_cons2 = [None, None]
    ev_bank = [None, None]
    spl_cur = [None] * 4
    tl_next = [None] * 4
    pend4 = []

    def wo_load(gidx, r4, aft):
        return chw.load(stg4[r4 % 2][:],
                        w_o[(gidx % 4) * 512 + r4 * 128:
                            (gidx % 4) * 512 + (r4 + 1) * 128, :],
                        after=tuple(t for t in aft if t is not None))

    def wo_split(gidx, r4, tl):
        hi, lo = wo_pairs[gidx % 2]
        tok = split3(hi[:, r4 * D:(r4 + 1) * D],
                     lo[:, r4 * D:(r4 + 1) * D], stg4[r4 % 2][:],
                     tuple(t for t in (tl, t_wo_cons2[gidx % 2])
                           if t is not None))
        spl_stg[r4 % 2] = tok
        return tok

    for gidx in range(8):
        qhalf, grp4 = gidx // 4, gidx % 4
        if gidx == 0:
            spl_cur[0] = wo_split(0, 0, tl_pre4[0])
            tl = wo_load(0, 2, (spl_stg[0],))
            spl_cur[1] = wo_split(0, 1, tl_pre4[1])
            tl3 = wo_load(0, 3, (spl_stg[1],))
            spl_cur[2] = wo_split(0, 2, tl)
            spl_cur[3] = wo_split(0, 3, tl3)
        if gidx < 7:
            tl_next[0] = wo_load(gidx + 1, 0, (spl_stg[0],))
            tl_next[1] = wo_load(gidx + 1, 1, (spl_stg[1],))

            def mk_early(g, r, tl):
                def fn():
                    tok = wo_split(g, r, tl)
                    tl_next[r + 2] = wo_load(g, r + 2, (tok,))
                    return tok
                return fn

            def mk_late(g, r):
                def fn():
                    return wo_split(g, r, tl_next[r])
                return fn

            pend4 = [(5, 0, mk_early(gidx + 1, 0, tl_next[0])),
                     (8, 1, mk_early(gidx + 1, 1, tl_next[1])),
                     (11, 2, mk_late(gidx + 1, 2)),
                     (14, 3, mk_late(gidx + 1, 3))]
        else:
            pend4 = []
        nxt = [None] * 4
        ev_prev = None
        for qt4 in range(4):
            qt = qhalf * 4 + qt4
            for nch in range(4):
                j = qt4 * 4 + nch
                bank = PSB[j % 2]
                hi, lo = wo_pairs[gidx % 2]
                mms = []
                pre = {}
                for ct4 in range(4):
                    hh = grp4 * 4 + ct4
                    pre[len(mms)] = (spl_cur[ct4],)
                    l_h = oa_hi[:, hh * TQ + qt * 128:
                                hh * TQ + (qt + 1) * 128]
                    l_l = oa_lo[:, hh * TQ + qt * 128:
                                hh * TQ + (qt + 1) * 128]
                    r_h = hi[:, ct4 * D + nch * 512:
                             ct4 * D + (nch + 1) * 512]
                    r_l = lo[:, ct4 * D + nch * 512:
                             ct4 * D + (nch + 1) * 512]
                    mms += [(l_h, r_h), (l_h, r_l), (l_l, r_h)]
                wts = []
                if ev_bank[j % 2] is not None:
                    wts.append(ev_bank[j % 2])
                t_mm = mm_group(bank[:], mms, waits=wts, pre=pre,
                                cons=(j == 15))
                if j == 15:
                    t_wo_cons2[gidx % 2] = t_mm
                b.wait("vector", t_mm)
                if grp4 == 0 and t_accst is not None:
                    b.wait("vector", t_accst)
                off = qt4 * D + nch * 512
                if grp4 == 0:
                    ev_prev = b.do("vector", lambda e, bank=bank, off=off:
                                   e.tensor_copy(
                        out=acc_tok[:, off:off + 512], in_=bank[:]),
                        mark=ve)
                else:
                    ev_prev = b.do("vector", lambda e, bank=bank, off=off:
                                   e.tensor_tensor(
                        out=acc_tok[:, off:off + 512], in0=bank[:],
                        in1=acc_tok[:, off:off + 512], op=OP.add),
                        mark=ve)
                ev_bank[j % 2] = ev_prev
                for at, idx, fn in [p for p in pend4 if p[0] == j]:
                    nxt[idx] = fn()
                    pend4.remove((at, idx, fn))
        spl_cur = nxt
        t_acc_last = ev_prev
        if grp4 == 3:
            for qt4 in range(4):
                qt = qhalf * 4 + qt4
                t_accst = chs.store(acc_scr[qt * 128:(qt + 1) * 128, :],
                                    acc_tok[:, qt4 * D:(qt4 + 1) * D],
                                    after=(t_acc_last,))
    if "acc" in taps:
        tt = tapt("acc", [128, 4 * D])
        cht.store(tt[:, :], acc_tok[:], after=(t_accst,))

    if stage <= 4:
        b.wait("scalar", (chs.c.sem, chs.c.n))
        if cht.c.n:
            b.wait("gpsimd", (cht.c.sem, cht.c.n))
        with nc.Block() as block:
            b.realize(block)
        return nc, tap_dr

    b.barrier()
    BIG.cur = 0
    l2f_hi = BIG.alloc("l2f_hi", [128, 16 * 512], BF16)
    l2f_lo = BIG.alloc("l2f_lo", [128, 16 * 512], BF16)
    hdn_hi = BIG.alloc("hdn_hi", [128, 22 * 512], BF16)
    hdn_lo = BIG.alloc("hdn_lo", [128, 22 * 512], BF16)
    dwT_blk = BIG.alloc("dwT_blk", [128, 22 * 256], BF16)
    dw_nat = BIG.alloc("dw_nat", [128, FI], BF16)
    fw_nat2 = [BIG.alloc(f"fw_nat{i}", [128, 2 * D], BF16) for i in range(2)]
    fwT2 = [BIG.alloc(f"fwT{i}", [128, D], BF16) for i in range(2)]
    dn_buf = BIG.alloc("dn_buf", [128, 4 * D])
    tmp_a = BIG.alloc("tmp_a", [128, 2048])
    tmp_b = BIG.alloc("tmp_b", [128, 2048])
    tmp_c = BIG.alloc("tmp_c", [128, 2048])
    xq_tile = BIG.alloc("xq_tile", [128, D])
    dsb_t = BIG.alloc("dsb_t", [128, D])
    gates_t = BIG.alloc("gates_t", [128, 64])
    upss_t = BIG.alloc("upss_t", [128, 64])

    chm.load(dsb_t[:], ds_b[:, :])
    chm.load(gates_t[:, 0:44], gate_s_t[:, :])
    t_meta5 = chm.load(upss_t[:, 0:44], up_s_t[:, :])
    b.wait("vector", t_meta5)

    t_x1st = [None] * 8
    t_l2f_used = None
    t_prev_store = [None]
    t_tmpa_read = [None]
    t_l2f_cell = [None]
    t_trguard = [None]
    pend_tr = [None]

    def emit_qt_ln(qt, tr_now=True):
        wts = []
        if t_prev_store[0] is not None:
            wts.append(t_prev_store[0])
        t_la = chs.load(tmp_a[:], acc_scr[qt * 128:(qt + 1) * 128, :],
                        after=(t_tmpa_read[0],) if t_tmpa_read[0] else ())
        tl = chx.load(xq_tile[:], xq[qt * 128:(qt + 1) * 128, :], after=wts)
        b.wait("vector", tl)
        b.wait("vector", t_la)
        t_x1 = ln_stats_apply(tmp_a[:], tmp_b[:], D, res_ap=xq_tile[:])
        t_tmpa_read[0] = t_x1
        t_st = chs.store(x1_scr[qt * 128:(qt + 1) * 128, :], tmp_b[:],
                         after=(t_x1,))
        t_x1st[qt] = t_st
        t_prev_store[0] = t_st
        t_ln2 = ln_stats_apply(tmp_b[:], tmp_c[:], D)

        def do_tr():
            wts = [t_ln2]
            if t_l2f_used is not None and qt % 4 == 0:
                wts.append(t_l2f_used)
            t_e, _ = tr_split_tile(tmp_c[:], l2f_hi, l2f_lo,
                                   (qt % 4) * 128, 512, waits=wts)
            t_l2f_cell[0] = t_e
            t_trguard[0] = t_e
        if tr_now:
            do_tr()
            return None
        return do_tr

    for sub in range(2):
        if sub == 0:
            for qt in range(4):
                emit_qt_ln(qt)
        t_l2f_e = t_l2f_cell[0]
        t_dn = [None] * 4
        for half in range(2):
            t_hdn_last = None
            t_gu_read = None
            b.wait("tensor", t_l2f_e)
            units = []
            for m22 in range(22):
                m = half * 22 + m22
                units.append((m, m22, "g", gate_w, gwT_scr))
                units.append((m, m22, "u", up_w, uwT_scr))
            t_mm_hist = [None, None]
            t_scr_hist = [None, None]
            t_fw_hist = [None, None]
            t_prev_evac = [None]
            t_wT_tok = [None] * len(units)

            def stage_unit(i):
                m, m22, wname, wsrc_d, scr = units[i]
                pb = i % 2
                if sub == 0:
                    awts = tuple(t for t in (t_fw_hist[pb],) if t)
                    tg = chw.load(fw_nat2[pb][:],
                                  wsrc_d[m * 128:(m + 1) * 128, :],
                                  after=awts)
                    prow_fw = list(fw_nat2[pb][:].ap[0])

                    def fw_hi(dct, pb=pb, prow=prow_fw):
                        return pap(fw_nat2[pb], 2 * dct * 128 + 1,
                                   [prow, [2, 128]])
                    t_wT = None
                    for hf in range(2):
                        b.wait("tensor", tg)
                        if t_wT is not None:
                            b.wait("tensor", t_wT)
                        elif t_prev_evac[0] is not None:
                            b.wait("tensor", t_prev_evac[0])
                        if t_mm_hist[pb] is not None:
                            b.wait("tensor", t_mm_hist[pb])
                        for j in range(7):
                            dct = hf * 8 + j
                            b.do("tensor", lambda e, j=j, dct=dct:
                                 e.transpose(
                                PSBF[:, j * 128:(j + 1) * 128],
                                fw_hi(dct), ident_b[:]))
                        b.drain("tensor")
                        t_tr = b.do("tensor", lambda e, dct=hf * 8 + 7:
                                    e.transpose(
                            PSBF[:, 7 * 128:8 * 128],
                            fw_hi(dct), ident_b[:]), mark=te)
                        t_fw_hist[pb] = t_tr
                        b.wait("vector", t_tr)
                        if t_scr_hist[pb] is not None:
                            b.wait("vector", t_scr_hist[pb])
                        if t_mm_hist[pb] is not None:
                            b.wait("vector", t_mm_hist[pb])
                        t_wT = b.do("vector", lambda e, hf=hf, pb=pb:
                                    e.tensor_copy(
                            out=fwT2[pb][:, hf * 1024:(hf + 1) * 1024],
                            in_=PSBF[:]), mark=ve)
                    t_scr_hist[pb] = chs.store(
                        scr[m * 128:(m + 1) * 128, :], fwT2[pb][:],
                        after=(t_wT,))
                    t_prev_evac[0] = t_wT
                    return t_wT
                else:
                    awts = tuple(t for t in (t_mm_hist[pb],) if t)
                    return chs.load(fwT2[pb][:],
                                    scr[m * 128:(m + 1) * 128, :],
                                    after=awts)

            t_wT_tok[0] = stage_unit(0)
            for i, (m, m22, wname, wsrc_d, scr) in enumerate(units):
                pb = i % 2
                if i + 1 < len(units):
                    t_wT_tok[i + 1] = stage_unit(i + 1)
                bank = PSB[0] if wname == "g" else PSB[1]
                mms = []
                for ct in range(16):
                    wt_sl = fwT2[pb][:, ct * 128:(ct + 1) * 128]
                    mms += [(wt_sl, l2f_hi[:, ct * 512:(ct + 1) * 512]),
                            (wt_sl, l2f_lo[:, ct * 512:(ct + 1) * 512])]
                wts2 = [t_wT_tok[i]]
                if t_hdn_last is not None:
                    wts2.append(t_hdn_last)
                t_mmg = mm_group(bank[:], mms, waits=wts2, cons=True)
                t_mm_hist[pb] = t_mmg
                if wname == "g":
                    t_g = t_mmg
                    continue
                t_u = t_mmg
                t_gu_read = t_u
                b.wait("vector", t_g)
                b.do("vector", lambda e, m=m: e.tensor_scalar(
                    out=tmp_c[:, 0:512], in0=PSB[0][:],
                    scalar1=gates_t[:, m:m + 1], scalar2=None, op0=OP.mult))
                b.drain("vector")
                t_t1 = b.do("vector", lambda e: e.engine_nop(), mark=ve)
                b.wait("scalar", t_t1)
                t_sg = b.do("scalar", lambda e: e.activation(
                    out=tmp_c[:, 512:1024], in_=tmp_c[:, 0:512],
                    func=AF.Sigmoid, scale=1.0), mark=ac)
                b.wait("vector", t_sg)
                b.wait("vector", t_u)
                b.do("vector", lambda e, m=m: e.scalar_tensor_tensor(
                    out=tmp_c[:, 1024:1536], in0=PSB[1][:],
                    scalar=upss_t[:, m:m + 1], in1=tmp_c[:, 512:1024],
                    op0=OP.mult, op1=OP.mult))
                b.drain("vector")
                b.do("vector", lambda e: e.tensor_tensor(
                    out=tmp_c[:, 0:512], in0=tmp_c[:, 0:512],
                    in1=tmp_c[:, 1024:1536], op=OP.mult))
                b.drain("vector")
                b.do("vector", lambda e, m22=m22: e.tensor_copy(
                    out=hdn_hi[:, m22 * 512:(m22 + 1) * 512],
                    in_=tmp_c[:, 0:512]))
                b.drain("vector")
                t_hdn_last = b.do("vector", lambda e, m22=m22:
                                  e.tensor_tensor(
                    out=hdn_lo[:, m22 * 512:(m22 + 1) * 512],
                    in0=tmp_c[:, 0:512],
                    in1=hdn_hi[:, m22 * 512:(m22 + 1) * 512],
                    op=OP.subtract), mark=ve)
            if half == 1:
                t_l2f_used = t_gu_read

            b.wait("vector", (cho.c.sem, cho.c.n) if cho.c.n else None)
            t_blk_read = None
            t_dw_rnd = [None] * 3
            t_dscr = None
            for nch in range(8):
                if sub == 0 and half == 1 and nch % 2 == 1:
                    if pend_tr[0] is not None:
                        pend_tr[0]()
                    pend_tr[0] = emit_qt_ln(4 + nch // 2, tr_now=False)
                scr_off = (half * 8 + nch) * 22 * 128
                scr_view = pap(dwT_scr.tensor, scr_off * 256,
                               [[256, 128], [128 * 256, 22], [1, 256]])
                blk_view = dwT_blk[:].rearrange("p (c f) -> p c f", f=256)
                if sub == 0:
                    t_blk = None
                    for j2 in range(2):
                        hrow = nch * 256 + j2 * 128
                        tdw_r = [None] * 3
                        for r in range(3):
                            c0, c1 = 2048 * r, min(FI, 2048 * (r + 1))
                            awts = tuple(t for t in (t_dw_rnd[r],) if t)
                            tdw_r[r] = chw.load(
                                dw_nat[:, c0:c1],
                                down_w[hrow:hrow + 128,
                                       half * FI + c0:half * FI + c1],
                                after=awts)
                        prow_dw = list(dw_nat[:].ap[0])

                        def dw_hi(ct, prow=prow_dw):
                            return pap(dw_nat, 2 * ct * 128 + 1,
                                       [prow, [2, 128]])
                        for rnd in range(3):
                            jmax = min(8, 22 - rnd * 8)
                            b.wait("tensor", tdw_r[rnd])
                            if t_trguard[0] is not None:
                                b.wait("tensor", t_trguard[0])
                                t_trguard[0] = None
                            if t_blk is not None:
                                b.wait("tensor", t_blk)
                            for j in range(jmax - 1):
                                ct = rnd * 8 + j
                                b.do("tensor", lambda e, j=j, ct=ct:
                                     e.transpose(
                                    PSBF[:, j * 128:(j + 1) * 128],
                                    dw_hi(ct), ident_b[:]))
                            b.drain("tensor")
                            t_tr = b.do("tensor", lambda e, j=jmax - 1,
                                        ct=rnd * 8 + jmax - 1: e.transpose(
                                PSBF[:, j * 128:(j + 1) * 128],
                                dw_hi(ct), ident_b[:]), mark=te)
                            t_dw_rnd[rnd] = t_tr
                            b.wait("vector", t_tr)
                            if t_dscr is not None:
                                b.wait("vector", t_dscr)
                            prow = list(dwT_blk[:].ap[0])
                            dview = pap(dwT_blk,
                                        (rnd * 8) * 256 + j2 * 128,
                                        [prow, [256, jmax], [1, 128]])
                            pview = PSBF[:, 0:jmax * 128].rearrange(
                                "p (a f) -> p a f", a=jmax)
                            t_blk = b.do("vector", lambda e, dv=dview,
                                         pv=pview: e.tensor_copy(
                                out=dv, in_=pv), mark=ve)
                    t_dscr = chs.store(scr_view, blk_view, after=(t_blk,))
                    t_blkr = t_blk
                else:
                    awts = tuple(t for t in (t_blk_read,) if t)
                    t_blkr = chs.load(blk_view, scr_view, after=awts)
                for qt4 in range(4):
                    bank = PSB[2 + qt4 % 2]
                    mms = []
                    for ct in range(22):
                        l_h = hdn_hi[:, ct * 512 + qt4 * 128:
                                     ct * 512 + (qt4 + 1) * 128]
                        l_l = hdn_lo[:, ct * 512 + qt4 * 128:
                                     ct * 512 + (qt4 + 1) * 128]
                        r = dwT_blk[:, ct * 256:(ct + 1) * 256]
                        mms += [(l_h, r), (l_l, r)]
                    wts3 = [t_blkr, t_hdn_last]
                    if t_dn[qt4] is not None:
                        wts3.append(t_dn[qt4])
                    if t_dn[(qt4 + 2) % 4] is not None:
                        wts3.append(t_dn[(qt4 + 2) % 4])
                    t_mm = mm_group(bank[0:128, 0:256], mms, waits=wts3,
                                    cons=(qt4 == 3))
                    b.wait("vector", t_mm)
                    off = qt4 * D + nch * 256
                    if half == 0:
                        t_dn[qt4] = b.do("vector", lambda e, bank=bank,
                                         off=off: e.tensor_copy(
                            out=dn_buf[:, off:off + 256],
                            in_=bank[0:128, 0:256]), mark=ve)
                    else:
                        t_dn[qt4] = b.do("vector", lambda e, bank=bank,
                                         off=off: e.tensor_tensor(
                            out=dn_buf[:, off:off + 256],
                            in0=bank[0:128, 0:256],
                            in1=dn_buf[:, off:off + 256], op=OP.add),
                            mark=ve)
                t_blk_read = t_mm

        if pend_tr[0] is not None:
            pend_tr[0]()
            pend_tr[0] = None

        for qt4 in range(4):
            qtg = sub * 4 + qt4
            b.wait("vector", t_dn[qt4])
            tx1 = chx.load(tmp_a[:], x1_scr[qtg * 128:(qtg + 1) * 128, :],
                           after=(t_x1st[qtg], t_tmpa_read[0]))
            b.wait("vector", tx1)
            dn_sl = dn_buf[:, qt4 * D:(qt4 + 1) * D]
            b.do("vector", lambda e, dn_sl=dn_sl: e.tensor_tensor(
                out=dn_sl, in0=dn_sl, in1=dsb_t[:], op=OP.mult))
            b.drain("vector")
            t_out = ln_stats_apply(dn_sl, dn_sl, D, res_ap=tmp_a[:])
            t_tmpa_read[0] = t_out
            t_prev_store[0] = cho.store(out_q[qtg * 128:(qtg + 1) * 128, :],
                                        dn_sl, after=(t_out,))

    b.wait("sync", (cho.c.sem, cho.c.n))
    if cht.c.n:
        b.wait("gpsimd", (cht.c.sem, cht.c.n))
    b.wait("scalar", (chs.c.sem, chs.c.n))

    with nc.Block() as block:
        b.realize(block)
    return nc, tap_dr


def _rope_tables_f32():
    t = (np.arange(0, DR, 2).astype(np.float32) / np.float32(DR))
    freqs = (np.float32(1.0) / (np.float32(10000.0) ** t)).astype(np.float32)
    ang = (np.arange(S, dtype=np.float32)[:, None] * freqs).astype(np.float32)
    ang = np.repeat(ang, 2, axis=-1)
    cos = np.cos(ang).astype(np.float32)
    sin = np.sin(ang).astype(np.float32)
    sinS = sin.copy()
    sinS[:, 0::2] *= np.float32(-1.0)
    return cos, sinS


def make_in_maps(inputs):
    f32 = lambda k: np.ascontiguousarray(np.asarray(inputs[k], dtype=np.float32))
    x = f32("L0_attn_x")
    cos, sinS = _rope_tables_f32()
    base = {
        "w_dq": f32("L0_attn_w_dq"), "w_uq": f32("L0_attn_w_uq"),
        "w_dkv": f32("L0_attn_w_dkv"), "w_ukrope": f32("L0_attn_w_ukrope"),
        "w_uv": f32("L0_attn_w_uv"), "w_uknope": f32("L0_attn_w_uknope"),
        "w_o": f32("L0_attn_w_o"),
        "gate_w": np.ascontiguousarray(f32("L0_ffn_gate_w")).view(
            ml_dtypes.bfloat16),
        "up_w": np.ascontiguousarray(f32("L0_ffn_up_w")).view(
            ml_dtypes.bfloat16),
        "down_w": np.ascontiguousarray(f32("L0_ffn_down_w")).view(
            ml_dtypes.bfloat16),
        "g_q_t": np.ascontiguousarray(f32("L0_attn_g_q").reshape(4, 128).T),
        "g_kv_t": np.ascontiguousarray(f32("L0_attn_g_kv").reshape(4, 128).T),
        "gate_s_t": np.ascontiguousarray(
            f32("L0_ffn_gate_s").reshape(44, 128).T),
        "up_s_t": np.ascontiguousarray(f32("L0_ffn_up_s").reshape(44, 128).T),
        "ds_b": np.ascontiguousarray(
            np.broadcast_to(f32("L0_ffn_down_s"), (128, D))),
        "cos_k": cos, "sinS_k": sinS,
    }
    in_maps, own_rows = [], []
    for cid in range(8):
        bb, par = cid // 2, cid % 2
        own = np.r_[par * 512:(par + 1) * 512, (3 - par) * 512:(4 - par) * 512]
        own_rows.append((bb, own))
        m = dict(base)
        m["x_b"] = np.ascontiguousarray(x[bb])
        m["xq"] = np.ascontiguousarray(x[bb][own])
        m["qpos_b"] = np.ascontiguousarray(
            np.broadcast_to(own.astype(np.float32), (128, TQ)))
        m["cos_q"] = np.ascontiguousarray(cos[own])
        m["sinS_q"] = np.ascontiguousarray(sinS[own])
        in_maps.append(m)
    return in_maps, own_rows


_PROG = {}


def kernel(**inputs):
    if "full" not in _PROG:
        _PROG["full"] = build_program(stage=9)
    nc, _ = _PROG["full"]
    in_maps, own_rows = make_in_maps(inputs)
    res = run_bass_kernel_spmd(nc, in_maps, core_ids=list(range(8)))
    out = np.empty((B, S, D), dtype=np.float32)
    for cid in range(8):
        bb, own = own_rows[cid]
        out[bb][own] = np.asarray(res.results[cid]["out_q"], dtype=np.float32)
    return out
